# version 14
# baseline (speedup 1.0000x reference)
import numpy as np
import ml_dtypes
import concourse.bass as bass
import concourse.mybir as mybir
from concourse.bass_utils import run_bass_kernel_spmd
from concourse.bass_isa import ReduceOp
RED_MAX = ReduceOp.max if hasattr(ReduceOp, "max") else "max"
RED_ADD = ReduceOp.add if hasattr(ReduceOp, "add") else "add" 

F32 = mybir.dt.float32
BF16 = mybir.dt.bfloat16
AF = mybir.ActivationFunctionType
ALU = mybir.AluOpType
AX = mybir.AxisListType

D_MODEL = 1024
D_INNER = 2048
D_STATE = 16
D_CONV = 4
DT_RANK = 64
NCORES = 8

LN_EPS_OUTER = 1e-5
LN_EPS_INNER = 1e-6
NEG_BIG = -30000.0
import os
SCE_EVAC = os.environ.get("SCE_EVAC", "0") == "1"



class Sem:
    __slots__ = ("h", "n")

    def __init__(self, h):
        self.h = h
        self.n = 0


class EngPlan:
    def __init__(self, name):
        self.name = name
        self.steps = []
        self.waited = {}

    def wait(self, sem, val=None):
        v = sem.n if val is None else val
        if self.waited.get(id(sem), -1) >= v:
            return
        self.waited[id(sem)] = v
        h = sem.h
        self.steps.append(lambda e, h=h, v=v: e.wait_ge(h, v))

    def run(self, fn, inc=None, n=1):
        if inc is not None:
            inc.n += n
            h = inc.h
            self.steps.append(lambda e, fn=fn, h=h, n=n: fn(e).then_inc(h, n))
        else:
            self.steps.append(fn)


class Prog:
    ENGS = ("sync", "vector", "scalar", "tensor", "gpsimd")

    def __init__(self, nc):
        self.nc = nc
        self.sems = {}
        self.plans = {k: EngPlan(k) for k in self.ENGS}
        self.sync = self.plans["sync"]
        self.vector = self.plans["vector"]
        self.scalar = self.plans["scalar"]
        self.tensor = self.plans["tensor"]
        self.gpsimd = self.plans["gpsimd"]
        self._phase_idx = 0
        self._stack = None

    def sem(self, name):
        if name not in self.sems:
            h = self._stack.enter_context(self.nc.semaphore(name))
            self.sems[name] = Sem(h)
        return self.sems[name]

    def setup_consts(self, values):
        n = len(values)
        t = self._stack.enter_context(self.nc.sbuf_tensor("cconsts", [128, max(n, 1)], F32))
        self._consts = t
        self._cmap = {}
        for i, (k, v) in enumerate(values.items()):
            self._cmap[k] = i
            self.vector.run(lambda e, i=i, v=v: e.memset(t[:, i:i + 1], v))
        s = self.sem("s_const")
        self.vector.run(lambda e: e.engine_nop(), inc=s)
        for k in ("scalar", "vector", "gpsimd", "tensor"):
            self.plans[k].wait(s)

    def tile_barrier(self):
        bar = self.sem("tilebar")
        for k in ("vector", "scalar", "tensor"):
            self.plans[k].run(lambda e: e.drain(), inc=bar, n=1)
        target = bar.n
        for k in ("vector", "scalar", "tensor", "sync", "gpsimd"):
            self.plans[k].wait(bar, target)

    def cc(self, name, rows=128):
        i = self._cmap[name]
        return self._consts[0:rows, i:i + 1]

    def dma(self, out, in_, sem, eng="sync"):
        self.plans[eng].run(lambda e, o=out, i=in_: e.dma_start(out=o, in_=i), inc=sem, n=16)

    def flush_phase(self):
        bar = self.sem("phasebar")
        for k in ("vector", "scalar", "tensor", "gpsimd"):
            self.plans[k].run(lambda e: e.drain(), inc=bar, n=1)
        self.plans["sync"].run(lambda e: e.sem_inc(bar.h, 1))
        bar.n += 1
        nc = self.nc
        with nc.Block() as blk:
            for k in self.ENGS:
                steps = self.plans[k].steps
                getattr(blk, k)(lambda e, steps=steps: [f(e) for f in steps])
        self._phase_idx += 1
        target = bar.n
        for k in self.ENGS:
            self.plans[k] = EngPlan(k)
        self.sync = self.plans["sync"]
        self.vector = self.plans["vector"]
        self.scalar = self.plans["scalar"]
        self.tensor = self.plans["tensor"]
        self.gpsimd = self.plans["gpsimd"]
        for k in self.ENGS:
            self.plans[k].wait(self.sems["phasebar"], target)



def _img_chunked(w, rows=128):
    K, M = w.shape
    return np.ascontiguousarray(w.reshape(K // rows, rows, M).transpose(1, 0, 2).reshape(rows, -1))


def sel_const():
    sel = np.zeros((32, 32 * 128), np.float32)
    for n in range(32):
        sel[n, n * 128:(n + 1) * 128] = 1.0
    return sel


def upv_img(wup, bup):
    out = np.zeros((65, 2048), np.float32)
    for dh in range(8):
        for q in range(2):
            for m in range(4):
                dd = dh * 8 + q * 4 + m
                col0 = dh * 256 + q * 128 + m * 32
                out[0:64, col0:col0 + 16] = wup[:, 16 * 0:16 * 16:128][:, :] if False else 0
                for g in range(16):
                    out[0:64, col0 + g] = wup[:, g * 128 + 64 + dd]
                    out[64, col0 + g] = bup[g * 128 + 64 + dd]
    return np.ascontiguousarray(out)


def host_layout(inp, core):
    f = np.ascontiguousarray
    d = {}
    d["xT"] = f(inp["L0_attn_x"][core].T)
    d["w_inproj"] = _img_chunked(f(inp["L0_attn_in_proj_w"].T))
    d["w_outproj"] = _img_chunked(f(inp["L0_attn_out_proj_w"].T))
    d["w_xproj"] = _img_chunked(f(inp["L0_attn_x_proj_w"].T))
    d["w_dtproj"] = f(inp["L0_attn_dt_proj_w"].T)
    d["v_convw"] = f(inp["L0_attn_conv_w"][:, 0, :].reshape(16, 128, 4).transpose(1, 0, 2).reshape(128, 64))
    d["v_convb"] = f(inp["L0_attn_conv_b"].reshape(16, 128).T)
    d["v_D"] = f(inp["L0_attn_D"].reshape(16, 128).T)
    d["v_dtb"] = f(inp["L0_attn_dt_proj_b"].reshape(16, 128).T)
    d["v_Alog"] = f(inp["L0_attn_A_log"].reshape(16, 128, 16).transpose(1, 0, 2).reshape(128, 256))
    d["v_lng"] = f(inp["L0_attn_ln_gamma"].reshape(8, 128).T)
    d["v_lnb"] = f(inp["L0_attn_ln_beta"].reshape(8, 128).T)
    d["w_q"] = _img_chunked(f(inp["L0_ffn_Wq"]))
    d["w_down"] = _img_chunked(f(inp["L0_ffn_Wdown"]))
    wup = inp["L0_ffn_Wup"]
    d["w_upk"] = f(wup.reshape(64, 16, 128)[:, :, 0:64].reshape(64, 16 * 64))
    d["w_upv"] = upv_img(wup, inp["L0_ffn_bup"])
    d["w_o"] = _img_chunked(f(inp["L0_ffn_Wo"]))
    d["w_g"] = _img_chunked(f(inp["L0_ffn_Wg"]))
    d["w_e"] = f(inp["L0_ffn_We"].reshape(8, 8, 128, 1024).transpose(2, 0, 1, 3).reshape(128, 8 * 8 * 1024))
    d["v_g1"] = f(inp["L0_ffn_gamma1"].reshape(8, 128).T)
    d["v_g2"] = f(inp["L0_ffn_gamma2"].reshape(8, 128).T)
    d["v_bq"] = f(inp["L0_ffn_bq"].reshape(8, 128).T)
    d["v_bo"] = f(inp["L0_ffn_bo"].reshape(8, 128).T)
    d["v_bdown"] = f(inp["L0_ffn_bdown"].reshape(64, 1))
    bup = inp["L0_ffn_bup"]
    d["v_bupk"] = f(bup.reshape(16, 128)[:, 0:64].T)
    d["v_bg"] = f(inp["L0_ffn_bg"].reshape(8, 1))
    d["c_sel"] = sel_const()
    d["c_mask"] = mask_const()
    d["c_eye"] = np.eye(128, dtype=np.float32)
    d["c_bgr"] = f(np.broadcast_to(inp["L0_ffn_bg"][None, :], (128, 8)).copy())
    return d


INPUT_SPECS = {
    "xT": ([1024, 4096], F32),
    "w_inproj": ([128, 8 * 4096], F32),
    "w_outproj": ([128, 16 * 1024], F32),
    "w_xproj": ([128, 16 * 96], F32),
    "w_dtproj": ([64, 2048], F32),
    "v_convw": ([128, 64], F32),
    "v_convb": ([128, 16], F32),
    "v_D": ([128, 16], F32),
    "v_dtb": ([128, 16], F32),
    "v_Alog": ([128, 256], F32),
    "v_lng": ([128, 8], F32),
    "v_lnb": ([128, 8], F32),
    "w_q": ([128, 8 * 1024], F32),
    "w_down": ([128, 8 * 64], F32),
    "w_upk": ([64, 16 * 64], F32),
    "w_upv": ([65, 2048], F32),
    "w_o": ([128, 8 * 1024], F32),
    "w_g": ([128, 64], F32),
    "w_e": ([128, 8 * 8 * 1024], F32),
    "v_g1": ([128, 8], F32),
    "v_g2": ([128, 8], F32),
    "v_bq": ([128, 8], F32),
    "v_bo": ([128, 8], F32),
    "v_bdown": ([64, 1], F32),
    "v_bupk": ([64, 16], F32),
    "v_bg": ([8, 1], F32),
    "c_sel": ([32, 32 * 128], F32),
    "c_mask": ([128, 512], F32),
    "c_eye": ([128, 128], F32),
    "c_bgr": ([128, 8], F32),
}



def emit_load_mamba_weights(P, st, dram):
    nc = P.nc
    sb = lambda n, s, d: st.enter_context(nc.sbuf_tensor(f"{n}_ph{P._phase_idx}", s, d))
    W = {}
    W["inW"] = sb("m_inW", [128, 8 * 4096], BF16)
    W["opW"] = sb("m_opW", [128, 16 * 1024], BF16)
    W["xpW"] = sb("m_xpW", [128, 16 * 96], BF16)
    W["dtW"] = sb("m_dtW", [64, 2048], BF16)
    W["cw"] = sb("m_cw", [128, 64], F32)
    W["cb"] = sb("m_cb", [128, 16], F32)
    W["Dv"] = sb("m_Dv", [128, 16], F32)
    W["dtb"] = sb("m_dtb", [128, 16], F32)
    W["An"] = sb("m_An", [128, 256], F32)
    W["lng"] = sb("m_lng", [128, 8], F32)
    W["lnb"] = sb("m_lnb", [128, 8], F32)
    W["onesc"] = sb("m_onesc", [128, 1], BF16)
    W["ones1"] = sb("m_ones1", [1, 128], BF16)
    W["ones1f"] = sb("m_ones1f", [1, 128], F32)
    s_w = P.sem("s_w")
    s_wg = P.sem("s_wg")
    P.dma(W["inW"][:], dram["w_inproj"], s_wg, eng="gpsimd")
    P.dma(W["opW"][:], dram["w_outproj"], s_wg, eng="gpsimd")
    P.dma(W["xpW"][:], dram["w_xproj"], s_wg, eng="gpsimd")
    P.dma(W["dtW"][:], dram["w_dtproj"], s_wg, eng="gpsimd")
    for k, src in [("cw", "v_convw"), ("cb", "v_convb"), ("Dv", "v_D"),
                   ("dtb", "v_dtb"), ("An", "v_Alog"), ("lng", "v_lng"), ("lnb", "v_lnb")]:
        P.dma(W[k][:], dram[src], s_w, eng="sync")
    P.vector.run(lambda e, W=W: e.memset(W["onesc"][:], 1.0))
    P.vector.run(lambda e, W=W: e.memset(W["ones1"][:], 1.0))
    P.vector.run(lambda e, W=W: e.memset(W["ones1f"][:], 1.0))
    P.scalar.wait(s_w)
    P.scalar.wait(s_wg)
    s_se = P.sem("s_se")
    P.scalar.run(lambda e, W=W: e.activation(W["An"][:], W["An"][:], AF.Exp), inc=s_se)
    P.vector.wait(s_se)
    s_ve = P.sem("s_ve")
    P.vector.run(lambda e, W=W: e.tensor_scalar_mul(W["An"][:], W["An"][:], -1.0), inc=s_ve)
    return W



def emit_mamba(P, st, W, in_dram, out_dram, in_sel, L=4096, T=256, dbg=None):
    nc = P.nc
    n_t = L // T
    NC = 16
    sb = lambda n, s, d: st.enter_context(nc.sbuf_tensor(f"{n}_ph{P._phase_idx}", s, d))
    rt = sb("m_rt", [128, 8 * T], F32)
    LN0 = sb("m_ln0", [128, 8 * T], F32)
    hm = sb("m_hm", [128, 8 * T], BF16)
    rb = hm
    UC = sb("m_UC", [128, NC * T], BF16)
    ZS = sb("m_ZS", [128, NC * T], BF16)
    Bb = sb("m_Bb", [128, NC * T], BF16)
    Cb = sb("m_Cb", [128, NC * T], BF16)
    DELTA = sb("m_DELTA", [128, NC * T], F32)
    SEL = sb("m_SEL", [32, 32 * 128], BF16)
    dtb_sb = sb("m_dtbsb", [64, T], BF16)
    bcrow = sb("m_bcrow", [32, T], BF16)
    ue = sb("m_ue", [128, T + 4], BF16)
    utail = sb("m_utail", [128, NC * 3], BF16)
    cacc = sb("m_cacc", [128, T], F32)
    accv = sb("m_accv", [128, T], F32)
    sp_e = sb("m_spe", [128, T], F32)
    sp_r = sb("m_spr", [128, T], F32)
    sg = sb("m_sg", [128, T], BF16)
    xt = sb("m_xt", [128, T], BF16)
    dab = [sb(f"m_dab{i}", [128, T], F32) for i in range(2)]
    dbu = [sb(f"m_dbu{i}", [128, T], BF16) for i in range(2)]
    h32 = [sb(f"m_h32{i}", [128, T], F32) for i in range(2)]
    yh2 = [sb(f"m_yh2{i}", [128, T], BF16) for i in range(2)]
    gacc = sb("m_gacc", [128, T], F32)
    yh = sb("m_yh", [128, T], BF16)
    y2 = sb("m_y2", [128, T], BF16)
    STATE = sb("m_state", [128, 256], F32)
    mu_r = sb("m_mu", [1, T], F32)
    ms_r = sb("m_ms", [1, T], F32)
    rs_r = sb("m_rs", [1, T], F32)
    ps = [st.enter_context(nc.psum_tensor(f"m_ps{i}_ph{P._phase_idx}", [128, 512], F32)) for i in range(8)]

    s_ld = P.sem("s_ld")
    s_st = P.sem("s_st")
    s_te = P.sem("s_te")
    s_ve = P.sem("s_ve")
    s_se = P.sem("s_se")
    s_w = P.sem("s_w")
    s_gp = P.sem("s_gp")

    bank_free = {}

    def te_bank(b):
        if b in bank_free:
            sem, cnt = bank_free[b]
            P.tensor.wait(sem, cnt)

    def consumed(b, sem):
        bank_free[b] = (sem, sem.n)

    in_v = in_dram.rearrange("(c p) t -> p c t", p=128)
    out_v = out_dram.rearrange("(c p) t -> p c t", p=128)
    rtv = rt[:].rearrange("p (c t) -> p c t", c=8)

    s_wg2 = P.sem("s_wg2")
    P.dma(SEL[:], in_sel, s_wg2, eng="gpsimd")
    for eng in ("vector", "scalar", "tensor"):
        P.plans[eng].wait(s_wg2)
    P.vector.run(lambda e: e.memset(STATE[:], 0.0))
    P.vector.run(lambda e: e.memset(utail[:], 0.0), inc=s_ve)
    for eng in ("vector", "scalar", "tensor"):
        P.plans[eng].wait(s_w)
        P.plans[eng].wait(P.sem("s_wg"))

    for i in range(n_t):
        t0 = i * T
        P.sync.wait(s_ve)
        P.sync.wait(s_st)
        P.dma(rtv, in_v[:, :, t0:t0 + T], s_ld)
        P.vector.wait(s_ld)
        P.vector.wait(s_te)
        P.vector.run(lambda e: e.tensor_copy(rb[:], rt[:]), inc=s_ve)
        P.tensor.wait(s_ve)
        te_bank(0)
        for c in range(8):
            P.tensor.run(lambda e, c=c: e.matmul(
                ps[0][0:1, 0:T], W["onesc"][:], rb[:, c * T:(c + 1) * T],
                start=(c == 0), stop=(c == 7)), inc=(s_te if c == 7 else None))
        P.scalar.wait(s_te)
        P.scalar.run(lambda e: e.activation(rb[:], rb[:], AF.Square), inc=s_se)
        P.tensor.wait(s_se)
        te_bank(1)
        for c in range(8):
            P.tensor.run(lambda e, c=c: e.matmul(
                ps[1][0:1, 0:T], W["onesc"][:], rb[:, c * T:(c + 1) * T],
                start=(c == 0), stop=(c == 7)), inc=(s_te if c == 7 else None))
        P.scalar.wait(s_te)
        P.scalar.run(lambda e: e.activation(mu_r[:], ps[0][0:1, 0:T], AF.Copy, scale=P.cc("r1024", 1)))
        P.scalar.run(lambda e: e.activation(ms_r[:], ps[1][0:1, 0:T], AF.Copy, scale=P.cc("r1024", 1)))
        P.scalar.run(lambda e: e.activation(rs_r[:], mu_r[:], AF.Square), inc=s_se)
        consumed(0, s_se)
        consumed(1, s_se)
        P.vector.wait(s_se)
        P.vector.run(lambda e: e.tensor_sub(ms_r[:], ms_r[:], rs_r[:]), inc=s_ve)
        P.scalar.wait(s_ve)
        P.scalar.run(lambda e: e.activation(ms_r[:], ms_r[:], AF.Ln, bias=P.cc("eps5", 1)))
        P.scalar.run(lambda e: e.activation(rs_r[:], ms_r[:], AF.Exp, scale=P.cc("negh", 1)), inc=s_se)
        P.tensor.wait(s_se)
        te_bank(0)
        te_bank(1)
        P.tensor.run(lambda e: e.matmul(ps[0][:, 0:T], W["ones1f"][:], mu_r[:], start=True, stop=True))
        P.tensor.run(lambda e: e.matmul(ps[1][:, 0:T], W["ones1f"][:], rs_r[:], start=True, stop=True), inc=s_te)
        P.vector.wait(s_te)
        mub = ps[0][:, 0:T].rearrange("p (o t) -> p o t", o=1).broadcast_to([128, 8, T])
        rsb = ps[1][:, 0:T].rearrange("p (o t) -> p o t", o=1).broadcast_to([128, 8, T])
        l3 = LN0[:].rearrange("p (c t) -> p c t", c=8)
        r3 = rt[:].rearrange("p (c t) -> p c t", c=8)
        P.vector.run(lambda e, l3=l3, r3=r3, mub=mub: e.tensor_sub(l3, r3, mub))
        P.vector.run(lambda e, l3=l3, rsb=rsb: e.tensor_mul(l3, l3, rsb))
        P.vector.run(lambda e, l3=l3, r3=r3: e.tensor_add(r3, r3, l3), inc=s_ve)
        consumed(0, s_ve)
        consumed(1, s_ve)
        P.vector.run(lambda e: e.tensor_copy(rb[:], LN0[:]), inc=s_ve)
        P.tensor.wait(s_ve)
        te_bank(0)
        for c in range(8):
            P.tensor.run(lambda e, c=c: e.matmul(
                ps[0][0:1, 0:T], W["onesc"][:], rb[:, c * T:(c + 1) * T],
                start=(c == 0), stop=(c == 7)), inc=(s_te if c == 7 else None))
        P.scalar.wait(s_te)
        P.scalar.run(lambda e: e.activation(rb[:], rb[:], AF.Square), inc=s_se)
        P.tensor.wait(s_se)
        te_bank(1)
        for c in range(8):
            P.tensor.run(lambda e, c=c: e.matmul(
                ps[1][0:1, 0:T], W["onesc"][:], rb[:, c * T:(c + 1) * T],
                start=(c == 0), stop=(c == 7)), inc=(s_te if c == 7 else None))
        P.scalar.wait(s_te)
        P.scalar.run(lambda e: e.activation(mu_r[:], ps[0][0:1, 0:T], AF.Copy, scale=P.cc("r1024", 1)))
        P.scalar.run(lambda e: e.activation(ms_r[:], ps[1][0:1, 0:T], AF.Copy, scale=P.cc("r1024", 1)))
        P.scalar.run(lambda e: e.activation(rs_r[:], mu_r[:], AF.Square), inc=s_se)
        consumed(0, s_se)
        consumed(1, s_se)
        P.vector.wait(s_se)
        P.vector.run(lambda e: e.tensor_sub(ms_r[:], ms_r[:], rs_r[:]), inc=s_ve)
        P.scalar.wait(s_ve)
        P.scalar.run(lambda e: e.activation(ms_r[:], ms_r[:], AF.Ln, bias=P.cc("eps6", 1)))
        P.scalar.run(lambda e: e.activation(rs_r[:], ms_r[:], AF.Exp, scale=P.cc("negh", 1)), inc=s_se)
        P.tensor.wait(s_se)
        te_bank(0)
        te_bank(1)
        P.tensor.run(lambda e: e.matmul(ps[0][:, 0:T], W["ones1f"][:], mu_r[:], start=True, stop=True))
        P.tensor.run(lambda e: e.matmul(ps[1][:, 0:T], W["ones1f"][:], rs_r[:], start=True, stop=True), inc=s_te)
        P.vector.wait(s_te)
        for c in range(8):
            cs = slice(c * T, (c + 1) * T)
            P.vector.run(lambda e, cs=cs: e.tensor_sub(accv[:], LN0[:, cs], ps[0][:, 0:T]))
            P.vector.run(lambda e: e.tensor_mul(accv[:], accv[:], ps[1][:, 0:T]))
            P.vector.run(lambda e, cs=cs, c=c: e.tensor_scalar(
                hm[:, cs], accv[:], W["lng"][:, c:c + 1], W["lnb"][:, c:c + 1],
                ALU.mult, ALU.add), inc=s_ve)
        consumed(0, s_ve)
        consumed(1, s_ve)
        P.tensor.wait(s_ve)
        te_bank(2)
        for mc in range(32):
            b = 3 + (mc % 2) * 4
            te_bank(b)
            for kc in range(8):
                P.tensor.run(lambda e, b=b, kc=kc, mc=mc: e.matmul(
                    ps[b][:, 0:T],
                    W["inW"][:, kc * 4096 + mc * 128: kc * 4096 + (mc + 1) * 128],
                    hm[:, kc * T:(kc + 1) * T],
                    start=(kc == 0), stop=(kc == 7)), inc=(s_te if kc == 7 else None))
            if mc < 16:
                P.vector.wait(s_te)
                P.vector.run(lambda e, mc=mc: e.tensor_copy(ue[:, 0:3], utail[:, mc * 3:mc * 3 + 3]))
                P.vector.run(lambda e, b=b: e.tensor_copy(ue[:, 3:3 + T], ps[b][:, 0:T]), inc=s_ve)
                consumed(b, s_ve)
                P.vector.run(lambda e, mc=mc: e.tensor_scalar(
                    cacc[:], ue[:, 0:T], W["cw"][:, mc * 4:mc * 4 + 1],
                    W["cb"][:, mc:mc + 1], ALU.mult, ALU.add))
                for k in range(1, 4):
                    P.vector.run(lambda e, k=k, mc=mc: e.scalar_tensor_tensor(
                        cacc[:], ue[:, k:k + T], W["cw"][:, mc * 4 + k:mc * 4 + k + 1], cacc[:],
                        ALU.mult, ALU.add), inc=(s_ve if k == 3 else None))
                P.vector.run(lambda e, mc=mc: e.tensor_copy(
                    utail[:, mc * 3:mc * 3 + 3], ue[:, T:T + 3]))
                P.scalar.wait(s_ve)
                P.scalar.run(lambda e, mc=mc: e.activation(
                    sg[:], cacc[:], AF.Sigmoid), inc=s_se)
                P.vector.wait(s_se)
                P.vector.run(lambda e, mc=mc: e.tensor_mul(
                    UC[:, mc * T:(mc + 1) * T], cacc[:], sg[:]), inc=s_ve)
                P.tensor.wait(s_ve)
                P.tensor.run(lambda e, mc=mc: e.matmul(
                    ps[2][0:96, 0:T], W["xpW"][:, mc * 96:(mc + 1) * 96],
                    UC[:, mc * T:(mc + 1) * T],
                    start=(mc == 0), stop=(mc == 15)), inc=(s_te if mc == 15 else None))
            else:
                zc = mc - 16
                P.scalar.wait(s_te)
                P.scalar.run(lambda e, b=b: e.activation(sg[:], ps[b][:, 0:T], AF.Sigmoid), inc=s_se)
                P.vector.wait(s_se)
                P.vector.run(lambda e, b=b, zc=zc: e.tensor_mul(
                    ZS[:, zc * T:(zc + 1) * T], ps[b][:, 0:T], sg[:]), inc=s_ve)
                consumed(b, s_ve)
        P.vector.wait(s_te)
        P.vector.run(lambda e: e.tensor_copy(dtb_sb[:], ps[2][0:64, 0:T]))
        P.vector.run(lambda e: e.tensor_copy(bcrow[:], ps[2][64:96, 0:T]), inc=s_ve)
        consumed(2, s_ve)
        P.tensor.wait(s_ve)
        for n in range(32):
            b = n % 2
            te_bank(b)
            P.tensor.run(lambda e, b=b, n=n: e.matmul(
                ps[b][:, 0:T], SEL[:, n * 128:(n + 1) * 128], bcrow[:, 0:T],
                start=True, stop=True), inc=s_te)
            P.vector.wait(s_te)
            dst = Bb if n < 16 else Cb
            nn = n % 16
            P.vector.run(lambda e, b=b, dst=dst, nn=nn: e.tensor_copy(
                dst[:, nn * T:(nn + 1) * T], ps[b][:, 0:T]), inc=s_ve)
            consumed(b, s_ve)
        for c in range(NC):
            b = 2 + (c % 2)
            te_bank(b)
            P.tensor.run(lambda e, b=b, c=c: e.matmul(
                ps[b][:, 0:T], W["dtW"][:, c * 128:(c + 1) * 128], dtb_sb[:],
                start=True, stop=True), inc=s_te)
            P.scalar.wait(s_te)
            P.scalar.wait(s_ve)
            P.scalar.run(lambda e, b=b, c=c: e.activation(
                sp_e[:], ps[b][:, 0:T], AF.Abs, bias=W["dtb"][:, c:c + 1]))
            P.scalar.run(lambda e: e.activation(sp_e[:], sp_e[:], AF.Exp, scale=P.cc("neg1")))
            P.scalar.run(lambda e: e.activation(sp_e[:], sp_e[:], AF.Ln, bias=P.cc("one")))
            P.scalar.run(lambda e, b=b, c=c: e.activation(
                sp_r[:], ps[b][:, 0:T], AF.Relu, bias=W["dtb"][:, c:c + 1]), inc=s_se)
            consumed(b, s_se)
            P.vector.wait(s_se)
            P.vector.run(lambda e, c=c: e.tensor_add(
                DELTA[:, c * T:(c + 1) * T], sp_e[:], sp_r[:]), inc=s_ve)
        for b in range(8):
            te_bank(b)
        scan_ve_marks = {}
        for c in range(NC):
            ds = slice(c * T, (c + 1) * T)
            P.vector.wait(s_gp)
            P.vector.run(lambda e, ds=ds, c=c: e.tensor_mul(xt[:], DELTA[:, ds], UC[:, ds]), inc=s_ve)
            P.scalar.wait(s_ve)
            P.vector.run(lambda e: e.tensor_mul(dbu[0][:], xt[:], Bb[:, 0:T]))
            gp_marks = {}
            for n in range(16):
                pn = n % 2
                col = c * 16 + n
                if col >= 2:
                    P.scalar.wait(s_ve, scan_ve_marks[col - 2])
                P.scalar.run(lambda e, pn=pn, col=col, ds=ds: e.activation(
                    dab[pn][:], DELTA[:, ds], AF.Exp, scale=W["An"][:, col:col + 1]), inc=s_se)
                if n >= 2:
                    P.vector.wait(s_gp, gp_marks[n - 2])
                P.vector.wait(s_se)
                P.vector.run(lambda e, pn=pn, col=col: e.tensor_tensor_scan(
                    h32[pn][:], dab[pn][:], dbu[pn][:],
                    STATE[:, col:col + 1], ALU.mult, ALU.add))
                if n < 15:
                    P.vector.run(lambda e, pn=pn, n=n: e.tensor_mul(
                        dbu[1 - pn][:], xt[:], Bb[:, (n + 1) * T:(n + 2) * T]))
                P.vector.run(lambda e, pn=pn, n=n: e.tensor_mul(
                    yh2[pn][:], h32[pn][:], Cb[:, n * T:(n + 1) * T]), inc=s_ve)
                scan_ve_marks[col] = s_ve.n
                P.gpsimd.wait(s_ve)
                if n == 0:
                    P.gpsimd.run(lambda e, pn=pn: e.tensor_copy(gacc[:], yh2[pn][:]))
                else:
                    P.gpsimd.run(lambda e, pn=pn: e.tensor_add(gacc[:], gacc[:], yh2[pn][:]))
                P.gpsimd.run(lambda e, pn=pn, col=col: e.tensor_copy(
                    STATE[:, col:col + 1], h32[pn][:, T - 1:T]), inc=s_gp)
                gp_marks[n] = s_gp.n
            if c > 0:
                P.vector.wait(s_te)
            P.vector.wait(s_gp)
            P.vector.run(lambda e, c=c: e.scalar_tensor_tensor(
                accv[:], UC[:, c * T:(c + 1) * T], W["Dv"][:, c:c + 1], gacc[:],
                ALU.mult, ALU.add))
            P.vector.run(lambda e, c=c: e.tensor_mul(y2[:], accv[:], ZS[:, c * T:(c + 1) * T]), inc=s_ve)
            P.tensor.wait(s_ve)
            for mc in range(8):
                P.tensor.run(lambda e, b=mc, mc=mc, c=c: e.matmul(
                    ps[b][:, 0:T],
                    W["opW"][:, c * 1024 + mc * 128: c * 1024 + (mc + 1) * 128],
                    y2[:],
                    start=(c == 0), stop=(c == NC - 1)),
                    inc=(s_te if mc == 7 else None))
        P.vector.wait(s_te)
        for mc in range(8):
            P.vector.run(lambda e, b=mc, mc=mc: e.tensor_add(
                rt[:, mc * T:(mc + 1) * T], rt[:, mc * T:(mc + 1) * T], ps[b][:, 0:T]),
                inc=s_ve)
        for b in range(8):
            consumed(b, s_ve)
        P.sync.wait(s_ve)
        P.dma(out_v[:, :, t0:t0 + T], rtv, s_st)
    if dbg is not None:
        P.plans["gpsimd"].wait(s_ve)
        P.plans["gpsimd"].wait(s_se)
        for k, buf in [("UC", UC), ("ZS", ZS), ("Bb", Bb), ("Cb", Cb),
                       ("DELTA", DELTA), ("hm", hm), ("LN0", LN0)]:
            if k in dbg:
                P.dma(dbg[k], buf[:], P.sem("s_stg"), eng="gpsimd")
        P.sync.wait(P.sem("s_stg"))
    P.sync.wait(s_st)



def emit_load_ffn_weights(P, st, dram, phase):
    nc = P.nc
    sb = lambda n, s, d: st.enter_context(nc.sbuf_tensor(f"{n}_ph{P._phase_idx}", s, d))
    W = {}
    s_w = P.sem("s_w")
    s_wg = P.sem(f"s_wgf{P._phase_idx}")
    if phase == "A":
        W["wq"] = sb("f_wq", [128, 8 * 1024], BF16)
        W["wdn"] = sb("f_wdn", [128, 8 * 64], BF16)
        W["wupk"] = sb("f_wupk", [64, 16 * 64], BF16)
        W["wupv"] = sb("f_wupv", [65, 2048], BF16)
        W["wo"] = sb("f_wo", [128, 8 * 1024], BF16)
        W["wg"] = sb("f_wg", [128, 64], BF16)
        W["g1"] = sb("f_g1", [128, 8], F32)
        W["g2"] = sb("f_g2", [128, 8], F32)
        W["bq"] = sb("f_bq", [128, 8], F32)
        W["bo"] = sb("f_bo", [128, 8], F32)
        W["bdn"] = sb("f_bdn", [64, 1], F32)
        W["bupk"] = sb("f_bupk", [64, 16], F32)
        W["bg"] = sb("f_bg", [8, 1], F32)
        W["bgr"] = sb("f_bgr", [128, 8], F32)
        W["eye"] = sb("f_eye", [128, 128], F32)
        W["mask"] = sb("f_mask", [128, 512], F32)
        W["sel"] = sb("f_sel", [32, 32 * 128], BF16)
        W["onesc"] = sb("f_onesc", [128, 1], BF16)
        W["ones1f"] = sb("f_ones1f", [1, 128], F32)
        for k, src in [("wq", "w_q"), ("wdn", "w_down"), ("wupk", "w_upk"),
                       ("wupv", "w_upv"), ("wo", "w_o"), ("wg", "w_g"), ("sel", "c_sel")]:
            P.dma(W[k][:], dram[src], s_wg, eng="gpsimd")
        for k, src in [("g1", "v_g1"), ("g2", "v_g2"), ("bq", "v_bq"), ("bo", "v_bo"),
                       ("bdn", "v_bdown"), ("bupk", "v_bupk"),
                       ("bg", "v_bg"), ("bgr", "c_bgr"), ("eye", "c_eye"), ("mask", "c_mask")]:
            P.dma(W[k][:], dram[src], s_w, eng="sync")
        P.vector.run(lambda e, W=W: e.memset(W["onesc"][:], 1.0))
        P.vector.run(lambda e, W=W: e.memset(W["ones1f"][:], 1.0), inc=P.sem("s_ve"))
    else:
        W["we"] = sb("f_we", [128, 8 * 8 * 1024], BF16)
        W["sel"] = sb("f_sel", [32, 32 * 128], BF16)
        P.dma(W["we"][:], dram["w_e"], s_wg, eng="gpsimd")
        P.dma(W["sel"][:], dram["c_sel"], s_wg, eng="gpsimd")
    for eng in ("vector", "scalar", "tensor"):
        P.plans[eng].wait(s_w)
        P.plans[eng].wait(s_wg)
    return W


def mask_const():
    m = np.full((128, 512), NEG_BIG, np.float32)
    for row in range(128):
        j = (row // 32) % 2
        for q in range(8):
            base = q * 64 + j * 32
            m[row, base:base + 16] = 0.0
    return m


def emit_ffnA(P, st, W, in_dram, s_out_dram, h2_out_dram, gate_out_dram, L=4096, T=256, dbg=None, do_att=True, do_gate=True):
    nc = P.nc
    n_t = L // T
    NP = T // 2
    NG = NP // 32
    sb = lambda n, s, d: st.enter_context(nc.sbuf_tensor(f"{n}_ph{P._phase_idx}", s, d))
    rt = sb("f_rt", [128, 8 * T], F32)
    LN0 = sb("f_ln0", [128, 8 * T], F32)
    X2 = sb("f_x2", [128, 8 * T], F32)
    h1 = sb("f_h1", [128, 8 * T], BF16)
    H2 = sb("f_h2", [128, 8 * T], BF16)
    rb = H2
    qzz = sb("f_qzz", [128, 64 * NP], BF16)
    kzz = sb("f_kzz", [128, 64 * NP], BF16)
    vzz = sb("f_vzz", [64, 32 * 128], BF16)
    lat = sb("f_lat", [65, T], BF16)
    Ohp = sb("f_ohp", [128, 8 * T], BF16)
    Asb = sb("f_asb", [128, 512], F32)
    Abf = sb("f_abf", [128, 512], BF16)
    At = sb("f_at", [128, 512], BF16)
    mx = sb("f_mx", [128, 8], F32)
    sm = sb("f_sm", [128, 8], F32)
    rc = sb("f_rc", [128, 8], F32)
    gt = sb("f_gt", [8, T], F32)
    gsb = sb("f_gsb", [128, 8 * (T // 128)], F32)
    bq8 = sb("f_bq8", [128, 8], F32)
    mu_r = sb("f_mu", [1, T], F32)
    ms_r = sb("f_ms", [1, T], F32)
    rs_r = sb("f_rs", [1, T], F32)
    ps = st.enter_context(nc.psum_tensor(f"f_ps_ph{P._phase_idx}", [128, 4096], F32))
    if dbg is not None:
        dbg_sraw = sb("dbg_sraw", [64, 4 * 512], F32)
        dbg_abf = sb("dbg_abf", [64, 4 * 512], F32)
        dbg_at = sb("dbg_at", [64, 4 * 512], F32)
        dbg_vzz = sb("dbg_vzz", [64, 4096], BF16)

    s_ld = P.sem("s_ld")
    s_st = P.sem("s_st")
    s_te = P.sem("s_te")
    s_ve = P.sem("s_ve")
    s_se = P.sem("s_se")
    s_gp = P.sem("s_gp")

    bank_free = {}

    def te_bank(b):
        for sem, cnt in bank_free.pop(b, []):
            P.tensor.wait(sem, cnt)

    def consumed(b, sem):
        bank_free.setdefault(b, []).append((sem, sem.n))

    def B(b, rows=128, cols=512, co=0):
        return ps[0:rows, b * 512 + co: b * 512 + co + cols]

    in_v = in_dram.rearrange("(c p) t -> p c t", p=128)
    s_v = s_out_dram.rearrange("(c p) t -> p c t", p=128)
    h2_v = h2_out_dram.rearrange("(c p) t -> p c t", p=128)
    rtv = rt[:].rearrange("p (c t) -> p c t", c=8)
    h2v = H2[:].rearrange("p (c t) -> p c t", c=8)

    P.vector.run(lambda e: e.memset(qzz[:], 0.0))
    P.vector.run(lambda e: e.memset(kzz[:], 0.0))
    P.vector.run(lambda e: e.memset(vzz[:], 0.0))
    P.vector.run(lambda e: e.memset(lat[64:65, :], 1.0))
    P.vector.run(lambda e: e.tensor_scalar_mul(bq8[:], W["bq"][:], 0.125), inc=s_ve)

    def ln_stats(src_bf, eps_name, out_mu=True):
        P.tensor.wait(s_ve)
        te_bank(0)
        te_bank(1)
        for c in range(8):
            P.tensor.run(lambda e, c=c: e.matmul(
                B(0, 1, T), W["onesc"][:], src_bf[:, c * T:(c + 1) * T],
                start=(c == 0), stop=(c == 7)), inc=(s_te if c == 7 else None))
        P.scalar.wait(s_te)
        P.scalar.run(lambda e: e.activation(src_bf[:], src_bf[:], AF.Square), inc=s_se)
        P.tensor.wait(s_se)
        for c in range(8):
            P.tensor.run(lambda e, c=c: e.matmul(
                B(1, 1, T), W["onesc"][:], src_bf[:, c * T:(c + 1) * T],
                start=(c == 0), stop=(c == 7)), inc=(s_te if c == 7 else None))
        P.scalar.wait(s_te)
        if out_mu:
            P.scalar.run(lambda e: e.activation(mu_r[:], B(0, 1, T), AF.Copy, scale=P.cc("r1024", 1)))
            P.scalar.run(lambda e: e.activation(ms_r[:], B(1, 1, T), AF.Copy, scale=P.cc("r1024", 1)))
            P.scalar.run(lambda e: e.activation(rs_r[:], mu_r[:], AF.Square), inc=s_se)
            consumed(0, s_se)
            consumed(1, s_se)
            P.vector.wait(s_se)
            P.vector.run(lambda e: e.tensor_sub(ms_r[:], ms_r[:], rs_r[:]), inc=s_ve)
            P.scalar.wait(s_ve)
        else:
            P.scalar.run(lambda e: e.activation(ms_r[:], B(1, 1, T), AF.Copy, scale=P.cc("r1024", 1)), inc=s_se)
            consumed(0, s_se)
            consumed(1, s_se)
        P.scalar.run(lambda e, eps=eps_name: e.activation(ms_r[:], ms_r[:], AF.Ln, bias=P.cc(eps, 1)))
        P.scalar.run(lambda e: e.activation(rs_r[:], ms_r[:], AF.Exp, scale=P.cc("negh", 1)), inc=s_se)
        P.tensor.wait(s_se)
        te_bank(0)
        te_bank(1)
        if out_mu:
            P.tensor.run(lambda e: e.matmul(B(0, cols=T), W["ones1f"][:], mu_r[:], start=True, stop=True))
        P.tensor.run(lambda e: e.matmul(B(1, cols=T), W["ones1f"][:], rs_r[:], start=True, stop=True), inc=s_te)
        P.vector.wait(s_te)

    for i in range(n_t):
        t0 = i * T
        P.sync.wait(s_ve)
        P.sync.wait(s_st)
        P.dma(rtv, in_v[:, :, t0:t0 + T], s_ld)
        P.vector.wait(s_ld)
        P.vector.wait(s_te)
        P.vector.wait(s_st)
        P.vector.run(lambda e: e.tensor_copy(rb[:], rt[:]), inc=s_ve)
        ln_stats(rb, "eps5", out_mu=True)
        mub = B(0, cols=T).rearrange("p (o t) -> p o t", o=1).broadcast_to([128, 8, T])
        rsb = B(1, cols=T).rearrange("p (o t) -> p o t", o=1).broadcast_to([128, 8, T])
        l3 = LN0[:].rearrange("p (c t) -> p c t", c=8)
        r3 = rt[:].rearrange("p (c t) -> p c t", c=8)
        P.vector.run(lambda e, l3=l3, r3=r3, mub=mub: e.tensor_sub(l3, r3, mub))
        P.vector.run(lambda e, l3=l3, rsb=rsb: e.tensor_mul(l3, l3, rsb), inc=s_ve)
        consumed(0, s_ve)
        consumed(1, s_ve)
        P.vector.run(lambda e: e.tensor_copy(rb[:], LN0[:]), inc=s_ve)
        ln_stats(rb, "eps6", out_mu=False)
        for c in range(8):
            cs = slice(c * T, (c + 1) * T)
            P.vector.run(lambda e, cs=cs, c=c: e.scalar_tensor_tensor(
                h1[:, cs], LN0[:, cs], W["g1"][:, c:c + 1], B(1, cols=T),
                ALU.mult, ALU.mult), inc=s_ve)
        consumed(1, s_ve)
        P.tensor.wait(s_ve)
        for hp in range(8):
            b = 2 + hp % 2
            te_bank(b)
            for kc in range(8):
                P.tensor.run(lambda e, b=b, hp=hp, kc=kc: e.matmul(
                    B(b, cols=T), W["wq"][:, kc * 1024 + hp * 128: kc * 1024 + (hp + 1) * 128],
                    h1[:, kc * T:(kc + 1) * T],
                    start=(kc == 0), stop=(kc == 7)), inc=(s_te if kc == 7 else None))
            P.vector.wait(s_te)
            P.scalar.wait(s_te)
            for hh in range(2):
                h = 2 * hp + hh
                for j in range(2):
                    src = B(b, cols=T).rearrange("p (pp two) -> p pp two", two=2)[hh * 64:(hh + 1) * 64, :, j]
                    dst = qzz[:].rearrange("p (pp cc) -> p pp cc", cc=64)[j * 64:(j + 1) * 64, :, j * 32 + h]
                    if SCE_EVAC and (hh + j) % 2 == 1:
                        P.scalar.run(lambda e, src=src, dst=dst, hh=hh, hp=hp: e.activation(
                            dst, src, AF.Identity, bias=bq8[hh * 64:(hh + 1) * 64, hp:hp + 1],
                            scale=P.cc("q8", 64)), inc=s_se)
                    else:
                        P.vector.run(lambda e, src=src, dst=dst, hh=hh, hp=hp: e.tensor_scalar(
                            dst, src, W["bq"][hh * 64:(hh + 1) * 64, hp:hp + 1],
                            P.cc("q8", 64), ALU.add, ALU.mult), inc=s_ve)
            consumed(b, s_ve)
            consumed(b, s_se)
        te_bank(4)
        P.tensor.wait(s_ve)
        for kc in range(8):
            P.tensor.run(lambda e, kc=kc: e.matmul(
                B(4, 64, T), W["wdn"][:, kc * 64:(kc + 1) * 64],
                h1[:, kc * T:(kc + 1) * T],
                start=(kc == 0), stop=(kc == 7)), inc=(s_te if kc == 7 else None))
        P.vector.wait(s_te)
        P.vector.run(lambda e: e.tensor_scalar(
            lat[0:64, :], B(4, 64, T), W["bdn"][:, 0:1], None, ALU.add), inc=s_ve)
        consumed(4, s_ve)
        P.tensor.wait(s_ve)
        for g in range(16):
            b = 2 + g % 2
            te_bank(b)
            P.tensor.run(lambda e, b=b, g=g: e.matmul(
                B(b, 64, T), W["wupk"][:, g * 64:(g + 1) * 64], lat[0:64, :],
                start=True, stop=True), inc=s_te)
            P.vector.wait(s_te)
            P.scalar.wait(s_te)
            for j in range(2):
                src = B(b, 64, T).rearrange("p (pp two) -> p pp two", two=2)[:, :, j]
                dst = kzz[:].rearrange("p (pp cc) -> p pp cc", cc=64)[j * 64:(j + 1) * 64, :, j * 32 + g]
                if SCE_EVAC and j == 1:
                    P.scalar.run(lambda e, src=src, dst=dst, g=g: e.activation(
                        dst, src, AF.Identity, bias=W["bupk"][:, g:g + 1]), inc=s_se)
                else:
                    P.vector.run(lambda e, src=src, dst=dst, g=g: e.tensor_scalar(
                        dst, src, W["bupk"][:, g:g + 1], None, ALU.add), inc=s_ve)
            consumed(b, s_ve)
            consumed(b, s_se)
        if not do_att:
            P.vector.run(lambda e: e.memset(Ohp[:], 0.0), inc=s_ve)
        for g0 in range(NG if do_att else 0):
            tok0 = g0 * 64
            te_bank(4)
            te_bank(5)
            P.tensor.wait(s_ve)
            P.tensor.wait(s_se)
            for dq in range(16):
                P.tensor.run(lambda e, dq=dq, tok0=tok0: e.matmul(
                    ps[:, 4 * 512 + dq * 64: 4 * 512 + (dq + 1) * 64],
                    W["wupv"][:, dq * 128:(dq + 1) * 128],
                    lat[:, tok0:tok0 + 64],
                    start=True, stop=True), inc=(s_te if dq == 15 else None))
            P.vector.wait(s_te)
            P.scalar.wait(s_te)
            for m in range(4):
                for j in range(2):
                    src = ps[32 * m:32 * m + 16, 4 * 512: 4 * 512 + 1024].rearrange(
                        "p (dh q pp two) -> p dh q pp two", dh=8, q=2, two=2)[:, :, :, :, j]
                    dst = vzz[j * 32:j * 32 + 16, :].rearrange(
                        "p (pp jj dh q mm) -> p dh q pp jj mm", jj=2, dh=8, q=2, mm=4)[:, :, :, :, j, m]
                    if SCE_EVAC and (m + j) % 2 == 1:
                        P.scalar.run(lambda e, src=src, dst=dst: e.copy(dst, src), inc=s_se)
                    else:
                        P.vector.run(lambda e, src=src, dst=dst: e.tensor_copy(dst, src), inc=s_ve)
            consumed(4, s_ve)
            consumed(4, s_se)
            consumed(5, s_ve)
            consumed(5, s_se)
            if dbg is not None and i == 0 and g0 == 0:
                P.vector.wait(s_se)
                P.vector.run(lambda e: e.tensor_copy(dbg_vzz[:], vzz[:]), inc=s_ve)
            for b in range(4):
                te_bank(b)
            P.tensor.wait(s_ve)
            P.tensor.wait(s_se)
            for p in range(32):
                pa = tok0 // 2 + p
                P.tensor.run(lambda e, p=p, pa=pa: e.matmul(
                    ps[0:64, (p // 8) * 512 + (p % 8) * 64: (p // 8) * 512 + (p % 8) * 64 + 64],
                    qzz[:, pa * 64:(pa + 1) * 64], kzz[:, pa * 64:(pa + 1) * 64],
                    start=True, stop=True), inc=(s_te if p == 31 else None))
            for sub in range(4):
                P.vector.wait(s_te)
                if dbg is not None and i == 0 and sub == 0:
                    P.vector.run(lambda e, g0=g0: e.tensor_copy(
                        dbg_sraw[:, g0 * 512:(g0 + 1) * 512], ps[0:64, 0:512]), inc=s_ve)
                P.vector.run(lambda e, sub=sub: e.tensor_add(
                    Asb[0:64, :], ps[0:64, sub * 512:(sub + 1) * 512],
                    W["mask"][0:64, :]), inc=s_ve)
                consumed(sub, s_ve)
                v3 = Asb[0:64, :].rearrange("p (q c) -> p q c", c=64)
                P.vector.run(lambda e, v3=v3: e.reduce_max(mx[0:64, :], v3, AX.X), inc=s_ve)
                P.vector.wait(s_ve)
                mxb = mx[0:64, :].rearrange("p (q o) -> p q o", o=1).broadcast_to([64, 8, 64])
                P.vector.run(lambda e, v3=v3, mxb=mxb: e.tensor_sub(v3, v3, mxb), inc=s_ve)
                P.scalar.wait(s_ve)
                P.scalar.run(lambda e: e.activation(Abf[0:64, :], Asb[0:64, :], AF.Exp), inc=s_se)
                P.vector.wait(s_se)
                a3 = Abf[0:64, :].rearrange("p (q c) -> p q c", c=64)
                P.vector.run(lambda e, a3=a3: e.reduce_sum(sm[0:64, :], a3, AX.X), inc=s_ve)
                P.vector.wait(s_ve)
                P.vector.run(lambda e: e.reciprocal(rc[0:64, :], sm[0:64, :]), inc=s_ve)
                P.vector.wait(s_ve)
                rcb = rc[0:64, :].rearrange("p (q o) -> p q o", o=1).broadcast_to([64, 8, 64])
                P.vector.run(lambda e, a3=a3, rcb=rcb: e.tensor_mul(a3, a3, rcb))
                P.vector.run(lambda e: e.transpose(At[0:64, :], Abf[0:64, :]), inc=s_ve)
                if dbg is not None and i == 0 and sub == 0:
                    P.vector.run(lambda e, g0=g0: e.tensor_copy(
                        dbg_abf[:, g0 * 512:(g0 + 1) * 512], Abf[0:64, :]))
                    P.vector.run(lambda e, g0=g0: e.tensor_copy(
                        dbg_at[:, g0 * 512:(g0 + 1) * 512], At[0:64, :]), inc=s_ve)
                bb = 4 + sub
                te_bank(bb)
                P.tensor.wait(s_ve)
                for qp in range(8):
                    p = sub * 8 + qp
                    P.tensor.run(lambda e, p=p, qp=qp, bb=bb: e.matmul(
                        ps[:, bb * 512 + qp * 64: bb * 512 + qp * 64 + 64],
                        vzz[:, p * 128:(p + 1) * 128],
                        At[0:64, qp * 64:(qp + 1) * 64],
                        start=True, stop=True), inc=(s_te if qp == 7 else None))
            P.vector.wait(s_te)
            P.scalar.wait(s_te)
            for h in range(16):
                for jm in range(2):
                    src = ps[jm * 64:(jm + 1) * 64, 4 * 512: 4 * 512 + 2048].rearrange(
                        "p (pp cc) -> p pp cc", cc=64)[:, :, jm * 32 + h]
                    hp = h // 2
                    dst = Ohp[(h % 2) * 64:(h % 2) * 64 + 64,
                              hp * T + tok0: hp * T + tok0 + 64].rearrange(
                        "p (pp two) -> p pp two", two=2)[:, :, jm]
                    if SCE_EVAC and (h + jm) % 2 == 1:
                        P.scalar.run(lambda e, src=src, dst=dst: e.copy(dst, src), inc=s_se)
                    else:
                        P.vector.run(lambda e, src=src, dst=dst: e.tensor_copy(dst, src), inc=s_ve)
            for b in range(4, 8):
                consumed(b, s_ve)
                consumed(b, s_se)
        P.tensor.wait(s_ve)
        P.tensor.wait(s_se)
        for mc in range(8):
            b = 2 + mc % 2
            te_bank(b)
            for kc in range(8):
                P.tensor.run(lambda e, b=b, mc=mc, kc=kc: e.matmul(
                    B(b, cols=T), W["wo"][:, kc * 1024 + mc * 128: kc * 1024 + (mc + 1) * 128],
                    Ohp[:, kc * T:(kc + 1) * T],
                    start=(kc == 0), stop=(kc == 7)), inc=(s_te if kc == 7 else None))
            P.vector.wait(s_te)
            cs = slice(mc * T, (mc + 1) * T)
            P.vector.run(lambda e, b=b, cs=cs, mc=mc: e.scalar_tensor_tensor(
                X2[:, cs], B(b, cols=T), W["bo"][:, mc:mc + 1], LN0[:, cs],
                ALU.add, ALU.add), inc=s_ve)
            consumed(b, s_ve)
        P.vector.run(lambda e: e.tensor_copy(rb[:], X2[:]), inc=s_ve)
        ln_stats(rb, "eps6", out_mu=False)
        for c in range(8):
            cs = slice(c * T, (c + 1) * T)
            P.vector.run(lambda e, cs=cs, c=c: e.scalar_tensor_tensor(
                H2[:, cs], X2[:, cs], W["g2"][:, c:c + 1], B(1, cols=T),
                ALU.mult, ALU.mult), inc=s_ve)
        consumed(1, s_ve)
        if not do_gate:
            P.vector.run(lambda e: e.memset(gt[:], 0.125), inc=s_ve)
        te_bank(2)
        te_bank(3)
        P.tensor.wait(s_ve)
        for tb in range(T // 128 if do_gate else 0):
            for kc in range(8):
                P.tensor.run(lambda e, tb=tb, kc=kc: e.matmul(
                    B(2, 128, 8, co=tb * 8),
                    H2[:, kc * T + tb * 128: kc * T + tb * 128 + 128],
                    W["wg"][:, kc * 8:(kc + 1) * 8],
                    start=(kc == 0), stop=(kc == 7)),
                    inc=(s_te if kc == 7 else None))
        P.vector.wait(s_te)
        for tb in range(T // 128 if do_gate else 0):
            P.vector.run(lambda e, tb=tb: e.tensor_add(
                gsb[:, tb * 8:(tb + 1) * 8], B(2, 128, 8, co=tb * 8),
                W["bgr"][0:1, :].broadcast_to([128, 8]) if False else W["bgr"][:, :]), inc=s_ve)
        consumed(2, s_ve)
        for tb in range(T // 128 if do_gate else 0):
            gs = gsb[:, tb * 8:(tb + 1) * 8]
            g3 = gs.rearrange("p (q c) -> p q c", c=8)
            P.vector.run(lambda e, gs=gs, g3=g3: e.reduce_max(mx[:, 0:1], g3, AX.X), inc=s_ve)
            P.vector.wait(s_ve)
            P.vector.run(lambda e, gs=gs: e.tensor_sub(
                gs, gs, mx[:, 0:1].broadcast_to([128, 8])), inc=s_ve)
            P.scalar.wait(s_ve)
            P.scalar.run(lambda e, gs=gs: e.activation(gs, gs, AF.Exp), inc=s_se)
            P.vector.wait(s_se)
            P.vector.run(lambda e, gs=gs: e.reduce_sum(
                sm[:, 0:1], gs.rearrange("p (q c) -> p q c", c=8), AX.X), inc=s_ve)
            P.vector.wait(s_ve)
            P.vector.run(lambda e: e.reciprocal(rc[:, 0:1], sm[:, 0:1]), inc=s_ve)
            P.vector.wait(s_ve)
            P.vector.run(lambda e, gs=gs: e.tensor_mul(
                gs, gs, rc[:, 0:1].broadcast_to([128, 8])), inc=s_ve)
        if do_gate:
            P.tensor.wait(s_ve)
            for tb in range(T // 128):
                P.tensor.run(lambda e, tb=tb: e.transpose(
                    B(3, 8, 128, co=tb * 128), gsb[:, tb * 8:(tb + 1) * 8], W["eye"][:]),
                    inc=(s_te if tb == (T // 128) - 1 else None))
            P.vector.wait(s_te)
            for tb in range(T // 128):
                P.vector.run(lambda e, tb=tb: e.tensor_copy(
                    gt[:, tb * 128:(tb + 1) * 128], B(3, 8, 128, co=tb * 128)), inc=s_ve)
            consumed(3, s_ve)
        for c in range(8):
            cs = slice(c * T, (c + 1) * T)
            P.vector.run(lambda e, cs=cs: e.scalar_tensor_tensor(
                rt[:, cs], X2[:, cs], P.cc("half"), rt[:, cs],
                ALU.mult, ALU.add), inc=s_ve)
        P.sync.wait(s_ve)
        P.sync.wait(s_se)
        P.dma(s_v[:, :, t0:t0 + T], rtv, s_st)
        P.dma(h2_v[:, :, t0:t0 + T], h2v, s_st)
        P.dma(gate_out_dram[:, t0:t0 + T], gt[:], s_st)
    if dbg is not None:
        P.sync.wait(s_ve)
        P.dma(dbg["sraw"], dbg_sraw[:], s_st)
        P.dma(dbg["abf"], dbg_abf[:], s_st)
        P.dma(dbg["at"], dbg_at[:], s_st)
        P.plans["gpsimd"].wait(s_ve)
        P.dma(dbg["vzz"], dbg_vzz[:], P.sem("s_stg2"), eng="gpsimd")
        P.sync.wait(P.sem("s_stg2"))
    P.sync.wait(s_st)


def emit_ffnB(P, st, W, s_in_dram, h2_in_dram, gate_in_dram, out_dram, L=4096, T=256, dbg=None):
    nc = P.nc
    n_t = L // T
    sb = lambda n, s, d: st.enter_context(nc.sbuf_tensor(f"{n}_ph{P._phase_idx}", s, d))
    stt = sb("b_st", [128, 8 * T], F32)
    H2 = sb("b_h2", [128, 8 * T], BF16)
    HS = sb("b_hs", [128, 8 * T], BF16)
    gt = sb("b_gt", [8, T], F32)
    gtb = sb("b_gtb", [32, T], BF16)
    gb8 = sb("b_gb8", [128, 8 * T], BF16)
    ps = st.enter_context(nc.psum_tensor(f"b_ps_ph{P._phase_idx}", [128, 4096], F32))
    if dbg is not None:
        dbg_gb = sb("dbg_gb", [128, 8 * T], F32)
        dbg_hs = sb("dbg_hs", [128, 8 * T], F32)
    s_ld = P.sem("s_ld")
    s_st = P.sem("s_st")
    s_te = P.sem("s_te")
    s_ve = P.sem("s_ve")
    bank_free = {}

    def te_bank(b):
        for sem, cnt in bank_free.pop(b, []):
            P.tensor.wait(sem, cnt)

    def consumed(b, sem):
        bank_free.setdefault(b, []).append((sem, sem.n))

    s_v = s_in_dram.rearrange("(c p) t -> p c t", p=128)
    h2_v = h2_in_dram.rearrange("(c p) t -> p c t", p=128)
    o_v = out_dram.rearrange("(c p) t -> p c t", p=128)
    stv = stt[:].rearrange("p (c t) -> p c t", c=8)
    h2v = H2[:].rearrange("p (c t) -> p c t", c=8)

    P.vector.run(lambda e: e.memset(gtb[:], 0.0), inc=s_ve)
    for i in range(n_t):
        t0 = i * T
        P.sync.wait(s_ve)
        P.sync.wait(s_st)
        P.dma(stv, s_v[:, :, t0:t0 + T], s_ld)
        P.dma(h2v, h2_v[:, :, t0:t0 + T], s_ld)
        P.dma(gt[:], gate_in_dram[:, t0:t0 + T], s_ld)
        P.vector.wait(s_ld)
        P.vector.run(lambda e: e.tensor_copy(gtb[0:8, :], gt[:]), inc=s_ve)
        P.tensor.wait(s_ve)
        te_bank(0)
        te_bank(1)
        for e_ in range(8):
            bco = (e_ % 2) * 512
            if e_ >= 2:
                P.tensor.wait(s_ve)
            P.tensor.run(lambda e, e_=e_, bco=bco: e.matmul(
                ps[:, bco: bco + T],
                W["sel"][0:32, e_ * 128:(e_ + 1) * 128], gtb[:],
                start=True, stop=True), inc=s_te)
            P.vector.wait(s_te)
            P.vector.run(lambda e, e_=e_, bco=bco: e.tensor_copy(
                gb8[:, e_ * T:(e_ + 1) * T], ps[:, bco: bco + T]), inc=s_ve)
        consumed(0, s_ve)
        consumed(1, s_ve)
        for e_ in range(8):
            if e_ > 0:
                P.vector.wait(s_te)
            gbv = gb8[:, 0:8 * T].rearrange("p (o t) -> p o t", o=8)[:, 0:1, :].broadcast_to([128, 8, T]) if False else None
            gview = gb8[:, :].rearrange("p (ee t) -> p ee t", ee=8)[:, e_:e_ + 1, :].broadcast_to([128, 8, T])
            P.vector.run(lambda e, e_=e_, gview=gview: e.tensor_mul(
                HS[:].rearrange("p (c t) -> p c t", c=8), H2[:].rearrange("p (c t) -> p c t", c=8),
                gview), inc=s_ve)
            P.tensor.wait(s_ve)
            for mc in range(8):
                b = mc
                if e_ == 0:
                    te_bank(b)
                for kc in range(8):
                    sh = 256 if T == 256 else 0
                    P.tensor.run(lambda e, b=b, mc=mc, kc=kc, e_=e_, sh=sh: e.matmul(
                        ps[:, b * 512 + sh: b * 512 + sh + T] if b <= 1 else ps[:, b * 512: b * 512 + T],
                        W["we"][:, e_ * 8192 + kc * 1024 + mc * 128: e_ * 8192 + kc * 1024 + (mc + 1) * 128],
                        HS[:, kc * T:(kc + 1) * T],
                        start=(e_ == 0 and kc == 0), stop=(e_ == 7 and kc == 7)),
                        inc=(s_te if (kc == 7 and mc == 7) else None))
        P.vector.wait(s_te)
        if dbg is not None and i == n_t - 1:
            P.vector.run(lambda e: e.tensor_copy(dbg_gb[:], gb8[:]))
            P.vector.run(lambda e: e.tensor_copy(dbg_hs[:], HS[:]), inc=s_ve)
        for mc in range(8):
            b = mc
            cs = slice(mc * T, (mc + 1) * T)
            sh = 256 if T == 256 else 0
            src = ps[:, b * 512 + sh: b * 512 + sh + T] if b <= 1 else ps[:, b * 512: b * 512 + T]
            P.vector.run(lambda e, src=src, cs=cs: e.scalar_tensor_tensor(
                stt[:, cs], src, P.cc("half"), stt[:, cs],
                ALU.mult, ALU.add), inc=s_ve)
        for b in range(8):
            consumed(b, s_ve)
        P.sync.wait(s_ve)
        P.dma(o_v[:, :, t0:t0 + T], stv, s_st)
    if dbg is not None:
        P.sync.wait(s_ve)
        P.dma(dbg["gb8"], dbg_gb[:], s_st)
        P.dma(dbg["hs"], dbg_hs[:], s_st)
    P.sync.wait(s_st)



from contextlib import ExitStack

CONSTS = {"eps5": 1e-5, "eps6": 1e-6, "negh": -0.5, "neg1": -1.0,
          "one": 1.0, "r1024": 1.0 / 1024.0, "q8": 0.125, "half": 0.5}


def build_program(L=4096, T_f=512, T_m=256):
    nc = bass.Bass("TRN2", target_bir_lowering=False, detect_race_conditions=False)
    dram = {}
    for k, (shape, dt) in INPUT_SPECS.items():
        dram[k] = nc.dram_tensor(k, list(shape), dt, kind="ExternalInput").ap()
    out = nc.dram_tensor("out", [1024, L], F32, kind="ExternalOutput").ap()
    s_scr = nc.dram_tensor("s_scr", [1024, L], F32, kind="Internal").ap()
    h2_scr = nc.dram_tensor("h2_scr", [1024, L], BF16, kind="Internal").ap()
    g_scr = nc.dram_tensor("g_scr", [8, L], F32, kind="Internal").ap()
    r1_scr = nc.dram_tensor("r1_scr", [1024, L], F32, kind="Internal").ap()
    r2_scr = nc.dram_tensor("r2_scr", [1024, L], F32, kind="Internal").ap()
    with ExitStack() as big:
        P = Prog(nc)
        P._stack = big
        P.setup_consts(CONSTS)
        with ExitStack() as st:
            W = emit_load_ffn_weights(P, st, dram, "A")
            emit_ffnA(P, st, W, dram["xT"], s_scr, h2_scr, g_scr, L=L, T=T_f)
            P.flush_phase()
        with ExitStack() as st:
            W = emit_load_ffn_weights(P, st, dram, "B")
            emit_ffnB(P, st, W, s_scr, h2_scr, g_scr, r1_scr, L=L, T=T_f)
            P.flush_phase()
        with ExitStack() as st:
            W = emit_load_mamba_weights(P, st, dram)
            emit_mamba(P, st, W, r1_scr, r2_scr, dram["c_sel"], L=L, T=T_m)
            P.flush_phase()
        with ExitStack() as st:
            W = emit_load_ffn_weights(P, st, dram, "A")
            emit_ffnA(P, st, W, r2_scr, s_scr, h2_scr, g_scr, L=L, T=T_f)
            P.flush_phase()
        with ExitStack() as st:
            W = emit_load_ffn_weights(P, st, dram, "B")
            emit_ffnB(P, st, W, s_scr, h2_scr, g_scr, out, L=L, T=T_f)
            P.flush_phase()
    return nc


def kernel(**inputs):
    common = host_layout(inputs, 0)
    del common["xT"]
    nc = build_program()
    in_maps = []
    for c in range(NCORES):
        m = dict(common)
        m["xT"] = np.ascontiguousarray(inputs["L0_attn_x"][c].T)
        in_maps.append(m)
    res = run_bass_kernel_spmd(nc, in_maps, core_ids=list(range(NCORES)))
    return np.stack([np.ascontiguousarray(r["out"].T) for r in res.results], axis=0)


# version 15
# speedup vs baseline: 1.0468x; 1.0468x over previous
import numpy as np
import ml_dtypes
import concourse.bass as bass
import concourse.mybir as mybir
from concourse.bass_utils import run_bass_kernel_spmd
from concourse.bass_isa import ReduceOp
RED_MAX = ReduceOp.max if hasattr(ReduceOp, "max") else "max"
RED_ADD = ReduceOp.add if hasattr(ReduceOp, "add") else "add" 

F32 = mybir.dt.float32
BF16 = mybir.dt.bfloat16
AF = mybir.ActivationFunctionType
ALU = mybir.AluOpType
AX = mybir.AxisListType

D_MODEL = 1024
D_INNER = 2048
D_STATE = 16
D_CONV = 4
DT_RANK = 64
NCORES = 8

LN_EPS_OUTER = 1e-5
LN_EPS_INNER = 1e-6
NEG_BIG = -30000.0
import os
SCE_EVAC = os.environ.get("SCE_EVAC", "0") == "1"



class Sem:
    __slots__ = ("h", "n")

    def __init__(self, h):
        self.h = h
        self.n = 0


class EngPlan:
    def __init__(self, name):
        self.name = name
        self.steps = []
        self.waited = {}

    def wait(self, sem, val=None):
        v = sem.n if val is None else val
        if self.waited.get(id(sem), -1) >= v:
            return
        self.waited[id(sem)] = v
        h = sem.h
        self.steps.append(lambda e, h=h, v=v: e.wait_ge(h, v))

    def run(self, fn, inc=None, n=1):
        if inc is not None:
            inc.n += n
            h = inc.h
            self.steps.append(lambda e, fn=fn, h=h, n=n: fn(e).then_inc(h, n))
        else:
            self.steps.append(fn)


class Prog:
    ENGS = ("sync", "vector", "scalar", "tensor", "gpsimd")

    def __init__(self, nc):
        self.nc = nc
        self.sems = {}
        self.plans = {k: EngPlan(k) for k in self.ENGS}
        self.sync = self.plans["sync"]
        self.vector = self.plans["vector"]
        self.scalar = self.plans["scalar"]
        self.tensor = self.plans["tensor"]
        self.gpsimd = self.plans["gpsimd"]
        self._phase_idx = 0
        self._stack = None

    def sem(self, name):
        if name not in self.sems:
            h = self._stack.enter_context(self.nc.semaphore(name))
            self.sems[name] = Sem(h)
        return self.sems[name]

    def setup_consts(self, values):
        n = len(values)
        t = self._stack.enter_context(self.nc.sbuf_tensor("cconsts", [128, max(n, 1)], F32))
        self._consts = t
        self._cmap = {}
        for i, (k, v) in enumerate(values.items()):
            self._cmap[k] = i
            self.vector.run(lambda e, i=i, v=v: e.memset(t[:, i:i + 1], v))
        s = self.sem("s_const")
        self.vector.run(lambda e: e.engine_nop(), inc=s)
        for k in ("scalar", "vector", "gpsimd", "tensor"):
            self.plans[k].wait(s)

    def tile_barrier(self):
        bar = self.sem("tilebar")
        for k in ("vector", "scalar", "tensor"):
            self.plans[k].run(lambda e: e.drain(), inc=bar, n=1)
        target = bar.n
        for k in ("vector", "scalar", "tensor", "sync", "gpsimd"):
            self.plans[k].wait(bar, target)

    def cc(self, name, rows=128):
        i = self._cmap[name]
        return self._consts[0:rows, i:i + 1]

    def dma(self, out, in_, sem, eng="sync"):
        self.plans[eng].run(lambda e, o=out, i=in_: e.dma_start(out=o, in_=i), inc=sem, n=16)

    def flush_phase(self):
        bar = self.sem("phasebar")
        for k in ("vector", "scalar", "tensor", "gpsimd"):
            self.plans[k].run(lambda e: e.drain(), inc=bar, n=1)
        self.plans["sync"].run(lambda e: e.sem_inc(bar.h, 1))
        bar.n += 1
        nc = self.nc
        with nc.Block() as blk:
            for k in self.ENGS:
                steps = self.plans[k].steps
                getattr(blk, k)(lambda e, steps=steps: [f(e) for f in steps])
        self._phase_idx += 1
        target = bar.n
        for k in self.ENGS:
            self.plans[k] = EngPlan(k)
        self.sync = self.plans["sync"]
        self.vector = self.plans["vector"]
        self.scalar = self.plans["scalar"]
        self.tensor = self.plans["tensor"]
        self.gpsimd = self.plans["gpsimd"]
        for k in self.ENGS:
            self.plans[k].wait(self.sems["phasebar"], target)



def _img_chunked(w, rows=128):
    K, M = w.shape
    return np.ascontiguousarray(w.reshape(K // rows, rows, M).transpose(1, 0, 2).reshape(rows, -1))


def sel_const():
    sel = np.zeros((32, 32 * 128), np.float32)
    for n in range(32):
        sel[n, n * 128:(n + 1) * 128] = 1.0
    return sel


def upv_img(wup, bup):
    out = np.zeros((65, 2048), np.float32)
    for dh in range(8):
        for q in range(2):
            for m in range(4):
                dd = dh * 8 + q * 4 + m
                col0 = dh * 256 + q * 128 + m * 32
                out[0:64, col0:col0 + 16] = wup[:, 16 * 0:16 * 16:128][:, :] if False else 0
                for g in range(16):
                    out[0:64, col0 + g] = wup[:, g * 128 + 64 + dd]
                    out[64, col0 + g] = bup[g * 128 + 64 + dd]
    return np.ascontiguousarray(out)


def host_layout(inp, core):
    f = np.ascontiguousarray
    d = {}
    d["xT"] = f(inp["L0_attn_x"][core].T)
    d["w_inproj"] = _img_chunked(f(inp["L0_attn_in_proj_w"].T))
    d["w_outproj"] = _img_chunked(f(inp["L0_attn_out_proj_w"].T))
    d["w_xproj"] = _img_chunked(f(inp["L0_attn_x_proj_w"].T))
    d["w_dtproj"] = f(inp["L0_attn_dt_proj_w"].T)
    d["v_convw"] = f(inp["L0_attn_conv_w"][:, 0, :].reshape(16, 128, 4).transpose(1, 0, 2).reshape(128, 64))
    d["v_convb"] = f(inp["L0_attn_conv_b"].reshape(16, 128).T)
    d["v_D"] = f(inp["L0_attn_D"].reshape(16, 128).T)
    d["v_dtb"] = f(inp["L0_attn_dt_proj_b"].reshape(16, 128).T)
    d["v_Alog"] = f(inp["L0_attn_A_log"].reshape(16, 128, 16).transpose(1, 0, 2).reshape(128, 256))
    d["v_lng"] = f(inp["L0_attn_ln_gamma"].reshape(8, 128).T)
    d["v_lnb"] = f(inp["L0_attn_ln_beta"].reshape(8, 128).T)
    d["w_q"] = _img_chunked(f(inp["L0_ffn_Wq"]))
    d["w_down"] = _img_chunked(f(inp["L0_ffn_Wdown"]))
    wup = inp["L0_ffn_Wup"]
    d["w_upk"] = f(wup.reshape(64, 16, 128)[:, :, 0:64].reshape(64, 16 * 64))
    d["w_upv"] = upv_img(wup, inp["L0_ffn_bup"])
    d["w_o"] = _img_chunked(f(inp["L0_ffn_Wo"]))
    d["w_g"] = _img_chunked(f(inp["L0_ffn_Wg"]))
    d["w_e"] = f(inp["L0_ffn_We"].reshape(8, 8, 128, 1024).transpose(2, 0, 1, 3).reshape(128, 8 * 8 * 1024))
    d["v_g1"] = f(inp["L0_ffn_gamma1"].reshape(8, 128).T)
    d["v_g2"] = f(inp["L0_ffn_gamma2"].reshape(8, 128).T)
    d["v_bq"] = f(inp["L0_ffn_bq"].reshape(8, 128).T)
    d["v_bo"] = f(inp["L0_ffn_bo"].reshape(8, 128).T)
    d["v_bdown"] = f(inp["L0_ffn_bdown"].reshape(64, 1))
    bup = inp["L0_ffn_bup"]
    d["v_bupk"] = f(bup.reshape(16, 128)[:, 0:64].T)
    d["v_bg"] = f(inp["L0_ffn_bg"].reshape(8, 1))
    d["c_sel"] = sel_const()
    d["c_mask"] = mask_const()
    d["c_eye"] = np.eye(128, dtype=np.float32)
    d["c_bgr"] = f(np.broadcast_to(inp["L0_ffn_bg"][None, :], (128, 8)).copy())
    return d


INPUT_SPECS = {
    "xT": ([1024, 4096], F32),
    "w_inproj": ([128, 8 * 4096], F32),
    "w_outproj": ([128, 16 * 1024], F32),
    "w_xproj": ([128, 16 * 96], F32),
    "w_dtproj": ([64, 2048], F32),
    "v_convw": ([128, 64], F32),
    "v_convb": ([128, 16], F32),
    "v_D": ([128, 16], F32),
    "v_dtb": ([128, 16], F32),
    "v_Alog": ([128, 256], F32),
    "v_lng": ([128, 8], F32),
    "v_lnb": ([128, 8], F32),
    "w_q": ([128, 8 * 1024], F32),
    "w_down": ([128, 8 * 64], F32),
    "w_upk": ([64, 16 * 64], F32),
    "w_upv": ([65, 2048], F32),
    "w_o": ([128, 8 * 1024], F32),
    "w_g": ([128, 64], F32),
    "w_e": ([128, 8 * 8 * 1024], F32),
    "v_g1": ([128, 8], F32),
    "v_g2": ([128, 8], F32),
    "v_bq": ([128, 8], F32),
    "v_bo": ([128, 8], F32),
    "v_bdown": ([64, 1], F32),
    "v_bupk": ([64, 16], F32),
    "v_bg": ([8, 1], F32),
    "c_sel": ([32, 32 * 128], F32),
    "c_mask": ([128, 512], F32),
    "c_eye": ([128, 128], F32),
    "c_bgr": ([128, 8], F32),
}



def emit_load_mamba_weights(P, st, dram):
    nc = P.nc
    sb = lambda n, s, d: st.enter_context(nc.sbuf_tensor(f"{n}_ph{P._phase_idx}", s, d))
    W = {}
    W["inW"] = sb("m_inW", [128, 8 * 4096], BF16)
    W["opW"] = sb("m_opW", [128, 16 * 1024], BF16)
    W["xpW"] = sb("m_xpW", [128, 16 * 96], BF16)
    W["dtW"] = sb("m_dtW", [64, 2048], BF16)
    W["cw"] = sb("m_cw", [128, 64], F32)
    W["cb"] = sb("m_cb", [128, 16], F32)
    W["Dv"] = sb("m_Dv", [128, 16], F32)
    W["dtb"] = sb("m_dtb", [128, 16], F32)
    W["An"] = sb("m_An", [128, 256], F32)
    W["lng"] = sb("m_lng", [128, 8], F32)
    W["lnb"] = sb("m_lnb", [128, 8], F32)
    W["onesc"] = sb("m_onesc", [128, 1], BF16)
    W["ones1"] = sb("m_ones1", [1, 128], BF16)
    W["ones1f"] = sb("m_ones1f", [1, 128], F32)
    s_w = P.sem("s_w")
    s_wg = P.sem("s_wg")
    P.dma(W["inW"][:], dram["w_inproj"], s_wg, eng="gpsimd")
    P.dma(W["opW"][:], dram["w_outproj"], s_wg, eng="gpsimd")
    P.dma(W["xpW"][:], dram["w_xproj"], s_wg, eng="gpsimd")
    P.dma(W["dtW"][:], dram["w_dtproj"], s_wg, eng="gpsimd")
    for k, src in [("cw", "v_convw"), ("cb", "v_convb"), ("Dv", "v_D"),
                   ("dtb", "v_dtb"), ("An", "v_Alog"), ("lng", "v_lng"), ("lnb", "v_lnb")]:
        P.dma(W[k][:], dram[src], s_w, eng="sync")
    P.vector.run(lambda e, W=W: e.memset(W["onesc"][:], 1.0))
    P.vector.run(lambda e, W=W: e.memset(W["ones1"][:], 1.0))
    P.vector.run(lambda e, W=W: e.memset(W["ones1f"][:], 1.0))
    P.scalar.wait(s_w)
    P.scalar.wait(s_wg)
    s_se = P.sem("s_se")
    P.scalar.run(lambda e, W=W: e.activation(W["An"][:], W["An"][:], AF.Exp), inc=s_se)
    P.vector.wait(s_se)
    s_ve = P.sem("s_ve")
    P.vector.run(lambda e, W=W: e.tensor_scalar_mul(W["An"][:], W["An"][:], -1.0), inc=s_ve)
    return W



def emit_mamba(P, st, W, in_dram, out_dram, in_sel, L=4096, T=256, dbg=None):
    nc = P.nc
    n_t = L // T
    NC = 16
    sb = lambda n, s, d: st.enter_context(nc.sbuf_tensor(f"{n}_ph{P._phase_idx}", s, d))
    rt = sb("m_rt", [128, 8 * T], F32)
    LN0 = sb("m_ln0", [128, 8 * T], F32)
    hm = sb("m_hm", [128, 8 * T], BF16)
    rb = hm
    UC = sb("m_UC", [128, NC * T], BF16)
    ZS = sb("m_ZS", [128, NC * T], BF16)
    Bb = sb("m_Bb", [128, NC * T], BF16)
    Cb = sb("m_Cb", [128, NC * T], BF16)
    DELTA = sb("m_DELTA", [128, NC * T], F32)
    SEL = sb("m_SEL", [32, 32 * 128], BF16)
    dtb_sb = sb("m_dtbsb", [64, T], BF16)
    bcrow = sb("m_bcrow", [32, T], BF16)
    ue = sb("m_ue", [128, T + 4], BF16)
    utail = sb("m_utail", [128, NC * 3], BF16)
    cacc = sb("m_cacc", [128, T], F32)
    accv = sb("m_accv", [128, T], F32)
    sp_e = sb("m_spe", [128, T], F32)
    sp_r = sb("m_spr", [128, T], F32)
    sg = sb("m_sg", [128, T], BF16)
    xt = sb("m_xt", [128, T], BF16)
    dab = [sb(f"m_dab{i}", [128, T], F32) for i in range(2)]
    h32 = [sb(f"m_h32{i}", [128, T], F32) for i in range(2)]
    DBU = sb("m_DBU", [128, 16 * T], BF16)
    yh = sb("m_yh", [128, T], BF16)
    y2 = sb("m_y2", [128, T], BF16)
    STATE = sb("m_state", [128, 256], F32)
    mu_r = sb("m_mu", [1, T], F32)
    ms_r = sb("m_ms", [1, T], F32)
    rs_r = sb("m_rs", [1, T], F32)
    ps = [st.enter_context(nc.psum_tensor(f"m_ps{i}_ph{P._phase_idx}", [128, 512], F32)) for i in range(8)]

    s_ld = P.sem("s_ld")
    s_st = P.sem("s_st")
    s_te = P.sem("s_te")
    s_ve = P.sem("s_ve")
    s_se = P.sem("s_se")
    s_w = P.sem("s_w")
    s_gp = P.sem("s_gp")

    bank_free = {}

    def te_bank(b):
        if b in bank_free:
            sem, cnt = bank_free[b]
            P.tensor.wait(sem, cnt)

    def consumed(b, sem):
        bank_free[b] = (sem, sem.n)

    in_v = in_dram.rearrange("(c p) t -> p c t", p=128)
    out_v = out_dram.rearrange("(c p) t -> p c t", p=128)
    rtv = rt[:].rearrange("p (c t) -> p c t", c=8)

    s_wg2 = P.sem("s_wg2")
    P.dma(SEL[:], in_sel, s_wg2, eng="gpsimd")
    for eng in ("vector", "scalar", "tensor"):
        P.plans[eng].wait(s_wg2)
    P.vector.run(lambda e: e.memset(STATE[:], 0.0))
    P.vector.run(lambda e: e.memset(utail[:], 0.0), inc=s_ve)
    for eng in ("vector", "scalar", "tensor"):
        P.plans[eng].wait(s_w)
        P.plans[eng].wait(P.sem("s_wg"))

    for i in range(n_t):
        t0 = i * T
        P.sync.wait(s_ve)
        P.sync.wait(s_st)
        P.dma(rtv, in_v[:, :, t0:t0 + T], s_ld)
        P.vector.wait(s_ld)
        P.vector.wait(s_te)
        P.vector.run(lambda e: e.tensor_copy(rb[:], rt[:]), inc=s_ve)
        P.tensor.wait(s_ve)
        te_bank(0)
        for c in range(8):
            P.tensor.run(lambda e, c=c: e.matmul(
                ps[0][0:1, 0:T], W["onesc"][:], rb[:, c * T:(c + 1) * T],
                start=(c == 0), stop=(c == 7)), inc=(s_te if c == 7 else None))
        P.scalar.wait(s_te)
        P.scalar.run(lambda e: e.activation(rb[:], rb[:], AF.Square), inc=s_se)
        P.tensor.wait(s_se)
        te_bank(1)
        for c in range(8):
            P.tensor.run(lambda e, c=c: e.matmul(
                ps[1][0:1, 0:T], W["onesc"][:], rb[:, c * T:(c + 1) * T],
                start=(c == 0), stop=(c == 7)), inc=(s_te if c == 7 else None))
        P.scalar.wait(s_te)
        P.scalar.run(lambda e: e.activation(mu_r[:], ps[0][0:1, 0:T], AF.Copy, scale=P.cc("r1024", 1)))
        P.scalar.run(lambda e: e.activation(ms_r[:], ps[1][0:1, 0:T], AF.Copy, scale=P.cc("r1024", 1)))
        P.scalar.run(lambda e: e.activation(rs_r[:], mu_r[:], AF.Square), inc=s_se)
        consumed(0, s_se)
        consumed(1, s_se)
        P.vector.wait(s_se)
        P.vector.run(lambda e: e.tensor_sub(ms_r[:], ms_r[:], rs_r[:]), inc=s_ve)
        P.scalar.wait(s_ve)
        P.scalar.run(lambda e: e.activation(ms_r[:], ms_r[:], AF.Ln, bias=P.cc("eps5", 1)))
        P.scalar.run(lambda e: e.activation(rs_r[:], ms_r[:], AF.Exp, scale=P.cc("negh", 1)), inc=s_se)
        P.tensor.wait(s_se)
        te_bank(0)
        te_bank(1)
        P.tensor.run(lambda e: e.matmul(ps[0][:, 0:T], W["ones1f"][:], mu_r[:], start=True, stop=True))
        P.tensor.run(lambda e: e.matmul(ps[1][:, 0:T], W["ones1f"][:], rs_r[:], start=True, stop=True), inc=s_te)
        P.vector.wait(s_te)
        mub = ps[0][:, 0:T].rearrange("p (o t) -> p o t", o=1).broadcast_to([128, 8, T])
        rsb = ps[1][:, 0:T].rearrange("p (o t) -> p o t", o=1).broadcast_to([128, 8, T])
        l3 = LN0[:].rearrange("p (c t) -> p c t", c=8)
        r3 = rt[:].rearrange("p (c t) -> p c t", c=8)
        P.vector.run(lambda e, l3=l3, r3=r3, mub=mub: e.tensor_sub(l3, r3, mub))
        P.vector.run(lambda e, l3=l3, rsb=rsb: e.tensor_mul(l3, l3, rsb))
        P.vector.run(lambda e, l3=l3, r3=r3: e.tensor_add(r3, r3, l3), inc=s_ve)
        consumed(0, s_ve)
        consumed(1, s_ve)
        P.vector.run(lambda e: e.tensor_copy(rb[:], LN0[:]), inc=s_ve)
        P.tensor.wait(s_ve)
        te_bank(0)
        for c in range(8):
            P.tensor.run(lambda e, c=c: e.matmul(
                ps[0][0:1, 0:T], W["onesc"][:], rb[:, c * T:(c + 1) * T],
                start=(c == 0), stop=(c == 7)), inc=(s_te if c == 7 else None))
        P.scalar.wait(s_te)
        P.scalar.run(lambda e: e.activation(rb[:], rb[:], AF.Square), inc=s_se)
        P.tensor.wait(s_se)
        te_bank(1)
        for c in range(8):
            P.tensor.run(lambda e, c=c: e.matmul(
                ps[1][0:1, 0:T], W["onesc"][:], rb[:, c * T:(c + 1) * T],
                start=(c == 0), stop=(c == 7)), inc=(s_te if c == 7 else None))
        P.scalar.wait(s_te)
        P.scalar.run(lambda e: e.activation(mu_r[:], ps[0][0:1, 0:T], AF.Copy, scale=P.cc("r1024", 1)))
        P.scalar.run(lambda e: e.activation(ms_r[:], ps[1][0:1, 0:T], AF.Copy, scale=P.cc("r1024", 1)))
        P.scalar.run(lambda e: e.activation(rs_r[:], mu_r[:], AF.Square), inc=s_se)
        consumed(0, s_se)
        consumed(1, s_se)
        P.vector.wait(s_se)
        P.vector.run(lambda e: e.tensor_sub(ms_r[:], ms_r[:], rs_r[:]), inc=s_ve)
        P.scalar.wait(s_ve)
        P.scalar.run(lambda e: e.activation(ms_r[:], ms_r[:], AF.Ln, bias=P.cc("eps6", 1)))
        P.scalar.run(lambda e: e.activation(rs_r[:], ms_r[:], AF.Exp, scale=P.cc("negh", 1)), inc=s_se)
        P.tensor.wait(s_se)
        te_bank(0)
        te_bank(1)
        P.tensor.run(lambda e: e.matmul(ps[0][:, 0:T], W["ones1f"][:], mu_r[:], start=True, stop=True))
        P.tensor.run(lambda e: e.matmul(ps[1][:, 0:T], W["ones1f"][:], rs_r[:], start=True, stop=True), inc=s_te)
        P.vector.wait(s_te)
        for c in range(8):
            cs = slice(c * T, (c + 1) * T)
            P.vector.run(lambda e, cs=cs: e.tensor_sub(accv[:], LN0[:, cs], ps[0][:, 0:T]))
            P.vector.run(lambda e: e.tensor_mul(accv[:], accv[:], ps[1][:, 0:T]))
            P.vector.run(lambda e, cs=cs, c=c: e.tensor_scalar(
                hm[:, cs], accv[:], W["lng"][:, c:c + 1], W["lnb"][:, c:c + 1],
                ALU.mult, ALU.add), inc=s_ve)
        consumed(0, s_ve)
        consumed(1, s_ve)
        P.tensor.wait(s_ve)
        te_bank(2)
        for mc in range(32):
            b = 3 + (mc % 2) * 4
            te_bank(b)
            for kc in range(8):
                P.tensor.run(lambda e, b=b, kc=kc, mc=mc: e.matmul(
                    ps[b][:, 0:T],
                    W["inW"][:, kc * 4096 + mc * 128: kc * 4096 + (mc + 1) * 128],
                    hm[:, kc * T:(kc + 1) * T],
                    start=(kc == 0), stop=(kc == 7)), inc=(s_te if kc == 7 else None))
            if mc < 16:
                P.vector.wait(s_te)
                P.vector.run(lambda e, mc=mc: e.tensor_copy(ue[:, 0:3], utail[:, mc * 3:mc * 3 + 3]))
                P.vector.run(lambda e, b=b: e.tensor_copy(ue[:, 3:3 + T], ps[b][:, 0:T]), inc=s_ve)
                consumed(b, s_ve)
                P.vector.run(lambda e, mc=mc: e.tensor_scalar(
                    cacc[:], ue[:, 0:T], W["cw"][:, mc * 4:mc * 4 + 1],
                    W["cb"][:, mc:mc + 1], ALU.mult, ALU.add))
                for k in range(1, 4):
                    P.vector.run(lambda e, k=k, mc=mc: e.scalar_tensor_tensor(
                        cacc[:], ue[:, k:k + T], W["cw"][:, mc * 4 + k:mc * 4 + k + 1], cacc[:],
                        ALU.mult, ALU.add), inc=(s_ve if k == 3 else None))
                P.vector.run(lambda e, mc=mc: e.tensor_copy(
                    utail[:, mc * 3:mc * 3 + 3], ue[:, T:T + 3]))
                P.scalar.wait(s_ve)
                P.scalar.run(lambda e, mc=mc: e.activation(
                    sg[:], cacc[:], AF.Sigmoid), inc=s_se)
                P.vector.wait(s_se)
                P.vector.run(lambda e, mc=mc: e.tensor_mul(
                    UC[:, mc * T:(mc + 1) * T], cacc[:], sg[:]), inc=s_ve)
                P.tensor.wait(s_ve)
                P.tensor.run(lambda e, mc=mc: e.matmul(
                    ps[2][0:96, 0:T], W["xpW"][:, mc * 96:(mc + 1) * 96],
                    UC[:, mc * T:(mc + 1) * T],
                    start=(mc == 0), stop=(mc == 15)), inc=(s_te if mc == 15 else None))
            else:
                zc = mc - 16
                P.scalar.wait(s_te)
                P.scalar.run(lambda e, b=b: e.activation(sg[:], ps[b][:, 0:T], AF.Sigmoid), inc=s_se)
                P.vector.wait(s_se)
                P.vector.run(lambda e, b=b, zc=zc: e.tensor_mul(
                    ZS[:, zc * T:(zc + 1) * T], ps[b][:, 0:T], sg[:]), inc=s_ve)
                consumed(b, s_ve)
        P.vector.wait(s_te)
        P.vector.run(lambda e: e.tensor_copy(dtb_sb[:], ps[2][0:64, 0:T]))
        P.vector.run(lambda e: e.tensor_copy(bcrow[:], ps[2][64:96, 0:T]), inc=s_ve)
        consumed(2, s_ve)
        P.tensor.wait(s_ve)
        for n in range(32):
            b = n % 2
            te_bank(b)
            P.tensor.run(lambda e, b=b, n=n: e.matmul(
                ps[b][:, 0:T], SEL[:, n * 128:(n + 1) * 128], bcrow[:, 0:T],
                start=True, stop=True), inc=s_te)
            P.vector.wait(s_te)
            dst = Bb if n < 16 else Cb
            nn = n % 16
            P.vector.run(lambda e, b=b, dst=dst, nn=nn: e.tensor_copy(
                dst[:, nn * T:(nn + 1) * T], ps[b][:, 0:T]), inc=s_ve)
            consumed(b, s_ve)
        for c in range(NC):
            b = 2 + (c % 2)
            te_bank(b)
            P.tensor.run(lambda e, b=b, c=c: e.matmul(
                ps[b][:, 0:T], W["dtW"][:, c * 128:(c + 1) * 128], dtb_sb[:],
                start=True, stop=True), inc=s_te)
            P.scalar.wait(s_te)
            P.scalar.wait(s_ve)
            P.scalar.run(lambda e, b=b, c=c: e.activation(
                sp_e[:], ps[b][:, 0:T], AF.Abs, bias=W["dtb"][:, c:c + 1]))
            P.scalar.run(lambda e: e.activation(sp_e[:], sp_e[:], AF.Exp, scale=P.cc("neg1")))
            P.scalar.run(lambda e: e.activation(sp_e[:], sp_e[:], AF.Ln, bias=P.cc("one")))
            P.scalar.run(lambda e, b=b, c=c: e.activation(
                sp_r[:], ps[b][:, 0:T], AF.Relu, bias=W["dtb"][:, c:c + 1]), inc=s_se)
            consumed(b, s_se)
            P.vector.wait(s_se)
            P.vector.run(lambda e, c=c: e.tensor_add(
                DELTA[:, c * T:(c + 1) * T], sp_e[:], sp_r[:]), inc=s_ve)
        for b in range(8):
            te_bank(b)
        scan_ve_marks = {}
        for c in range(NC):
            ds = slice(c * T, (c + 1) * T)
            P.vector.run(lambda e, ds=ds, c=c: e.tensor_mul(xt[:], DELTA[:, ds], UC[:, ds]), inc=s_ve)
            P.scalar.wait(s_ve)
            xtv = xt[:].rearrange("p (o t) -> p o t", o=1).broadcast_to([128, 16, T])
            P.vector.run(lambda e, xtv=xtv: e.tensor_mul(
                DBU[:].rearrange("p (n t) -> p n t", n=16), xtv,
                Bb[:].rearrange("p (n t) -> p n t", n=16)))
            for n in range(16):
                pn = n % 2
                col = c * 16 + n
                if col >= 2:
                    P.scalar.wait(s_ve, scan_ve_marks[col - 2])
                P.scalar.run(lambda e, pn=pn, col=col, ds=ds: e.activation(
                    dab[pn][:], DELTA[:, ds], AF.Exp, scale=W["An"][:, col:col + 1]), inc=s_se)
                P.vector.wait(s_se)
                P.vector.run(lambda e, pn=pn, col=col, n=n: e.tensor_tensor_scan(
                    h32[pn][:], dab[pn][:], DBU[:, n * T:(n + 1) * T],
                    STATE[:, col:col + 1], ALU.mult, ALU.add))
                P.vector.run(lambda e, pn=pn, n=n: e.tensor_mul(
                    yh[:], h32[pn][:], Cb[:, n * T:(n + 1) * T]))
                P.vector.run(lambda e, pn=pn, col=col: e.tensor_copy(
                    STATE[:, col:col + 1], h32[pn][:, T - 1:T]))
                if n == 0:
                    P.vector.run(lambda e: e.tensor_copy(accv[:], yh[:]), inc=s_ve)
                else:
                    P.vector.run(lambda e: e.tensor_add(accv[:], accv[:], yh[:]), inc=s_ve)
                scan_ve_marks[col] = s_ve.n
            if c > 0:
                P.vector.wait(s_te)
            P.vector.run(lambda e, c=c: e.scalar_tensor_tensor(
                accv[:], UC[:, c * T:(c + 1) * T], W["Dv"][:, c:c + 1], accv[:],
                ALU.mult, ALU.add))
            P.vector.run(lambda e, c=c: e.tensor_mul(y2[:], accv[:], ZS[:, c * T:(c + 1) * T]), inc=s_ve)
            P.tensor.wait(s_ve)
            for mc in range(8):
                P.tensor.run(lambda e, b=mc, mc=mc, c=c: e.matmul(
                    ps[b][:, 0:T],
                    W["opW"][:, c * 1024 + mc * 128: c * 1024 + (mc + 1) * 128],
                    y2[:],
                    start=(c == 0), stop=(c == NC - 1)),
                    inc=(s_te if mc == 7 else None))
        P.vector.wait(s_te)
        for mc in range(8):
            P.vector.run(lambda e, b=mc, mc=mc: e.tensor_add(
                rt[:, mc * T:(mc + 1) * T], rt[:, mc * T:(mc + 1) * T], ps[b][:, 0:T]),
                inc=s_ve)
        for b in range(8):
            consumed(b, s_ve)
        P.sync.wait(s_ve)
        P.dma(out_v[:, :, t0:t0 + T], rtv, s_st)
    if dbg is not None:
        P.plans["gpsimd"].wait(s_ve)
        P.plans["gpsimd"].wait(s_se)
        for k, buf in [("UC", UC), ("ZS", ZS), ("Bb", Bb), ("Cb", Cb),
                       ("DELTA", DELTA), ("hm", hm), ("LN0", LN0)]:
            if k in dbg:
                P.dma(dbg[k], buf[:], P.sem("s_stg"), eng="gpsimd")
        P.sync.wait(P.sem("s_stg"))
    P.sync.wait(s_st)



def emit_load_ffn_weights(P, st, dram, phase):
    nc = P.nc
    sb = lambda n, s, d: st.enter_context(nc.sbuf_tensor(f"{n}_ph{P._phase_idx}", s, d))
    W = {}
    s_w = P.sem("s_w")
    s_wg = P.sem(f"s_wgf{P._phase_idx}")
    if phase == "A":
        W["wq"] = sb("f_wq", [128, 8 * 1024], BF16)
        W["wdn"] = sb("f_wdn", [128, 8 * 64], BF16)
        W["wupk"] = sb("f_wupk", [64, 16 * 64], BF16)
        W["wupv"] = sb("f_wupv", [65, 2048], BF16)
        W["wo"] = sb("f_wo", [128, 8 * 1024], BF16)
        W["wg"] = sb("f_wg", [128, 64], BF16)
        W["g1"] = sb("f_g1", [128, 8], F32)
        W["g2"] = sb("f_g2", [128, 8], F32)
        W["bq"] = sb("f_bq", [128, 8], F32)
        W["bo"] = sb("f_bo", [128, 8], F32)
        W["bdn"] = sb("f_bdn", [64, 1], F32)
        W["bupk"] = sb("f_bupk", [64, 16], F32)
        W["bg"] = sb("f_bg", [8, 1], F32)
        W["bgr"] = sb("f_bgr", [128, 8], F32)
        W["eye"] = sb("f_eye", [128, 128], F32)
        W["mask"] = sb("f_mask", [128, 512], F32)
        W["sel"] = sb("f_sel", [32, 32 * 128], BF16)
        W["onesc"] = sb("f_onesc", [128, 1], BF16)
        W["ones1f"] = sb("f_ones1f", [1, 128], F32)
        for k, src in [("wq", "w_q"), ("wdn", "w_down"), ("wupk", "w_upk"),
                       ("wupv", "w_upv"), ("wo", "w_o"), ("wg", "w_g"), ("sel", "c_sel")]:
            P.dma(W[k][:], dram[src], s_wg, eng="gpsimd")
        for k, src in [("g1", "v_g1"), ("g2", "v_g2"), ("bq", "v_bq"), ("bo", "v_bo"),
                       ("bdn", "v_bdown"), ("bupk", "v_bupk"),
                       ("bg", "v_bg"), ("bgr", "c_bgr"), ("eye", "c_eye"), ("mask", "c_mask")]:
            P.dma(W[k][:], dram[src], s_w, eng="sync")
        P.vector.run(lambda e, W=W: e.memset(W["onesc"][:], 1.0))
        P.vector.run(lambda e, W=W: e.memset(W["ones1f"][:], 1.0), inc=P.sem("s_ve"))
    else:
        W["we"] = sb("f_we", [128, 8 * 8 * 1024], BF16)
        W["sel"] = sb("f_sel", [32, 32 * 128], BF16)
        P.dma(W["we"][:], dram["w_e"], s_wg, eng="gpsimd")
        P.dma(W["sel"][:], dram["c_sel"], s_wg, eng="gpsimd")
    for eng in ("vector", "scalar", "tensor"):
        P.plans[eng].wait(s_w)
        P.plans[eng].wait(s_wg)
    return W


def mask_const():
    m = np.full((128, 512), NEG_BIG, np.float32)
    for row in range(128):
        j = (row // 32) % 2
        for q in range(8):
            base = q * 64 + j * 32
            m[row, base:base + 16] = 0.0
    return m


def emit_ffnA(P, st, W, in_dram, s_out_dram, h2_out_dram, gate_out_dram, L=4096, T=256, dbg=None, do_att=True, do_gate=True):
    nc = P.nc
    n_t = L // T
    NP = T // 2
    NG = NP // 32
    sb = lambda n, s, d: st.enter_context(nc.sbuf_tensor(f"{n}_ph{P._phase_idx}", s, d))
    rt = sb("f_rt", [128, 8 * T], F32)
    LN0 = sb("f_ln0", [128, 8 * T], F32)
    X2 = sb("f_x2", [128, 8 * T], F32)
    h1 = sb("f_h1", [128, 8 * T], BF16)
    H2 = sb("f_h2", [128, 8 * T], BF16)
    rb = H2
    qzz = sb("f_qzz", [128, 64 * NP], BF16)
    kzz = sb("f_kzz", [128, 64 * NP], BF16)
    vzz = sb("f_vzz", [64, 32 * 128], BF16)
    lat = sb("f_lat", [65, T], BF16)
    Ohp = sb("f_ohp", [128, 8 * T], BF16)
    Asb = sb("f_asb", [128, 512], F32)
    Abf = sb("f_abf", [128, 512], BF16)
    At = sb("f_at", [128, 512], BF16)
    mx = sb("f_mx", [128, 8], F32)
    sm = sb("f_sm", [128, 8], F32)
    rc = sb("f_rc", [128, 8], F32)
    gt = sb("f_gt", [8, T], F32)
    gsb = sb("f_gsb", [128, 8 * (T // 128)], F32)
    bq8 = sb("f_bq8", [128, 8], F32)
    mu_r = sb("f_mu", [1, T], F32)
    ms_r = sb("f_ms", [1, T], F32)
    rs_r = sb("f_rs", [1, T], F32)
    ps = st.enter_context(nc.psum_tensor(f"f_ps_ph{P._phase_idx}", [128, 4096], F32))
    if dbg is not None:
        dbg_sraw = sb("dbg_sraw", [64, 4 * 512], F32)
        dbg_abf = sb("dbg_abf", [64, 4 * 512], F32)
        dbg_at = sb("dbg_at", [64, 4 * 512], F32)
        dbg_vzz = sb("dbg_vzz", [64, 4096], BF16)

    s_ld = P.sem("s_ld")
    s_st = P.sem("s_st")
    s_te = P.sem("s_te")
    s_ve = P.sem("s_ve")
    s_se = P.sem("s_se")
    s_gp = P.sem("s_gp")

    bank_free = {}

    def te_bank(b):
        for sem, cnt in bank_free.pop(b, []):
            P.tensor.wait(sem, cnt)

    def consumed(b, sem):
        bank_free.setdefault(b, []).append((sem, sem.n))

    def B(b, rows=128, cols=512, co=0):
        return ps[0:rows, b * 512 + co: b * 512 + co + cols]

    in_v = in_dram.rearrange("(c p) t -> p c t", p=128)
    s_v = s_out_dram.rearrange("(c p) t -> p c t", p=128)
    h2_v = h2_out_dram.rearrange("(c p) t -> p c t", p=128)
    rtv = rt[:].rearrange("p (c t) -> p c t", c=8)
    h2v = H2[:].rearrange("p (c t) -> p c t", c=8)

    P.vector.run(lambda e: e.memset(qzz[:], 0.0))
    P.vector.run(lambda e: e.memset(kzz[:], 0.0))
    P.vector.run(lambda e: e.memset(vzz[:], 0.0))
    P.vector.run(lambda e: e.memset(lat[64:65, :], 1.0))
    P.vector.run(lambda e: e.tensor_scalar_mul(bq8[:], W["bq"][:], 0.125), inc=s_ve)

    def ln_stats(src_bf, eps_name, out_mu=True):
        P.tensor.wait(s_ve)
        te_bank(0)
        te_bank(1)
        for c in range(8):
            P.tensor.run(lambda e, c=c: e.matmul(
                B(0, 1, T), W["onesc"][:], src_bf[:, c * T:(c + 1) * T],
                start=(c == 0), stop=(c == 7)), inc=(s_te if c == 7 else None))
        P.scalar.wait(s_te)
        P.scalar.run(lambda e: e.activation(src_bf[:], src_bf[:], AF.Square), inc=s_se)
        P.tensor.wait(s_se)
        for c in range(8):
            P.tensor.run(lambda e, c=c: e.matmul(
                B(1, 1, T), W["onesc"][:], src_bf[:, c * T:(c + 1) * T],
                start=(c == 0), stop=(c == 7)), inc=(s_te if c == 7 else None))
        P.scalar.wait(s_te)
        if out_mu:
            P.scalar.run(lambda e: e.activation(mu_r[:], B(0, 1, T), AF.Copy, scale=P.cc("r1024", 1)))
            P.scalar.run(lambda e: e.activation(ms_r[:], B(1, 1, T), AF.Copy, scale=P.cc("r1024", 1)))
            P.scalar.run(lambda e: e.activation(rs_r[:], mu_r[:], AF.Square), inc=s_se)
            consumed(0, s_se)
            consumed(1, s_se)
            P.vector.wait(s_se)
            P.vector.run(lambda e: e.tensor_sub(ms_r[:], ms_r[:], rs_r[:]), inc=s_ve)
            P.scalar.wait(s_ve)
        else:
            P.scalar.run(lambda e: e.activation(ms_r[:], B(1, 1, T), AF.Copy, scale=P.cc("r1024", 1)), inc=s_se)
            consumed(0, s_se)
            consumed(1, s_se)
        P.scalar.run(lambda e, eps=eps_name: e.activation(ms_r[:], ms_r[:], AF.Ln, bias=P.cc(eps, 1)))
        P.scalar.run(lambda e: e.activation(rs_r[:], ms_r[:], AF.Exp, scale=P.cc("negh", 1)), inc=s_se)
        P.tensor.wait(s_se)
        te_bank(0)
        te_bank(1)
        if out_mu:
            P.tensor.run(lambda e: e.matmul(B(0, cols=T), W["ones1f"][:], mu_r[:], start=True, stop=True))
        P.tensor.run(lambda e: e.matmul(B(1, cols=T), W["ones1f"][:], rs_r[:], start=True, stop=True), inc=s_te)
        P.vector.wait(s_te)

    for i in range(n_t):
        t0 = i * T
        P.sync.wait(s_ve)
        P.sync.wait(s_st)
        P.dma(rtv, in_v[:, :, t0:t0 + T], s_ld)
        P.vector.wait(s_ld)
        P.vector.wait(s_te)
        P.vector.wait(s_st)
        P.vector.run(lambda e: e.tensor_copy(rb[:], rt[:]), inc=s_ve)
        ln_stats(rb, "eps5", out_mu=True)
        mub = B(0, cols=T).rearrange("p (o t) -> p o t", o=1).broadcast_to([128, 8, T])
        rsb = B(1, cols=T).rearrange("p (o t) -> p o t", o=1).broadcast_to([128, 8, T])
        l3 = LN0[:].rearrange("p (c t) -> p c t", c=8)
        r3 = rt[:].rearrange("p (c t) -> p c t", c=8)
        P.vector.run(lambda e, l3=l3, r3=r3, mub=mub: e.tensor_sub(l3, r3, mub))
        P.vector.run(lambda e, l3=l3, rsb=rsb: e.tensor_mul(l3, l3, rsb), inc=s_ve)
        consumed(0, s_ve)
        consumed(1, s_ve)
        P.vector.run(lambda e: e.tensor_copy(rb[:], LN0[:]), inc=s_ve)
        ln_stats(rb, "eps6", out_mu=False)
        for c in range(8):
            cs = slice(c * T, (c + 1) * T)
            P.vector.run(lambda e, cs=cs, c=c: e.scalar_tensor_tensor(
                h1[:, cs], LN0[:, cs], W["g1"][:, c:c + 1], B(1, cols=T),
                ALU.mult, ALU.mult), inc=s_ve)
        consumed(1, s_ve)
        P.tensor.wait(s_ve)
        for hp in range(8):
            b = 2 + hp % 2
            te_bank(b)
            for kc in range(8):
                P.tensor.run(lambda e, b=b, hp=hp, kc=kc: e.matmul(
                    B(b, cols=T), W["wq"][:, kc * 1024 + hp * 128: kc * 1024 + (hp + 1) * 128],
                    h1[:, kc * T:(kc + 1) * T],
                    start=(kc == 0), stop=(kc == 7)), inc=(s_te if kc == 7 else None))
            P.vector.wait(s_te)
            P.scalar.wait(s_te)
            for hh in range(2):
                h = 2 * hp + hh
                for j in range(2):
                    src = B(b, cols=T).rearrange("p (pp two) -> p pp two", two=2)[hh * 64:(hh + 1) * 64, :, j]
                    dst = qzz[:].rearrange("p (pp cc) -> p pp cc", cc=64)[j * 64:(j + 1) * 64, :, j * 32 + h]
                    if SCE_EVAC and (hh + j) % 2 == 1:
                        P.scalar.run(lambda e, src=src, dst=dst, hh=hh, hp=hp: e.activation(
                            dst, src, AF.Identity, bias=bq8[hh * 64:(hh + 1) * 64, hp:hp + 1],
                            scale=P.cc("q8", 64)), inc=s_se)
                    else:
                        P.vector.run(lambda e, src=src, dst=dst, hh=hh, hp=hp: e.tensor_scalar(
                            dst, src, W["bq"][hh * 64:(hh + 1) * 64, hp:hp + 1],
                            P.cc("q8", 64), ALU.add, ALU.mult), inc=s_ve)
            consumed(b, s_ve)
            consumed(b, s_se)
        te_bank(4)
        P.tensor.wait(s_ve)
        for kc in range(8):
            P.tensor.run(lambda e, kc=kc: e.matmul(
                B(4, 64, T), W["wdn"][:, kc * 64:(kc + 1) * 64],
                h1[:, kc * T:(kc + 1) * T],
                start=(kc == 0), stop=(kc == 7)), inc=(s_te if kc == 7 else None))
        P.vector.wait(s_te)
        P.vector.run(lambda e: e.tensor_scalar(
            lat[0:64, :], B(4, 64, T), W["bdn"][:, 0:1], None, ALU.add), inc=s_ve)
        consumed(4, s_ve)
        P.tensor.wait(s_ve)
        for g in range(16):
            b = 2 + g % 2
            te_bank(b)
            P.tensor.run(lambda e, b=b, g=g: e.matmul(
                B(b, 64, T), W["wupk"][:, g * 64:(g + 1) * 64], lat[0:64, :],
                start=True, stop=True), inc=s_te)
            P.vector.wait(s_te)
            P.scalar.wait(s_te)
            for j in range(2):
                src = B(b, 64, T).rearrange("p (pp two) -> p pp two", two=2)[:, :, j]
                dst = kzz[:].rearrange("p (pp cc) -> p pp cc", cc=64)[j * 64:(j + 1) * 64, :, j * 32 + g]
                if SCE_EVAC and j == 1:
                    P.scalar.run(lambda e, src=src, dst=dst, g=g: e.activation(
                        dst, src, AF.Identity, bias=W["bupk"][:, g:g + 1]), inc=s_se)
                else:
                    P.vector.run(lambda e, src=src, dst=dst, g=g: e.tensor_scalar(
                        dst, src, W["bupk"][:, g:g + 1], None, ALU.add), inc=s_ve)
            consumed(b, s_ve)
            consumed(b, s_se)
        if not do_att:
            P.vector.run(lambda e: e.memset(Ohp[:], 0.0), inc=s_ve)
        for g0 in range(NG if do_att else 0):
            tok0 = g0 * 64
            te_bank(4)
            te_bank(5)
            P.tensor.wait(s_ve)
            P.tensor.wait(s_se)
            for dq in range(16):
                P.tensor.run(lambda e, dq=dq, tok0=tok0: e.matmul(
                    ps[:, 4 * 512 + dq * 64: 4 * 512 + (dq + 1) * 64],
                    W["wupv"][:, dq * 128:(dq + 1) * 128],
                    lat[:, tok0:tok0 + 64],
                    start=True, stop=True), inc=(s_te if dq == 15 else None))
            P.vector.wait(s_te)
            P.scalar.wait(s_te)
            for m in range(4):
                for j in range(2):
                    src = ps[32 * m:32 * m + 16, 4 * 512: 4 * 512 + 1024].rearrange(
                        "p (dh q pp two) -> p dh q pp two", dh=8, q=2, two=2)[:, :, :, :, j]
                    dst = vzz[j * 32:j * 32 + 16, :].rearrange(
                        "p (pp jj dh q mm) -> p dh q pp jj mm", jj=2, dh=8, q=2, mm=4)[:, :, :, :, j, m]
                    if SCE_EVAC and (m + j) % 2 == 1:
                        P.scalar.run(lambda e, src=src, dst=dst: e.copy(dst, src), inc=s_se)
                    else:
                        P.vector.run(lambda e, src=src, dst=dst: e.tensor_copy(dst, src), inc=s_ve)
            consumed(4, s_ve)
            consumed(4, s_se)
            consumed(5, s_ve)
            consumed(5, s_se)
            if dbg is not None and i == 0 and g0 == 0:
                P.vector.wait(s_se)
                P.vector.run(lambda e: e.tensor_copy(dbg_vzz[:], vzz[:]), inc=s_ve)
            for b in range(4):
                te_bank(b)
            P.tensor.wait(s_ve)
            P.tensor.wait(s_se)
            for p in range(32):
                pa = tok0 // 2 + p
                P.tensor.run(lambda e, p=p, pa=pa: e.matmul(
                    ps[0:64, (p // 8) * 512 + (p % 8) * 64: (p // 8) * 512 + (p % 8) * 64 + 64],
                    qzz[:, pa * 64:(pa + 1) * 64], kzz[:, pa * 64:(pa + 1) * 64],
                    start=True, stop=True), inc=(s_te if p == 31 else None))
            for sub in range(4):
                P.vector.wait(s_te)
                if dbg is not None and i == 0 and sub == 0:
                    P.vector.run(lambda e, g0=g0: e.tensor_copy(
                        dbg_sraw[:, g0 * 512:(g0 + 1) * 512], ps[0:64, 0:512]), inc=s_ve)
                P.vector.run(lambda e, sub=sub: e.tensor_add(
                    Asb[0:64, :], ps[0:64, sub * 512:(sub + 1) * 512],
                    W["mask"][0:64, :]), inc=s_ve)
                consumed(sub, s_ve)
                v3 = Asb[0:64, :].rearrange("p (q c) -> p q c", c=64)
                P.vector.run(lambda e, v3=v3: e.reduce_max(mx[0:64, :], v3, AX.X), inc=s_ve)
                P.vector.wait(s_ve)
                mxb = mx[0:64, :].rearrange("p (q o) -> p q o", o=1).broadcast_to([64, 8, 64])
                P.vector.run(lambda e, v3=v3, mxb=mxb: e.tensor_sub(v3, v3, mxb), inc=s_ve)
                P.scalar.wait(s_ve)
                P.scalar.run(lambda e: e.activation(Abf[0:64, :], Asb[0:64, :], AF.Exp), inc=s_se)
                P.vector.wait(s_se)
                a3 = Abf[0:64, :].rearrange("p (q c) -> p q c", c=64)
                P.vector.run(lambda e, a3=a3: e.reduce_sum(sm[0:64, :], a3, AX.X), inc=s_ve)
                P.vector.wait(s_ve)
                P.vector.run(lambda e: e.reciprocal(rc[0:64, :], sm[0:64, :]), inc=s_ve)
                P.vector.wait(s_ve)
                rcb = rc[0:64, :].rearrange("p (q o) -> p q o", o=1).broadcast_to([64, 8, 64])
                P.vector.run(lambda e, a3=a3, rcb=rcb: e.tensor_mul(a3, a3, rcb))
                P.vector.run(lambda e: e.transpose(At[0:64, :], Abf[0:64, :]), inc=s_ve)
                if dbg is not None and i == 0 and sub == 0:
                    P.vector.run(lambda e, g0=g0: e.tensor_copy(
                        dbg_abf[:, g0 * 512:(g0 + 1) * 512], Abf[0:64, :]))
                    P.vector.run(lambda e, g0=g0: e.tensor_copy(
                        dbg_at[:, g0 * 512:(g0 + 1) * 512], At[0:64, :]), inc=s_ve)
                bb = 4 + sub
                te_bank(bb)
                P.tensor.wait(s_ve)
                for qp in range(8):
                    p = sub * 8 + qp
                    P.tensor.run(lambda e, p=p, qp=qp, bb=bb: e.matmul(
                        ps[:, bb * 512 + qp * 64: bb * 512 + qp * 64 + 64],
                        vzz[:, p * 128:(p + 1) * 128],
                        At[0:64, qp * 64:(qp + 1) * 64],
                        start=True, stop=True), inc=(s_te if qp == 7 else None))
            P.vector.wait(s_te)
            P.scalar.wait(s_te)
            for h in range(16):
                for jm in range(2):
                    src = ps[jm * 64:(jm + 1) * 64, 4 * 512: 4 * 512 + 2048].rearrange(
                        "p (pp cc) -> p pp cc", cc=64)[:, :, jm * 32 + h]
                    hp = h // 2
                    dst = Ohp[(h % 2) * 64:(h % 2) * 64 + 64,
                              hp * T + tok0: hp * T + tok0 + 64].rearrange(
                        "p (pp two) -> p pp two", two=2)[:, :, jm]
                    if SCE_EVAC and (h + jm) % 2 == 1:
                        P.scalar.run(lambda e, src=src, dst=dst: e.copy(dst, src), inc=s_se)
                    else:
                        P.vector.run(lambda e, src=src, dst=dst: e.tensor_copy(dst, src), inc=s_ve)
            for b in range(4, 8):
                consumed(b, s_ve)
                consumed(b, s_se)
        P.tensor.wait(s_ve)
        P.tensor.wait(s_se)
        for mc in range(8):
            b = 2 + mc % 2
            te_bank(b)
            for kc in range(8):
                P.tensor.run(lambda e, b=b, mc=mc, kc=kc: e.matmul(
                    B(b, cols=T), W["wo"][:, kc * 1024 + mc * 128: kc * 1024 + (mc + 1) * 128],
                    Ohp[:, kc * T:(kc + 1) * T],
                    start=(kc == 0), stop=(kc == 7)), inc=(s_te if kc == 7 else None))
            P.vector.wait(s_te)
            cs = slice(mc * T, (mc + 1) * T)
            P.vector.run(lambda e, b=b, cs=cs, mc=mc: e.scalar_tensor_tensor(
                X2[:, cs], B(b, cols=T), W["bo"][:, mc:mc + 1], LN0[:, cs],
                ALU.add, ALU.add), inc=s_ve)
            consumed(b, s_ve)
        P.vector.run(lambda e: e.tensor_copy(rb[:], X2[:]), inc=s_ve)
        ln_stats(rb, "eps6", out_mu=False)
        for c in range(8):
            cs = slice(c * T, (c + 1) * T)
            P.vector.run(lambda e, cs=cs, c=c: e.scalar_tensor_tensor(
                H2[:, cs], X2[:, cs], W["g2"][:, c:c + 1], B(1, cols=T),
                ALU.mult, ALU.mult), inc=s_ve)
        consumed(1, s_ve)
        if not do_gate:
            P.vector.run(lambda e: e.memset(gt[:], 0.125), inc=s_ve)
        te_bank(2)
        te_bank(3)
        P.tensor.wait(s_ve)
        for tb in range(T // 128 if do_gate else 0):
            for kc in range(8):
                P.tensor.run(lambda e, tb=tb, kc=kc: e.matmul(
                    B(2, 128, 8, co=tb * 8),
                    H2[:, kc * T + tb * 128: kc * T + tb * 128 + 128],
                    W["wg"][:, kc * 8:(kc + 1) * 8],
                    start=(kc == 0), stop=(kc == 7)),
                    inc=(s_te if kc == 7 else None))
        P.vector.wait(s_te)
        for tb in range(T // 128 if do_gate else 0):
            P.vector.run(lambda e, tb=tb: e.tensor_add(
                gsb[:, tb * 8:(tb + 1) * 8], B(2, 128, 8, co=tb * 8),
                W["bgr"][0:1, :].broadcast_to([128, 8]) if False else W["bgr"][:, :]), inc=s_ve)
        consumed(2, s_ve)
        for tb in range(T // 128 if do_gate else 0):
            gs = gsb[:, tb * 8:(tb + 1) * 8]
            g3 = gs.rearrange("p (q c) -> p q c", c=8)
            P.vector.run(lambda e, gs=gs, g3=g3: e.reduce_max(mx[:, 0:1], g3, AX.X), inc=s_ve)
            P.vector.wait(s_ve)
            P.vector.run(lambda e, gs=gs: e.tensor_sub(
                gs, gs, mx[:, 0:1].broadcast_to([128, 8])), inc=s_ve)
            P.scalar.wait(s_ve)
            P.scalar.run(lambda e, gs=gs: e.activation(gs, gs, AF.Exp), inc=s_se)
            P.vector.wait(s_se)
            P.vector.run(lambda e, gs=gs: e.reduce_sum(
                sm[:, 0:1], gs.rearrange("p (q c) -> p q c", c=8), AX.X), inc=s_ve)
            P.vector.wait(s_ve)
            P.vector.run(lambda e: e.reciprocal(rc[:, 0:1], sm[:, 0:1]), inc=s_ve)
            P.vector.wait(s_ve)
            P.vector.run(lambda e, gs=gs: e.tensor_mul(
                gs, gs, rc[:, 0:1].broadcast_to([128, 8])), inc=s_ve)
        if do_gate:
            P.tensor.wait(s_ve)
            for tb in range(T // 128):
                P.tensor.run(lambda e, tb=tb: e.transpose(
                    B(3, 8, 128, co=tb * 128), gsb[:, tb * 8:(tb + 1) * 8], W["eye"][:]),
                    inc=(s_te if tb == (T // 128) - 1 else None))
            P.vector.wait(s_te)
            for tb in range(T // 128):
                P.vector.run(lambda e, tb=tb: e.tensor_copy(
                    gt[:, tb * 128:(tb + 1) * 128], B(3, 8, 128, co=tb * 128)), inc=s_ve)
            consumed(3, s_ve)
        for c in range(8):
            cs = slice(c * T, (c + 1) * T)
            P.vector.run(lambda e, cs=cs: e.scalar_tensor_tensor(
                rt[:, cs], X2[:, cs], P.cc("half"), rt[:, cs],
                ALU.mult, ALU.add), inc=s_ve)
        P.sync.wait(s_ve)
        P.sync.wait(s_se)
        P.dma(s_v[:, :, t0:t0 + T], rtv, s_st)
        P.dma(h2_v[:, :, t0:t0 + T], h2v, s_st)
        P.dma(gate_out_dram[:, t0:t0 + T], gt[:], s_st)
    if dbg is not None:
        P.sync.wait(s_ve)
        P.dma(dbg["sraw"], dbg_sraw[:], s_st)
        P.dma(dbg["abf"], dbg_abf[:], s_st)
        P.dma(dbg["at"], dbg_at[:], s_st)
        P.plans["gpsimd"].wait(s_ve)
        P.dma(dbg["vzz"], dbg_vzz[:], P.sem("s_stg2"), eng="gpsimd")
        P.sync.wait(P.sem("s_stg2"))
    P.sync.wait(s_st)


def emit_ffnB(P, st, W, s_in_dram, h2_in_dram, gate_in_dram, out_dram, L=4096, T=256, dbg=None):
    nc = P.nc
    n_t = L // T
    sb = lambda n, s, d: st.enter_context(nc.sbuf_tensor(f"{n}_ph{P._phase_idx}", s, d))
    stt = sb("b_st", [128, 8 * T], F32)
    H2 = sb("b_h2", [128, 8 * T], BF16)
    HS = sb("b_hs", [128, 8 * T], BF16)
    gt = sb("b_gt", [8, T], F32)
    gtb = sb("b_gtb", [32, T], BF16)
    gb8 = sb("b_gb8", [128, 8 * T], BF16)
    ps = st.enter_context(nc.psum_tensor(f"b_ps_ph{P._phase_idx}", [128, 4096], F32))
    if dbg is not None:
        dbg_gb = sb("dbg_gb", [128, 8 * T], F32)
        dbg_hs = sb("dbg_hs", [128, 8 * T], F32)
    s_ld = P.sem("s_ld")
    s_st = P.sem("s_st")
    s_te = P.sem("s_te")
    s_ve = P.sem("s_ve")
    bank_free = {}

    def te_bank(b):
        for sem, cnt in bank_free.pop(b, []):
            P.tensor.wait(sem, cnt)

    def consumed(b, sem):
        bank_free.setdefault(b, []).append((sem, sem.n))

    s_v = s_in_dram.rearrange("(c p) t -> p c t", p=128)
    h2_v = h2_in_dram.rearrange("(c p) t -> p c t", p=128)
    o_v = out_dram.rearrange("(c p) t -> p c t", p=128)
    stv = stt[:].rearrange("p (c t) -> p c t", c=8)
    h2v = H2[:].rearrange("p (c t) -> p c t", c=8)

    P.vector.run(lambda e: e.memset(gtb[:], 0.0), inc=s_ve)
    for i in range(n_t):
        t0 = i * T
        P.sync.wait(s_ve)
        P.sync.wait(s_st)
        P.dma(stv, s_v[:, :, t0:t0 + T], s_ld)
        P.dma(h2v, h2_v[:, :, t0:t0 + T], s_ld)
        P.dma(gt[:], gate_in_dram[:, t0:t0 + T], s_ld)
        P.vector.wait(s_ld)
        P.vector.run(lambda e: e.tensor_copy(gtb[0:8, :], gt[:]), inc=s_ve)
        P.tensor.wait(s_ve)
        te_bank(0)
        te_bank(1)
        for e_ in range(8):
            bco = (e_ % 2) * 512
            if e_ >= 2:
                P.tensor.wait(s_ve)
            P.tensor.run(lambda e, e_=e_, bco=bco: e.matmul(
                ps[:, bco: bco + T],
                W["sel"][0:32, e_ * 128:(e_ + 1) * 128], gtb[:],
                start=True, stop=True), inc=s_te)
            P.vector.wait(s_te)
            P.vector.run(lambda e, e_=e_, bco=bco: e.tensor_copy(
                gb8[:, e_ * T:(e_ + 1) * T], ps[:, bco: bco + T]), inc=s_ve)
        consumed(0, s_ve)
        consumed(1, s_ve)
        for e_ in range(8):
            if e_ > 0:
                P.vector.wait(s_te)
            gbv = gb8[:, 0:8 * T].rearrange("p (o t) -> p o t", o=8)[:, 0:1, :].broadcast_to([128, 8, T]) if False else None
            gview = gb8[:, :].rearrange("p (ee t) -> p ee t", ee=8)[:, e_:e_ + 1, :].broadcast_to([128, 8, T])
            P.vector.run(lambda e, e_=e_, gview=gview: e.tensor_mul(
                HS[:].rearrange("p (c t) -> p c t", c=8), H2[:].rearrange("p (c t) -> p c t", c=8),
                gview), inc=s_ve)
            P.tensor.wait(s_ve)
            for mc in range(8):
                b = mc
                if e_ == 0:
                    te_bank(b)
                for kc in range(8):
                    sh = 256 if T == 256 else 0
                    P.tensor.run(lambda e, b=b, mc=mc, kc=kc, e_=e_, sh=sh: e.matmul(
                        ps[:, b * 512 + sh: b * 512 + sh + T] if b <= 1 else ps[:, b * 512: b * 512 + T],
                        W["we"][:, e_ * 8192 + kc * 1024 + mc * 128: e_ * 8192 + kc * 1024 + (mc + 1) * 128],
                        HS[:, kc * T:(kc + 1) * T],
                        start=(e_ == 0 and kc == 0), stop=(e_ == 7 and kc == 7)),
                        inc=(s_te if (kc == 7 and mc == 7) else None))
        P.vector.wait(s_te)
        if dbg is not None and i == n_t - 1:
            P.vector.run(lambda e: e.tensor_copy(dbg_gb[:], gb8[:]))
            P.vector.run(lambda e: e.tensor_copy(dbg_hs[:], HS[:]), inc=s_ve)
        for mc in range(8):
            b = mc
            cs = slice(mc * T, (mc + 1) * T)
            sh = 256 if T == 256 else 0
            src = ps[:, b * 512 + sh: b * 512 + sh + T] if b <= 1 else ps[:, b * 512: b * 512 + T]
            P.vector.run(lambda e, src=src, cs=cs: e.scalar_tensor_tensor(
                stt[:, cs], src, P.cc("half"), stt[:, cs],
                ALU.mult, ALU.add), inc=s_ve)
        for b in range(8):
            consumed(b, s_ve)
        P.sync.wait(s_ve)
        P.dma(o_v[:, :, t0:t0 + T], stv, s_st)
    if dbg is not None:
        P.sync.wait(s_ve)
        P.dma(dbg["gb8"], dbg_gb[:], s_st)
        P.dma(dbg["hs"], dbg_hs[:], s_st)
    P.sync.wait(s_st)



from contextlib import ExitStack

CONSTS = {"eps5": 1e-5, "eps6": 1e-6, "negh": -0.5, "neg1": -1.0,
          "one": 1.0, "r1024": 1.0 / 1024.0, "q8": 0.125, "half": 0.5}


def build_program(L=4096, T_f=512, T_m=256):
    nc = bass.Bass("TRN2", target_bir_lowering=False, detect_race_conditions=False)
    dram = {}
    for k, (shape, dt) in INPUT_SPECS.items():
        dram[k] = nc.dram_tensor(k, list(shape), dt, kind="ExternalInput").ap()
    out = nc.dram_tensor("out", [1024, L], F32, kind="ExternalOutput").ap()
    s_scr = nc.dram_tensor("s_scr", [1024, L], F32, kind="Internal").ap()
    h2_scr = nc.dram_tensor("h2_scr", [1024, L], BF16, kind="Internal").ap()
    g_scr = nc.dram_tensor("g_scr", [8, L], F32, kind="Internal").ap()
    r1_scr = nc.dram_tensor("r1_scr", [1024, L], F32, kind="Internal").ap()
    r2_scr = nc.dram_tensor("r2_scr", [1024, L], F32, kind="Internal").ap()
    with ExitStack() as big:
        P = Prog(nc)
        P._stack = big
        P.setup_consts(CONSTS)
        with ExitStack() as st:
            W = emit_load_ffn_weights(P, st, dram, "A")
            emit_ffnA(P, st, W, dram["xT"], s_scr, h2_scr, g_scr, L=L, T=T_f)
            P.flush_phase()
        with ExitStack() as st:
            W = emit_load_ffn_weights(P, st, dram, "B")
            emit_ffnB(P, st, W, s_scr, h2_scr, g_scr, r1_scr, L=L, T=T_f)
            P.flush_phase()
        with ExitStack() as st:
            W = emit_load_mamba_weights(P, st, dram)
            emit_mamba(P, st, W, r1_scr, r2_scr, dram["c_sel"], L=L, T=T_m)
            P.flush_phase()
        with ExitStack() as st:
            W = emit_load_ffn_weights(P, st, dram, "A")
            emit_ffnA(P, st, W, r2_scr, s_scr, h2_scr, g_scr, L=L, T=T_f)
            P.flush_phase()
        with ExitStack() as st:
            W = emit_load_ffn_weights(P, st, dram, "B")
            emit_ffnB(P, st, W, s_scr, h2_scr, g_scr, out, L=L, T=T_f)
            P.flush_phase()
    return nc


def kernel(**inputs):
    common = host_layout(inputs, 0)
    del common["xT"]
    nc = build_program()
    in_maps = []
    for c in range(NCORES):
        m = dict(common)
        m["xT"] = np.ascontiguousarray(inputs["L0_attn_x"][c].T)
        in_maps.append(m)
    res = run_bass_kernel_spmd(nc, in_maps, core_ids=list(range(NCORES)))
    return np.stack([np.ascontiguousarray(r["out"].T) for r in res.results], axis=0)


# version 17
# speedup vs baseline: 1.0549x; 1.0077x over previous
import numpy as np
import ml_dtypes
import concourse.bass as bass
import concourse.mybir as mybir
from concourse.bass_utils import run_bass_kernel_spmd
from concourse.bass_isa import ReduceOp
RED_MAX = ReduceOp.max if hasattr(ReduceOp, "max") else "max"
RED_ADD = ReduceOp.add if hasattr(ReduceOp, "add") else "add" 

F32 = mybir.dt.float32
BF16 = mybir.dt.bfloat16
AF = mybir.ActivationFunctionType
ALU = mybir.AluOpType
AX = mybir.AxisListType

D_MODEL = 1024
D_INNER = 2048
D_STATE = 16
D_CONV = 4
DT_RANK = 64
NCORES = 8

LN_EPS_OUTER = 1e-5
LN_EPS_INNER = 1e-6
NEG_BIG = -30000.0
import os
SCE_EVAC = os.environ.get("SCE_EVAC", "0") == "1"



class Sem:
    __slots__ = ("h", "n")

    def __init__(self, h):
        self.h = h
        self.n = 0


class EngPlan:
    def __init__(self, name):
        self.name = name
        self.steps = []
        self.waited = {}

    def wait(self, sem, val=None):
        v = sem.n if val is None else val
        if self.waited.get(id(sem), -1) >= v:
            return
        self.waited[id(sem)] = v
        h = sem.h
        self.steps.append(lambda e, h=h, v=v: e.wait_ge(h, v))

    def run(self, fn, inc=None, n=1):
        if inc is not None:
            inc.n += n
            h = inc.h
            self.steps.append(lambda e, fn=fn, h=h, n=n: fn(e).then_inc(h, n))
        else:
            self.steps.append(fn)


class Prog:
    ENGS = ("sync", "vector", "scalar", "tensor", "gpsimd")

    def __init__(self, nc):
        self.nc = nc
        self.sems = {}
        self.plans = {k: EngPlan(k) for k in self.ENGS}
        self.sync = self.plans["sync"]
        self.vector = self.plans["vector"]
        self.scalar = self.plans["scalar"]
        self.tensor = self.plans["tensor"]
        self.gpsimd = self.plans["gpsimd"]
        self._phase_idx = 0
        self._stack = None

    def sem(self, name):
        if name not in self.sems:
            h = self._stack.enter_context(self.nc.semaphore(name))
            self.sems[name] = Sem(h)
        return self.sems[name]

    def setup_consts(self, values):
        n = len(values)
        t = self._stack.enter_context(self.nc.sbuf_tensor("cconsts", [128, max(n, 1)], F32))
        self._consts = t
        self._cmap = {}
        for i, (k, v) in enumerate(values.items()):
            self._cmap[k] = i
            self.vector.run(lambda e, i=i, v=v: e.memset(t[:, i:i + 1], v))
        s = self.sem("s_const")
        self.vector.run(lambda e: e.engine_nop(), inc=s)
        for k in ("scalar", "vector", "gpsimd", "tensor"):
            self.plans[k].wait(s)

    def tile_barrier(self):
        bar = self.sem("tilebar")
        for k in ("vector", "scalar", "tensor"):
            self.plans[k].run(lambda e: e.drain(), inc=bar, n=1)
        target = bar.n
        for k in ("vector", "scalar", "tensor", "sync", "gpsimd"):
            self.plans[k].wait(bar, target)

    def cc(self, name, rows=128):
        i = self._cmap[name]
        return self._consts[0:rows, i:i + 1]

    def dma(self, out, in_, sem, eng="sync"):
        self.plans[eng].run(lambda e, o=out, i=in_: e.dma_start(out=o, in_=i), inc=sem, n=16)

    def flush_phase(self):
        bar = self.sem("phasebar")
        for k in ("vector", "scalar", "tensor", "gpsimd"):
            self.plans[k].run(lambda e: e.drain(), inc=bar, n=1)
        self.plans["sync"].run(lambda e: e.sem_inc(bar.h, 1))
        bar.n += 1
        nc = self.nc
        with nc.Block() as blk:
            for k in self.ENGS:
                steps = self.plans[k].steps
                getattr(blk, k)(lambda e, steps=steps: [f(e) for f in steps])
        self._phase_idx += 1
        target = bar.n
        for k in self.ENGS:
            self.plans[k] = EngPlan(k)
        self.sync = self.plans["sync"]
        self.vector = self.plans["vector"]
        self.scalar = self.plans["scalar"]
        self.tensor = self.plans["tensor"]
        self.gpsimd = self.plans["gpsimd"]
        for k in self.ENGS:
            self.plans[k].wait(self.sems["phasebar"], target)



def _img_chunked(w, rows=128):
    K, M = w.shape
    return np.ascontiguousarray(w.reshape(K // rows, rows, M).transpose(1, 0, 2).reshape(rows, -1))


def sel_const():
    sel = np.zeros((32, 32 * 128), np.float32)
    for n in range(32):
        sel[n, n * 128:(n + 1) * 128] = 1.0
    return sel


def upv_img(wup, bup):
    out = np.zeros((65, 2048), np.float32)
    for dh in range(8):
        for q in range(2):
            for m in range(4):
                dd = dh * 8 + q * 4 + m
                col0 = dh * 256 + q * 128 + m * 32
                out[0:64, col0:col0 + 16] = wup[:, 16 * 0:16 * 16:128][:, :] if False else 0
                for g in range(16):
                    out[0:64, col0 + g] = wup[:, g * 128 + 64 + dd]
                    out[64, col0 + g] = bup[g * 128 + 64 + dd]
    return np.ascontiguousarray(out)


def host_layout(inp, core):
    f = np.ascontiguousarray
    d = {}
    d["xT"] = f(inp["L0_attn_x"][core].T)
    d["w_inproj"] = _img_chunked(f(inp["L0_attn_in_proj_w"].T))
    d["w_outproj"] = _img_chunked(f(inp["L0_attn_out_proj_w"].T))
    d["w_xproj"] = _img_chunked(f(inp["L0_attn_x_proj_w"].T))
    d["w_dtproj"] = f(inp["L0_attn_dt_proj_w"].T)
    d["v_convw"] = f(inp["L0_attn_conv_w"][:, 0, :].reshape(16, 128, 4).transpose(1, 0, 2).reshape(128, 64))
    d["v_convb"] = f(inp["L0_attn_conv_b"].reshape(16, 128).T)
    d["v_D"] = f(inp["L0_attn_D"].reshape(16, 128).T)
    d["v_dtb"] = f(inp["L0_attn_dt_proj_b"].reshape(16, 128).T)
    d["v_Alog"] = f(inp["L0_attn_A_log"].reshape(16, 128, 16).transpose(1, 0, 2).reshape(128, 256))
    d["v_lng"] = f(inp["L0_attn_ln_gamma"].reshape(8, 128).T)
    d["v_lnb"] = f(inp["L0_attn_ln_beta"].reshape(8, 128).T)
    d["w_q"] = _img_chunked(f(inp["L0_ffn_Wq"]))
    d["w_down"] = _img_chunked(f(inp["L0_ffn_Wdown"]))
    wup = inp["L0_ffn_Wup"]
    d["w_upk"] = f(wup.reshape(64, 16, 128)[:, :, 0:64].reshape(64, 16 * 64))
    d["w_upv"] = upv_img(wup, inp["L0_ffn_bup"])
    d["w_o"] = _img_chunked(f(inp["L0_ffn_Wo"]))
    d["w_g"] = _img_chunked(f(inp["L0_ffn_Wg"]))
    d["w_e"] = f(inp["L0_ffn_We"].reshape(8, 8, 128, 1024).transpose(2, 0, 1, 3).reshape(128, 8 * 8 * 1024))
    d["v_g1"] = f(inp["L0_ffn_gamma1"].reshape(8, 128).T)
    d["v_g2"] = f(inp["L0_ffn_gamma2"].reshape(8, 128).T)
    d["v_bq"] = f(inp["L0_ffn_bq"].reshape(8, 128).T)
    d["v_bo"] = f(inp["L0_ffn_bo"].reshape(8, 128).T)
    d["v_bdown"] = f(inp["L0_ffn_bdown"].reshape(64, 1))
    bup = inp["L0_ffn_bup"]
    d["v_bupk"] = f(bup.reshape(16, 128)[:, 0:64].T)
    d["v_bg"] = f(inp["L0_ffn_bg"].reshape(8, 1))
    d["c_sel"] = sel_const()
    d["c_mask"] = mask_const()
    d["c_eye"] = np.eye(128, dtype=np.float32)
    d["c_bgr"] = f(np.broadcast_to(inp["L0_ffn_bg"][None, :], (128, 8)).copy())
    return d


INPUT_SPECS = {
    "xT": ([1024, 4096], F32),
    "w_inproj": ([128, 8 * 4096], F32),
    "w_outproj": ([128, 16 * 1024], F32),
    "w_xproj": ([128, 16 * 96], F32),
    "w_dtproj": ([64, 2048], F32),
    "v_convw": ([128, 64], F32),
    "v_convb": ([128, 16], F32),
    "v_D": ([128, 16], F32),
    "v_dtb": ([128, 16], F32),
    "v_Alog": ([128, 256], F32),
    "v_lng": ([128, 8], F32),
    "v_lnb": ([128, 8], F32),
    "w_q": ([128, 8 * 1024], F32),
    "w_down": ([128, 8 * 64], F32),
    "w_upk": ([64, 16 * 64], F32),
    "w_upv": ([65, 2048], F32),
    "w_o": ([128, 8 * 1024], F32),
    "w_g": ([128, 64], F32),
    "w_e": ([128, 8 * 8 * 1024], F32),
    "v_g1": ([128, 8], F32),
    "v_g2": ([128, 8], F32),
    "v_bq": ([128, 8], F32),
    "v_bo": ([128, 8], F32),
    "v_bdown": ([64, 1], F32),
    "v_bupk": ([64, 16], F32),
    "v_bg": ([8, 1], F32),
    "c_sel": ([32, 32 * 128], F32),
    "c_mask": ([128, 512], F32),
    "c_eye": ([128, 128], F32),
    "c_bgr": ([128, 8], F32),
}



def emit_load_mamba_weights(P, st, dram):
    nc = P.nc
    sb = lambda n, s, d: st.enter_context(nc.sbuf_tensor(f"{n}_ph{P._phase_idx}", s, d))
    W = {}
    W["inW"] = sb("m_inW", [128, 8 * 4096], BF16)
    W["opW"] = sb("m_opW", [128, 16 * 1024], BF16)
    W["xpW"] = sb("m_xpW", [128, 16 * 96], BF16)
    W["dtW"] = sb("m_dtW", [64, 2048], BF16)
    W["cw"] = sb("m_cw", [128, 64], F32)
    W["cb"] = sb("m_cb", [128, 16], F32)
    W["Dv"] = sb("m_Dv", [128, 16], F32)
    W["dtb"] = sb("m_dtb", [128, 16], F32)
    W["An"] = sb("m_An", [128, 256], F32)
    W["lng"] = sb("m_lng", [128, 8], F32)
    W["lnb"] = sb("m_lnb", [128, 8], F32)
    W["onesc"] = sb("m_onesc", [128, 1], BF16)
    W["ones1"] = sb("m_ones1", [1, 128], BF16)
    W["ones1f"] = sb("m_ones1f", [1, 128], F32)
    s_w = P.sem("s_w")
    s_wg = P.sem("s_wg")
    P.dma(W["inW"][:], dram["w_inproj"], s_wg, eng="gpsimd")
    P.dma(W["opW"][:], dram["w_outproj"], s_wg, eng="gpsimd")
    P.dma(W["xpW"][:], dram["w_xproj"], s_wg, eng="gpsimd")
    P.dma(W["dtW"][:], dram["w_dtproj"], s_wg, eng="gpsimd")
    for k, src in [("cw", "v_convw"), ("cb", "v_convb"), ("Dv", "v_D"),
                   ("dtb", "v_dtb"), ("An", "v_Alog"), ("lng", "v_lng"), ("lnb", "v_lnb")]:
        P.dma(W[k][:], dram[src], s_w, eng="sync")
    P.vector.run(lambda e, W=W: e.memset(W["onesc"][:], 1.0))
    P.vector.run(lambda e, W=W: e.memset(W["ones1"][:], 1.0))
    P.vector.run(lambda e, W=W: e.memset(W["ones1f"][:], 1.0))
    P.scalar.wait(s_w)
    P.scalar.wait(s_wg)
    s_se = P.sem("s_se")
    P.scalar.run(lambda e, W=W: e.activation(W["An"][:], W["An"][:], AF.Exp), inc=s_se)
    P.vector.wait(s_se)
    s_ve = P.sem("s_ve")
    P.vector.run(lambda e, W=W: e.tensor_scalar_mul(W["An"][:], W["An"][:], -1.0), inc=s_ve)
    return W



def emit_mamba(P, st, W, in_dram, out_dram, in_sel, L=4096, T=256, dbg=None):
    nc = P.nc
    n_t = L // T
    NC = 16
    sb = lambda n, s, d: st.enter_context(nc.sbuf_tensor(f"{n}_ph{P._phase_idx}", s, d))
    rt = sb("m_rt", [128, 8 * T], F32)
    LN0 = sb("m_ln0", [128, 8 * T], F32)
    hm = sb("m_hm", [128, 8 * T], BF16)
    rb = hm
    UC = sb("m_UC", [128, NC * T], BF16)
    ZS = sb("m_ZS", [128, NC * T], BF16)
    Bb = sb("m_Bb", [128, NC * T], BF16)
    Cb = sb("m_Cb", [128, NC * T], BF16)
    DELTA = sb("m_DELTA", [128, NC * T], F32)
    SEL = sb("m_SEL", [32, 32 * 128], BF16)
    dtb_sb = sb("m_dtbsb", [64, T], BF16)
    bcrow = sb("m_bcrow", [32, T], BF16)
    ue = sb("m_ue", [128, T + 4], BF16)
    utail = sb("m_utail", [128, NC * 3], BF16)
    cacc = sb("m_cacc", [128, T], F32)
    accv = sb("m_accv", [128, T], F32)
    sp_e = sb("m_spe", [128, T], F32)
    sp_r = sb("m_spr", [128, T], F32)
    sg = sb("m_sg", [128, T], BF16)
    xt = sb("m_xt", [128, T], BF16)
    dab = [sb(f"m_dab{i}", [128, T], F32) for i in range(2)]
    h32 = [sb(f"m_h32{i}", [128, T], F32) for i in range(2)]
    DBU = sb("m_DBU", [128, 16 * T], BF16)
    yh = sb("m_yh", [128, T], BF16)
    y2 = sb("m_y2", [128, T], BF16)
    STATE = sb("m_state", [128, 256], F32)
    mu_r = sb("m_mu", [1, T], F32)
    ms_r = sb("m_ms", [1, T], F32)
    rs_r = sb("m_rs", [1, T], F32)
    ps = [st.enter_context(nc.psum_tensor(f"m_ps{i}_ph{P._phase_idx}", [128, 512], F32)) for i in range(8)]

    s_ld = P.sem("s_ld")
    s_st = P.sem("s_st")
    s_te = P.sem("s_te")
    s_ve = P.sem("s_ve")
    s_se = P.sem("s_se")
    s_w = P.sem("s_w")
    s_gp = P.sem("s_gp")

    bank_free = {}

    def te_bank(b):
        if b in bank_free:
            sem, cnt = bank_free[b]
            P.tensor.wait(sem, cnt)

    def consumed(b, sem):
        bank_free[b] = (sem, sem.n)

    in_v = in_dram.rearrange("(c p) t -> p c t", p=128)
    out_v = out_dram.rearrange("(c p) t -> p c t", p=128)
    rtv = rt[:].rearrange("p (c t) -> p c t", c=8)

    s_wg2 = P.sem("s_wg2")
    P.dma(SEL[:], in_sel, s_wg2, eng="gpsimd")
    for eng in ("vector", "scalar", "tensor"):
        P.plans[eng].wait(s_wg2)
    P.vector.run(lambda e: e.memset(STATE[:], 0.0))
    P.vector.run(lambda e: e.memset(utail[:], 0.0), inc=s_ve)
    for eng in ("vector", "scalar", "tensor"):
        P.plans[eng].wait(s_w)
        P.plans[eng].wait(P.sem("s_wg"))

    for i in range(n_t):
        t0 = i * T
        P.sync.wait(s_ve)
        P.sync.wait(s_st)
        P.dma(rtv, in_v[:, :, t0:t0 + T], s_ld)
        P.vector.wait(s_ld)
        P.vector.wait(s_te)
        P.vector.run(lambda e: e.tensor_copy(rb[:], rt[:]), inc=s_ve)
        P.tensor.wait(s_ve)
        te_bank(0)
        for c in range(8):
            P.tensor.run(lambda e, c=c: e.matmul(
                ps[0][0:1, 0:T], W["onesc"][:], rb[:, c * T:(c + 1) * T],
                start=(c == 0), stop=(c == 7)), inc=(s_te if c == 7 else None))
        P.scalar.wait(s_te)
        P.scalar.run(lambda e: e.activation(rb[:], rb[:], AF.Square), inc=s_se)
        P.tensor.wait(s_se)
        te_bank(1)
        for c in range(8):
            P.tensor.run(lambda e, c=c: e.matmul(
                ps[1][0:1, 0:T], W["onesc"][:], rb[:, c * T:(c + 1) * T],
                start=(c == 0), stop=(c == 7)), inc=(s_te if c == 7 else None))
        P.scalar.wait(s_te)
        P.scalar.run(lambda e: e.activation(mu_r[:], ps[0][0:1, 0:T], AF.Copy, scale=P.cc("r1024", 1)))
        P.scalar.run(lambda e: e.activation(ms_r[:], ps[1][0:1, 0:T], AF.Copy, scale=P.cc("r1024", 1)))
        P.scalar.run(lambda e: e.activation(rs_r[:], mu_r[:], AF.Square), inc=s_se)
        consumed(0, s_se)
        consumed(1, s_se)
        P.vector.wait(s_se)
        P.vector.run(lambda e: e.tensor_sub(ms_r[:], ms_r[:], rs_r[:]), inc=s_ve)
        P.scalar.wait(s_ve)
        P.scalar.run(lambda e: e.activation(ms_r[:], ms_r[:], AF.Ln, bias=P.cc("eps5", 1)))
        P.scalar.run(lambda e: e.activation(rs_r[:], ms_r[:], AF.Exp, scale=P.cc("negh", 1)), inc=s_se)
        P.tensor.wait(s_se)
        te_bank(0)
        te_bank(1)
        P.tensor.run(lambda e: e.matmul(ps[0][:, 0:T], W["ones1f"][:], mu_r[:], start=True, stop=True))
        P.tensor.run(lambda e: e.matmul(ps[1][:, 0:T], W["ones1f"][:], rs_r[:], start=True, stop=True), inc=s_te)
        P.vector.wait(s_te)
        mub = ps[0][:, 0:T].rearrange("p (o t) -> p o t", o=1).broadcast_to([128, 8, T])
        rsb = ps[1][:, 0:T].rearrange("p (o t) -> p o t", o=1).broadcast_to([128, 8, T])
        l3 = LN0[:].rearrange("p (c t) -> p c t", c=8)
        r3 = rt[:].rearrange("p (c t) -> p c t", c=8)
        P.vector.run(lambda e, l3=l3, r3=r3, mub=mub: e.tensor_sub(l3, r3, mub))
        P.vector.run(lambda e, l3=l3, rsb=rsb: e.tensor_mul(l3, l3, rsb))
        P.vector.run(lambda e, l3=l3, r3=r3: e.tensor_add(r3, r3, l3), inc=s_ve)
        consumed(0, s_ve)
        consumed(1, s_ve)
        P.vector.run(lambda e: e.tensor_copy(rb[:], LN0[:]), inc=s_ve)
        P.tensor.wait(s_ve)
        te_bank(0)
        for c in range(8):
            P.tensor.run(lambda e, c=c: e.matmul(
                ps[0][0:1, 0:T], W["onesc"][:], rb[:, c * T:(c + 1) * T],
                start=(c == 0), stop=(c == 7)), inc=(s_te if c == 7 else None))
        P.scalar.wait(s_te)
        P.scalar.run(lambda e: e.activation(rb[:], rb[:], AF.Square), inc=s_se)
        P.tensor.wait(s_se)
        te_bank(1)
        for c in range(8):
            P.tensor.run(lambda e, c=c: e.matmul(
                ps[1][0:1, 0:T], W["onesc"][:], rb[:, c * T:(c + 1) * T],
                start=(c == 0), stop=(c == 7)), inc=(s_te if c == 7 else None))
        P.scalar.wait(s_te)
        P.scalar.run(lambda e: e.activation(mu_r[:], ps[0][0:1, 0:T], AF.Copy, scale=P.cc("r1024", 1)))
        P.scalar.run(lambda e: e.activation(ms_r[:], ps[1][0:1, 0:T], AF.Copy, scale=P.cc("r1024", 1)))
        P.scalar.run(lambda e: e.activation(rs_r[:], mu_r[:], AF.Square), inc=s_se)
        consumed(0, s_se)
        consumed(1, s_se)
        P.vector.wait(s_se)
        P.vector.run(lambda e: e.tensor_sub(ms_r[:], ms_r[:], rs_r[:]), inc=s_ve)
        P.scalar.wait(s_ve)
        P.scalar.run(lambda e: e.activation(ms_r[:], ms_r[:], AF.Ln, bias=P.cc("eps6", 1)))
        P.scalar.run(lambda e: e.activation(rs_r[:], ms_r[:], AF.Exp, scale=P.cc("negh", 1)), inc=s_se)
        P.tensor.wait(s_se)
        te_bank(0)
        te_bank(1)
        P.tensor.run(lambda e: e.matmul(ps[0][:, 0:T], W["ones1f"][:], mu_r[:], start=True, stop=True))
        P.tensor.run(lambda e: e.matmul(ps[1][:, 0:T], W["ones1f"][:], rs_r[:], start=True, stop=True), inc=s_te)
        P.vector.wait(s_te)
        for c in range(8):
            cs = slice(c * T, (c + 1) * T)
            P.vector.run(lambda e, cs=cs: e.tensor_sub(accv[:], LN0[:, cs], ps[0][:, 0:T]))
            P.vector.run(lambda e: e.tensor_mul(accv[:], accv[:], ps[1][:, 0:T]))
            P.vector.run(lambda e, cs=cs, c=c: e.tensor_scalar(
                hm[:, cs], accv[:], W["lng"][:, c:c + 1], W["lnb"][:, c:c + 1],
                ALU.mult, ALU.add), inc=s_ve)
        consumed(0, s_ve)
        consumed(1, s_ve)
        P.tensor.wait(s_ve)
        te_bank(2)
        for mc in range(32):
            b = 3 + (mc % 2) * 4
            te_bank(b)
            for kc in range(8):
                P.tensor.run(lambda e, b=b, kc=kc, mc=mc: e.matmul(
                    ps[b][:, 0:T],
                    W["inW"][:, kc * 4096 + mc * 128: kc * 4096 + (mc + 1) * 128],
                    hm[:, kc * T:(kc + 1) * T],
                    start=(kc == 0), stop=(kc == 7)), inc=(s_te if kc == 7 else None))
            if mc < 16:
                P.vector.wait(s_te)
                P.vector.run(lambda e, mc=mc: e.tensor_copy(ue[:, 0:3], utail[:, mc * 3:mc * 3 + 3]))
                P.vector.run(lambda e, b=b: e.tensor_copy(ue[:, 3:3 + T], ps[b][:, 0:T]), inc=s_ve)
                consumed(b, s_ve)
                P.vector.run(lambda e, mc=mc: e.tensor_scalar(
                    cacc[:], ue[:, 0:T], W["cw"][:, mc * 4:mc * 4 + 1],
                    W["cb"][:, mc:mc + 1], ALU.mult, ALU.add))
                for k in range(1, 4):
                    P.vector.run(lambda e, k=k, mc=mc: e.scalar_tensor_tensor(
                        cacc[:], ue[:, k:k + T], W["cw"][:, mc * 4 + k:mc * 4 + k + 1], cacc[:],
                        ALU.mult, ALU.add), inc=(s_ve if k == 3 else None))
                P.vector.run(lambda e, mc=mc: e.tensor_copy(
                    utail[:, mc * 3:mc * 3 + 3], ue[:, T:T + 3]))
                P.scalar.wait(s_ve)
                P.scalar.run(lambda e, mc=mc: e.activation(
                    sg[:], cacc[:], AF.Sigmoid), inc=s_se)
                P.vector.wait(s_se)
                P.vector.run(lambda e, mc=mc: e.tensor_mul(
                    UC[:, mc * T:(mc + 1) * T], cacc[:], sg[:]), inc=s_ve)
                P.tensor.wait(s_ve)
                P.tensor.run(lambda e, mc=mc: e.matmul(
                    ps[2][0:96, 0:T], W["xpW"][:, mc * 96:(mc + 1) * 96],
                    UC[:, mc * T:(mc + 1) * T],
                    start=(mc == 0), stop=(mc == 15)), inc=(s_te if mc == 15 else None))
            else:
                zc = mc - 16
                P.scalar.wait(s_te)
                P.scalar.run(lambda e, b=b: e.activation(sg[:], ps[b][:, 0:T], AF.Sigmoid), inc=s_se)
                P.vector.wait(s_se)
                P.vector.run(lambda e, b=b, zc=zc: e.tensor_mul(
                    ZS[:, zc * T:(zc + 1) * T], ps[b][:, 0:T], sg[:]), inc=s_ve)
                consumed(b, s_ve)
        P.vector.wait(s_te)
        P.vector.run(lambda e: e.tensor_copy(dtb_sb[:], ps[2][0:64, 0:T]))
        P.vector.run(lambda e: e.tensor_copy(bcrow[:], ps[2][64:96, 0:T]), inc=s_ve)
        consumed(2, s_ve)
        P.tensor.wait(s_ve)
        for n in range(32):
            b = n % 2
            te_bank(b)
            P.tensor.run(lambda e, b=b, n=n: e.matmul(
                ps[b][:, 0:T], SEL[:, n * 128:(n + 1) * 128], bcrow[:, 0:T],
                start=True, stop=True), inc=s_te)
            P.vector.wait(s_te)
            dst = Bb if n < 16 else Cb
            nn = n % 16
            P.vector.run(lambda e, b=b, dst=dst, nn=nn: e.tensor_copy(
                dst[:, nn * T:(nn + 1) * T], ps[b][:, 0:T]), inc=s_ve)
            consumed(b, s_ve)
        for c in range(NC):
            b = 2 + (c % 2)
            te_bank(b)
            P.tensor.run(lambda e, b=b, c=c: e.matmul(
                ps[b][:, 0:T], W["dtW"][:, c * 128:(c + 1) * 128], dtb_sb[:],
                start=True, stop=True), inc=s_te)
            P.scalar.wait(s_te)
            P.scalar.wait(s_ve)
            P.scalar.run(lambda e, b=b, c=c: e.activation(
                sp_e[:], ps[b][:, 0:T], AF.Abs, bias=W["dtb"][:, c:c + 1]))
            P.scalar.run(lambda e: e.activation(sp_e[:], sp_e[:], AF.Exp, scale=P.cc("neg1")))
            P.scalar.run(lambda e: e.activation(sp_e[:], sp_e[:], AF.Ln, bias=P.cc("one")))
            P.scalar.run(lambda e, b=b, c=c: e.activation(
                sp_r[:], ps[b][:, 0:T], AF.Relu, bias=W["dtb"][:, c:c + 1]), inc=s_se)
            consumed(b, s_se)
            P.vector.wait(s_se)
            P.vector.run(lambda e, c=c: e.tensor_add(
                DELTA[:, c * T:(c + 1) * T], sp_e[:], sp_r[:]), inc=s_ve)
        for b in range(8):
            te_bank(b)
        scan_ve_marks = {}
        for c in range(NC):
            ds = slice(c * T, (c + 1) * T)
            P.vector.run(lambda e, ds=ds, c=c: e.tensor_mul(xt[:], DELTA[:, ds], UC[:, ds]), inc=s_ve)
            P.scalar.wait(s_ve)
            xtv = xt[:].rearrange("p (o t) -> p o t", o=1).broadcast_to([128, 16, T])
            P.vector.run(lambda e, xtv=xtv: e.tensor_mul(
                DBU[:].rearrange("p (n t) -> p n t", n=16), xtv,
                Bb[:].rearrange("p (n t) -> p n t", n=16)))
            for n in range(16):
                pn = n % 2
                col = c * 16 + n
                if col >= 2:
                    P.scalar.wait(s_ve, scan_ve_marks[col - 2])
                    P.scalar.run(lambda e, pn=pn, col=col: e.activation(
                        STATE[:, col - 2:col - 1], h32[pn][:, T - 1:T],
                        AF.Copy, scale=P.cc("one")))
                P.scalar.run(lambda e, pn=pn, col=col, ds=ds: e.activation(
                    dab[pn][:], DELTA[:, ds], AF.Exp, scale=W["An"][:, col:col + 1]), inc=s_se)
                P.vector.wait(s_se)
                P.vector.run(lambda e, pn=pn, col=col, n=n: e.tensor_tensor_scan(
                    h32[pn][:], dab[pn][:], DBU[:, n * T:(n + 1) * T],
                    STATE[:, col:col + 1], ALU.mult, ALU.add))
                P.vector.run(lambda e, pn=pn, n=n: e.tensor_mul(
                    yh[:], h32[pn][:], Cb[:, n * T:(n + 1) * T]))
                if n == 0:
                    P.vector.run(lambda e: e.tensor_copy(accv[:], yh[:]), inc=s_ve)
                else:
                    P.vector.run(lambda e: e.tensor_add(accv[:], accv[:], yh[:]), inc=s_ve)
                scan_ve_marks[col] = s_ve.n
            if c > 0:
                P.vector.wait(s_te)
            P.vector.run(lambda e, c=c: e.scalar_tensor_tensor(
                accv[:], UC[:, c * T:(c + 1) * T], W["Dv"][:, c:c + 1], accv[:],
                ALU.mult, ALU.add))
            P.vector.run(lambda e, c=c: e.tensor_mul(y2[:], accv[:], ZS[:, c * T:(c + 1) * T]), inc=s_ve)
            P.tensor.wait(s_ve)
            for mc in range(8):
                P.tensor.run(lambda e, b=mc, mc=mc, c=c: e.matmul(
                    ps[b][:, 0:T],
                    W["opW"][:, c * 1024 + mc * 128: c * 1024 + (mc + 1) * 128],
                    y2[:],
                    start=(c == 0), stop=(c == NC - 1)),
                    inc=(s_te if mc == 7 else None))
        for col in (NC * 16 - 2, NC * 16 - 1):
            P.scalar.wait(s_ve, scan_ve_marks[col])
            P.scalar.run(lambda e, col=col: e.activation(
                STATE[:, col:col + 1], h32[col % 2][:, T - 1:T],
                AF.Copy, scale=P.cc("one")), inc=(s_se if col == NC * 16 - 1 else None))
        P.vector.wait(s_te)
        for mc in range(8):
            P.vector.run(lambda e, b=mc, mc=mc: e.tensor_add(
                rt[:, mc * T:(mc + 1) * T], rt[:, mc * T:(mc + 1) * T], ps[b][:, 0:T]),
                inc=s_ve)
        for b in range(8):
            consumed(b, s_ve)
        P.sync.wait(s_ve)
        P.dma(out_v[:, :, t0:t0 + T], rtv, s_st)
    if dbg is not None:
        P.plans["gpsimd"].wait(s_ve)
        P.plans["gpsimd"].wait(s_se)
        for k, buf in [("UC", UC), ("ZS", ZS), ("Bb", Bb), ("Cb", Cb),
                       ("DELTA", DELTA), ("hm", hm), ("LN0", LN0)]:
            if k in dbg:
                P.dma(dbg[k], buf[:], P.sem("s_stg"), eng="gpsimd")
        P.sync.wait(P.sem("s_stg"))
    P.sync.wait(s_st)



def emit_load_ffn_weights(P, st, dram, phase):
    nc = P.nc
    sb = lambda n, s, d: st.enter_context(nc.sbuf_tensor(f"{n}_ph{P._phase_idx}", s, d))
    W = {}
    s_w = P.sem("s_w")
    s_wg = P.sem(f"s_wgf{P._phase_idx}")
    if phase == "A":
        W["wq"] = sb("f_wq", [128, 8 * 1024], BF16)
        W["wdn"] = sb("f_wdn", [128, 8 * 64], BF16)
        W["wupk"] = sb("f_wupk", [64, 16 * 64], BF16)
        W["wupv"] = sb("f_wupv", [65, 2048], BF16)
        W["wo"] = sb("f_wo", [128, 8 * 1024], BF16)
        W["wg"] = sb("f_wg", [128, 64], BF16)
        W["g1"] = sb("f_g1", [128, 8], F32)
        W["g2"] = sb("f_g2", [128, 8], F32)
        W["bq"] = sb("f_bq", [128, 8], F32)
        W["bo"] = sb("f_bo", [128, 8], F32)
        W["bdn"] = sb("f_bdn", [64, 1], F32)
        W["bupk"] = sb("f_bupk", [64, 16], F32)
        W["bg"] = sb("f_bg", [8, 1], F32)
        W["bgr"] = sb("f_bgr", [128, 8], F32)
        W["eye"] = sb("f_eye", [128, 128], F32)
        W["mask"] = sb("f_mask", [128, 512], F32)
        W["sel"] = sb("f_sel", [32, 32 * 128], BF16)
        W["onesc"] = sb("f_onesc", [128, 1], BF16)
        W["ones1f"] = sb("f_ones1f", [1, 128], F32)
        for k, src in [("wq", "w_q"), ("wdn", "w_down"), ("wupk", "w_upk"),
                       ("wupv", "w_upv"), ("wo", "w_o"), ("wg", "w_g"), ("sel", "c_sel")]:
            P.dma(W[k][:], dram[src], s_wg, eng="gpsimd")
        for k, src in [("g1", "v_g1"), ("g2", "v_g2"), ("bq", "v_bq"), ("bo", "v_bo"),
                       ("bdn", "v_bdown"), ("bupk", "v_bupk"),
                       ("bg", "v_bg"), ("bgr", "c_bgr"), ("eye", "c_eye"), ("mask", "c_mask")]:
            P.dma(W[k][:], dram[src], s_w, eng="sync")
        P.vector.run(lambda e, W=W: e.memset(W["onesc"][:], 1.0))
        P.vector.run(lambda e, W=W: e.memset(W["ones1f"][:], 1.0), inc=P.sem("s_ve"))
    else:
        W["we"] = sb("f_we", [128, 8 * 8 * 1024], BF16)
        W["sel"] = sb("f_sel", [32, 32 * 128], BF16)
        P.dma(W["we"][:], dram["w_e"], s_wg, eng="gpsimd")
        P.dma(W["sel"][:], dram["c_sel"], s_wg, eng="gpsimd")
    for eng in ("vector", "scalar", "tensor"):
        P.plans[eng].wait(s_w)
        P.plans[eng].wait(s_wg)
    return W


def mask_const():
    m = np.full((128, 512), NEG_BIG, np.float32)
    for row in range(128):
        j = (row // 32) % 2
        for q in range(8):
            base = q * 64 + j * 32
            m[row, base:base + 16] = 0.0
    return m


def emit_ffnA(P, st, W, in_dram, s_out_dram, h2_out_dram, gate_out_dram, L=4096, T=256, dbg=None, do_att=True, do_gate=True):
    nc = P.nc
    n_t = L // T
    NP = T // 2
    NG = NP // 32
    sb = lambda n, s, d: st.enter_context(nc.sbuf_tensor(f"{n}_ph{P._phase_idx}", s, d))
    rt = sb("f_rt", [128, 8 * T], F32)
    LN0 = sb("f_ln0", [128, 8 * T], F32)
    X2 = sb("f_x2", [128, 8 * T], F32)
    h1 = sb("f_h1", [128, 8 * T], BF16)
    H2 = sb("f_h2", [128, 8 * T], BF16)
    rb = H2
    qzz = sb("f_qzz", [128, 64 * NP], BF16)
    kzz = sb("f_kzz", [128, 64 * NP], BF16)
    vzz = sb("f_vzz", [64, 32 * 128], BF16)
    lat = sb("f_lat", [65, T], BF16)
    Ohp = sb("f_ohp", [128, 8 * T], BF16)
    Asb = sb("f_asb", [128, 512], F32)
    Abf = sb("f_abf", [128, 512], BF16)
    At = sb("f_at", [128, 512], BF16)
    mx = sb("f_mx", [128, 8], F32)
    sm = sb("f_sm", [128, 8], F32)
    rc = sb("f_rc", [128, 8], F32)
    gt = sb("f_gt", [8, T], F32)
    gsb = sb("f_gsb", [128, 8 * (T // 128)], F32)
    bq8 = sb("f_bq8", [128, 8], F32)
    mu_r = sb("f_mu", [1, T], F32)
    ms_r = sb("f_ms", [1, T], F32)
    rs_r = sb("f_rs", [1, T], F32)
    ps = st.enter_context(nc.psum_tensor(f"f_ps_ph{P._phase_idx}", [128, 4096], F32))
    if dbg is not None:
        dbg_sraw = sb("dbg_sraw", [64, 4 * 512], F32)
        dbg_abf = sb("dbg_abf", [64, 4 * 512], F32)
        dbg_at = sb("dbg_at", [64, 4 * 512], F32)
        dbg_vzz = sb("dbg_vzz", [64, 4096], BF16)

    s_ld = P.sem("s_ld")
    s_st = P.sem("s_st")
    s_te = P.sem("s_te")
    s_ve = P.sem("s_ve")
    s_se = P.sem("s_se")
    s_gp = P.sem("s_gp")

    bank_free = {}

    def te_bank(b):
        for sem, cnt in bank_free.pop(b, []):
            P.tensor.wait(sem, cnt)

    def consumed(b, sem):
        bank_free.setdefault(b, []).append((sem, sem.n))

    def B(b, rows=128, cols=512, co=0):
        return ps[0:rows, b * 512 + co: b * 512 + co + cols]

    in_v = in_dram.rearrange("(c p) t -> p c t", p=128)
    s_v = s_out_dram.rearrange("(c p) t -> p c t", p=128)
    h2_v = h2_out_dram.rearrange("(c p) t -> p c t", p=128)
    rtv = rt[:].rearrange("p (c t) -> p c t", c=8)
    h2v = H2[:].rearrange("p (c t) -> p c t", c=8)

    P.vector.run(lambda e: e.memset(qzz[:], 0.0))
    P.vector.run(lambda e: e.memset(kzz[:], 0.0))
    P.vector.run(lambda e: e.memset(vzz[:], 0.0))
    P.vector.run(lambda e: e.memset(lat[64:65, :], 1.0))
    P.vector.run(lambda e: e.tensor_scalar_mul(bq8[:], W["bq"][:], 0.125), inc=s_ve)

    def ln_stats(src_bf, eps_name, out_mu=True):
        P.tensor.wait(s_ve)
        te_bank(0)
        te_bank(1)
        for c in range(8):
            P.tensor.run(lambda e, c=c: e.matmul(
                B(0, 1, T), W["onesc"][:], src_bf[:, c * T:(c + 1) * T],
                start=(c == 0), stop=(c == 7)), inc=(s_te if c == 7 else None))
        P.scalar.wait(s_te)
        P.scalar.run(lambda e: e.activation(src_bf[:], src_bf[:], AF.Square), inc=s_se)
        P.tensor.wait(s_se)
        for c in range(8):
            P.tensor.run(lambda e, c=c: e.matmul(
                B(1, 1, T), W["onesc"][:], src_bf[:, c * T:(c + 1) * T],
                start=(c == 0), stop=(c == 7)), inc=(s_te if c == 7 else None))
        P.scalar.wait(s_te)
        if out_mu:
            P.scalar.run(lambda e: e.activation(mu_r[:], B(0, 1, T), AF.Copy, scale=P.cc("r1024", 1)))
            P.scalar.run(lambda e: e.activation(ms_r[:], B(1, 1, T), AF.Copy, scale=P.cc("r1024", 1)))
            P.scalar.run(lambda e: e.activation(rs_r[:], mu_r[:], AF.Square), inc=s_se)
            consumed(0, s_se)
            consumed(1, s_se)
            P.vector.wait(s_se)
            P.vector.run(lambda e: e.tensor_sub(ms_r[:], ms_r[:], rs_r[:]), inc=s_ve)
            P.scalar.wait(s_ve)
        else:
            P.scalar.run(lambda e: e.activation(ms_r[:], B(1, 1, T), AF.Copy, scale=P.cc("r1024", 1)), inc=s_se)
            consumed(0, s_se)
            consumed(1, s_se)
        P.scalar.run(lambda e, eps=eps_name: e.activation(ms_r[:], ms_r[:], AF.Ln, bias=P.cc(eps, 1)))
        P.scalar.run(lambda e: e.activation(rs_r[:], ms_r[:], AF.Exp, scale=P.cc("negh", 1)), inc=s_se)
        P.tensor.wait(s_se)
        te_bank(0)
        te_bank(1)
        if out_mu:
            P.tensor.run(lambda e: e.matmul(B(0, cols=T), W["ones1f"][:], mu_r[:], start=True, stop=True))
        P.tensor.run(lambda e: e.matmul(B(1, cols=T), W["ones1f"][:], rs_r[:], start=True, stop=True), inc=s_te)
        P.vector.wait(s_te)

    for i in range(n_t):
        t0 = i * T
        P.sync.wait(s_ve)
        P.sync.wait(s_st)
        P.dma(rtv, in_v[:, :, t0:t0 + T], s_ld)
        P.vector.wait(s_ld)
        P.vector.wait(s_te)
        P.vector.wait(s_st)
        P.vector.run(lambda e: e.tensor_copy(rb[:], rt[:]), inc=s_ve)
        ln_stats(rb, "eps5", out_mu=True)
        mub = B(0, cols=T).rearrange("p (o t) -> p o t", o=1).broadcast_to([128, 8, T])
        rsb = B(1, cols=T).rearrange("p (o t) -> p o t", o=1).broadcast_to([128, 8, T])
        l3 = LN0[:].rearrange("p (c t) -> p c t", c=8)
        r3 = rt[:].rearrange("p (c t) -> p c t", c=8)
        P.vector.run(lambda e, l3=l3, r3=r3, mub=mub: e.tensor_sub(l3, r3, mub))
        P.vector.run(lambda e, l3=l3, rsb=rsb: e.tensor_mul(l3, l3, rsb), inc=s_ve)
        consumed(0, s_ve)
        consumed(1, s_ve)
        P.vector.run(lambda e: e.tensor_copy(rb[:], LN0[:]), inc=s_ve)
        ln_stats(rb, "eps6", out_mu=False)
        for c in range(8):
            cs = slice(c * T, (c + 1) * T)
            P.vector.run(lambda e, cs=cs, c=c: e.scalar_tensor_tensor(
                h1[:, cs], LN0[:, cs], W["g1"][:, c:c + 1], B(1, cols=T),
                ALU.mult, ALU.mult), inc=s_ve)
        consumed(1, s_ve)
        P.tensor.wait(s_ve)
        for hp in range(8):
            b = 2 + hp % 2
            te_bank(b)
            for kc in range(8):
                P.tensor.run(lambda e, b=b, hp=hp, kc=kc: e.matmul(
                    B(b, cols=T), W["wq"][:, kc * 1024 + hp * 128: kc * 1024 + (hp + 1) * 128],
                    h1[:, kc * T:(kc + 1) * T],
                    start=(kc == 0), stop=(kc == 7)), inc=(s_te if kc == 7 else None))
            P.vector.wait(s_te)
            P.scalar.wait(s_te)
            for hh in range(2):
                h = 2 * hp + hh
                for j in range(2):
                    src = B(b, cols=T).rearrange("p (pp two) -> p pp two", two=2)[hh * 64:(hh + 1) * 64, :, j]
                    dst = qzz[:].rearrange("p (pp cc) -> p pp cc", cc=64)[j * 64:(j + 1) * 64, :, j * 32 + h]
                    if SCE_EVAC and (hh + j) % 2 == 1:
                        P.scalar.run(lambda e, src=src, dst=dst, hh=hh, hp=hp: e.activation(
                            dst, src, AF.Identity, bias=bq8[hh * 64:(hh + 1) * 64, hp:hp + 1],
                            scale=P.cc("q8", 64)), inc=s_se)
                    else:
                        P.vector.run(lambda e, src=src, dst=dst, hh=hh, hp=hp: e.tensor_scalar(
                            dst, src, W["bq"][hh * 64:(hh + 1) * 64, hp:hp + 1],
                            P.cc("q8", 64), ALU.add, ALU.mult), inc=s_ve)
            consumed(b, s_ve)
            consumed(b, s_se)
        te_bank(4)
        P.tensor.wait(s_ve)
        for kc in range(8):
            P.tensor.run(lambda e, kc=kc: e.matmul(
                B(4, 64, T), W["wdn"][:, kc * 64:(kc + 1) * 64],
                h1[:, kc * T:(kc + 1) * T],
                start=(kc == 0), stop=(kc == 7)), inc=(s_te if kc == 7 else None))
        P.vector.wait(s_te)
        P.vector.run(lambda e: e.tensor_scalar(
            lat[0:64, :], B(4, 64, T), W["bdn"][:, 0:1], None, ALU.add), inc=s_ve)
        consumed(4, s_ve)
        P.tensor.wait(s_ve)
        for g in range(16):
            b = 2 + g % 2
            te_bank(b)
            P.tensor.run(lambda e, b=b, g=g: e.matmul(
                B(b, 64, T), W["wupk"][:, g * 64:(g + 1) * 64], lat[0:64, :],
                start=True, stop=True), inc=s_te)
            P.vector.wait(s_te)
            P.scalar.wait(s_te)
            for j in range(2):
                src = B(b, 64, T).rearrange("p (pp two) -> p pp two", two=2)[:, :, j]
                dst = kzz[:].rearrange("p (pp cc) -> p pp cc", cc=64)[j * 64:(j + 1) * 64, :, j * 32 + g]
                if SCE_EVAC and j == 1:
                    P.scalar.run(lambda e, src=src, dst=dst, g=g: e.activation(
                        dst, src, AF.Identity, bias=W["bupk"][:, g:g + 1]), inc=s_se)
                else:
                    P.vector.run(lambda e, src=src, dst=dst, g=g: e.tensor_scalar(
                        dst, src, W["bupk"][:, g:g + 1], None, ALU.add), inc=s_ve)
            consumed(b, s_ve)
            consumed(b, s_se)
        if not do_att:
            P.vector.run(lambda e: e.memset(Ohp[:], 0.0), inc=s_ve)
        for g0 in range(NG if do_att else 0):
            tok0 = g0 * 64
            te_bank(4)
            te_bank(5)
            P.tensor.wait(s_ve)
            P.tensor.wait(s_se)
            for dq in range(16):
                P.tensor.run(lambda e, dq=dq, tok0=tok0: e.matmul(
                    ps[:, 4 * 512 + dq * 64: 4 * 512 + (dq + 1) * 64],
                    W["wupv"][:, dq * 128:(dq + 1) * 128],
                    lat[:, tok0:tok0 + 64],
                    start=True, stop=True), inc=(s_te if dq == 15 else None))
            P.vector.wait(s_te)
            P.scalar.wait(s_te)
            for m in range(4):
                for j in range(2):
                    src = ps[32 * m:32 * m + 16, 4 * 512: 4 * 512 + 1024].rearrange(
                        "p (dh q pp two) -> p dh q pp two", dh=8, q=2, two=2)[:, :, :, :, j]
                    dst = vzz[j * 32:j * 32 + 16, :].rearrange(
                        "p (pp jj dh q mm) -> p dh q pp jj mm", jj=2, dh=8, q=2, mm=4)[:, :, :, :, j, m]
                    if SCE_EVAC and (m + j) % 2 == 1:
                        P.scalar.run(lambda e, src=src, dst=dst: e.copy(dst, src), inc=s_se)
                    else:
                        P.vector.run(lambda e, src=src, dst=dst: e.tensor_copy(dst, src), inc=s_ve)
            consumed(4, s_ve)
            consumed(4, s_se)
            consumed(5, s_ve)
            consumed(5, s_se)
            if dbg is not None and i == 0 and g0 == 0:
                P.vector.wait(s_se)
                P.vector.run(lambda e: e.tensor_copy(dbg_vzz[:], vzz[:]), inc=s_ve)
            for b in range(4):
                te_bank(b)
            P.tensor.wait(s_ve)
            P.tensor.wait(s_se)
            for p in range(32):
                pa = tok0 // 2 + p
                P.tensor.run(lambda e, p=p, pa=pa: e.matmul(
                    ps[0:64, (p // 8) * 512 + (p % 8) * 64: (p // 8) * 512 + (p % 8) * 64 + 64],
                    qzz[:, pa * 64:(pa + 1) * 64], kzz[:, pa * 64:(pa + 1) * 64],
                    start=True, stop=True), inc=(s_te if p == 31 else None))
            for sub in range(4):
                P.vector.wait(s_te)
                if dbg is not None and i == 0 and sub == 0:
                    P.vector.run(lambda e, g0=g0: e.tensor_copy(
                        dbg_sraw[:, g0 * 512:(g0 + 1) * 512], ps[0:64, 0:512]), inc=s_ve)
                P.vector.run(lambda e, sub=sub: e.tensor_add(
                    Asb[0:64, :], ps[0:64, sub * 512:(sub + 1) * 512],
                    W["mask"][0:64, :]), inc=s_ve)
                consumed(sub, s_ve)
                v3 = Asb[0:64, :].rearrange("p (q c) -> p q c", c=64)
                P.vector.run(lambda e, v3=v3: e.reduce_max(mx[0:64, :], v3, AX.X), inc=s_ve)
                P.vector.wait(s_ve)
                mxb = mx[0:64, :].rearrange("p (q o) -> p q o", o=1).broadcast_to([64, 8, 64])
                P.vector.run(lambda e, v3=v3, mxb=mxb: e.tensor_sub(v3, v3, mxb), inc=s_ve)
                P.scalar.wait(s_ve)
                P.scalar.run(lambda e: e.activation(Abf[0:64, :], Asb[0:64, :], AF.Exp), inc=s_se)
                P.vector.wait(s_se)
                a3 = Abf[0:64, :].rearrange("p (q c) -> p q c", c=64)
                P.vector.run(lambda e, a3=a3: e.reduce_sum(sm[0:64, :], a3, AX.X), inc=s_ve)
                P.vector.wait(s_ve)
                P.vector.run(lambda e: e.reciprocal(rc[0:64, :], sm[0:64, :]), inc=s_ve)
                P.vector.wait(s_ve)
                rcb = rc[0:64, :].rearrange("p (q o) -> p q o", o=1).broadcast_to([64, 8, 64])
                P.vector.run(lambda e, a3=a3, rcb=rcb: e.tensor_mul(a3, a3, rcb))
                P.vector.run(lambda e: e.transpose(At[0:64, :], Abf[0:64, :]), inc=s_ve)
                if dbg is not None and i == 0 and sub == 0:
                    P.vector.run(lambda e, g0=g0: e.tensor_copy(
                        dbg_abf[:, g0 * 512:(g0 + 1) * 512], Abf[0:64, :]))
                    P.vector.run(lambda e, g0=g0: e.tensor_copy(
                        dbg_at[:, g0 * 512:(g0 + 1) * 512], At[0:64, :]), inc=s_ve)
                bb = 4 + sub
                te_bank(bb)
                P.tensor.wait(s_ve)
                for qp in range(8):
                    p = sub * 8 + qp
                    P.tensor.run(lambda e, p=p, qp=qp, bb=bb: e.matmul(
                        ps[:, bb * 512 + qp * 64: bb * 512 + qp * 64 + 64],
                        vzz[:, p * 128:(p + 1) * 128],
                        At[0:64, qp * 64:(qp + 1) * 64],
                        start=True, stop=True), inc=(s_te if qp == 7 else None))
            P.vector.wait(s_te)
            P.scalar.wait(s_te)
            for h in range(16):
                for jm in range(2):
                    src = ps[jm * 64:(jm + 1) * 64, 4 * 512: 4 * 512 + 2048].rearrange(
                        "p (pp cc) -> p pp cc", cc=64)[:, :, jm * 32 + h]
                    hp = h // 2
                    dst = Ohp[(h % 2) * 64:(h % 2) * 64 + 64,
                              hp * T + tok0: hp * T + tok0 + 64].rearrange(
                        "p (pp two) -> p pp two", two=2)[:, :, jm]
                    if SCE_EVAC and (h + jm) % 2 == 1:
                        P.scalar.run(lambda e, src=src, dst=dst: e.copy(dst, src), inc=s_se)
                    else:
                        P.vector.run(lambda e, src=src, dst=dst: e.tensor_copy(dst, src), inc=s_ve)
            for b in range(4, 8):
                consumed(b, s_ve)
                consumed(b, s_se)
        P.tensor.wait(s_ve)
        P.tensor.wait(s_se)
        for mc in range(8):
            b = 2 + mc % 2
            te_bank(b)
            for kc in range(8):
                P.tensor.run(lambda e, b=b, mc=mc, kc=kc: e.matmul(
                    B(b, cols=T), W["wo"][:, kc * 1024 + mc * 128: kc * 1024 + (mc + 1) * 128],
                    Ohp[:, kc * T:(kc + 1) * T],
                    start=(kc == 0), stop=(kc == 7)), inc=(s_te if kc == 7 else None))
            P.vector.wait(s_te)
            cs = slice(mc * T, (mc + 1) * T)
            P.vector.run(lambda e, b=b, cs=cs, mc=mc: e.scalar_tensor_tensor(
                X2[:, cs], B(b, cols=T), W["bo"][:, mc:mc + 1], LN0[:, cs],
                ALU.add, ALU.add), inc=s_ve)
            consumed(b, s_ve)
        P.vector.run(lambda e: e.tensor_copy(rb[:], X2[:]), inc=s_ve)
        ln_stats(rb, "eps6", out_mu=False)
        for c in range(8):
            cs = slice(c * T, (c + 1) * T)
            P.vector.run(lambda e, cs=cs, c=c: e.scalar_tensor_tensor(
                H2[:, cs], X2[:, cs], W["g2"][:, c:c + 1], B(1, cols=T),
                ALU.mult, ALU.mult), inc=s_ve)
        consumed(1, s_ve)
        if not do_gate:
            P.vector.run(lambda e: e.memset(gt[:], 0.125), inc=s_ve)
        te_bank(2)
        te_bank(3)
        P.tensor.wait(s_ve)
        for tb in range(T // 128 if do_gate else 0):
            for kc in range(8):
                P.tensor.run(lambda e, tb=tb, kc=kc: e.matmul(
                    B(2, 128, 8, co=tb * 8),
                    H2[:, kc * T + tb * 128: kc * T + tb * 128 + 128],
                    W["wg"][:, kc * 8:(kc + 1) * 8],
                    start=(kc == 0), stop=(kc == 7)),
                    inc=(s_te if kc == 7 else None))
        P.vector.wait(s_te)
        for tb in range(T // 128 if do_gate else 0):
            P.vector.run(lambda e, tb=tb: e.tensor_add(
                gsb[:, tb * 8:(tb + 1) * 8], B(2, 128, 8, co=tb * 8),
                W["bgr"][0:1, :].broadcast_to([128, 8]) if False else W["bgr"][:, :]), inc=s_ve)
        consumed(2, s_ve)
        for tb in range(T // 128 if do_gate else 0):
            gs = gsb[:, tb * 8:(tb + 1) * 8]
            g3 = gs.rearrange("p (q c) -> p q c", c=8)
            P.vector.run(lambda e, gs=gs, g3=g3: e.reduce_max(mx[:, 0:1], g3, AX.X), inc=s_ve)
            P.vector.wait(s_ve)
            P.vector.run(lambda e, gs=gs: e.tensor_sub(
                gs, gs, mx[:, 0:1].broadcast_to([128, 8])), inc=s_ve)
            P.scalar.wait(s_ve)
            P.scalar.run(lambda e, gs=gs: e.activation(gs, gs, AF.Exp), inc=s_se)
            P.vector.wait(s_se)
            P.vector.run(lambda e, gs=gs: e.reduce_sum(
                sm[:, 0:1], gs.rearrange("p (q c) -> p q c", c=8), AX.X), inc=s_ve)
            P.vector.wait(s_ve)
            P.vector.run(lambda e: e.reciprocal(rc[:, 0:1], sm[:, 0:1]), inc=s_ve)
            P.vector.wait(s_ve)
            P.vector.run(lambda e, gs=gs: e.tensor_mul(
                gs, gs, rc[:, 0:1].broadcast_to([128, 8])), inc=s_ve)
        if do_gate:
            P.tensor.wait(s_ve)
            for tb in range(T // 128):
                P.tensor.run(lambda e, tb=tb: e.transpose(
                    B(3, 8, 128, co=tb * 128), gsb[:, tb * 8:(tb + 1) * 8], W["eye"][:]),
                    inc=(s_te if tb == (T // 128) - 1 else None))
            P.vector.wait(s_te)
            for tb in range(T // 128):
                P.vector.run(lambda e, tb=tb: e.tensor_copy(
                    gt[:, tb * 128:(tb + 1) * 128], B(3, 8, 128, co=tb * 128)), inc=s_ve)
            consumed(3, s_ve)
        for c in range(8):
            cs = slice(c * T, (c + 1) * T)
            P.vector.run(lambda e, cs=cs: e.scalar_tensor_tensor(
                rt[:, cs], X2[:, cs], P.cc("half"), rt[:, cs],
                ALU.mult, ALU.add), inc=s_ve)
        P.sync.wait(s_ve)
        P.sync.wait(s_se)
        P.dma(s_v[:, :, t0:t0 + T], rtv, s_st)
        P.dma(h2_v[:, :, t0:t0 + T], h2v, s_st)
        P.dma(gate_out_dram[:, t0:t0 + T], gt[:], s_st)
    if dbg is not None:
        P.sync.wait(s_ve)
        P.dma(dbg["sraw"], dbg_sraw[:], s_st)
        P.dma(dbg["abf"], dbg_abf[:], s_st)
        P.dma(dbg["at"], dbg_at[:], s_st)
        P.plans["gpsimd"].wait(s_ve)
        P.dma(dbg["vzz"], dbg_vzz[:], P.sem("s_stg2"), eng="gpsimd")
        P.sync.wait(P.sem("s_stg2"))
    P.sync.wait(s_st)


def emit_ffnB(P, st, W, s_in_dram, h2_in_dram, gate_in_dram, out_dram, L=4096, T=256, dbg=None):
    nc = P.nc
    n_t = L // T
    sb = lambda n, s, d: st.enter_context(nc.sbuf_tensor(f"{n}_ph{P._phase_idx}", s, d))
    stt = sb("b_st", [128, 8 * T], F32)
    H2 = sb("b_h2", [128, 8 * T], BF16)
    HS = sb("b_hs", [128, 8 * T], BF16)
    gt = sb("b_gt", [8, T], F32)
    gtb = sb("b_gtb", [32, T], BF16)
    gb8 = sb("b_gb8", [128, 8 * T], BF16)
    ps = st.enter_context(nc.psum_tensor(f"b_ps_ph{P._phase_idx}", [128, 4096], F32))
    if dbg is not None:
        dbg_gb = sb("dbg_gb", [128, 8 * T], F32)
        dbg_hs = sb("dbg_hs", [128, 8 * T], F32)
    s_ld = P.sem("s_ld")
    s_st = P.sem("s_st")
    s_te = P.sem("s_te")
    s_ve = P.sem("s_ve")
    bank_free = {}

    def te_bank(b):
        for sem, cnt in bank_free.pop(b, []):
            P.tensor.wait(sem, cnt)

    def consumed(b, sem):
        bank_free.setdefault(b, []).append((sem, sem.n))

    s_v = s_in_dram.rearrange("(c p) t -> p c t", p=128)
    h2_v = h2_in_dram.rearrange("(c p) t -> p c t", p=128)
    o_v = out_dram.rearrange("(c p) t -> p c t", p=128)
    stv = stt[:].rearrange("p (c t) -> p c t", c=8)
    h2v = H2[:].rearrange("p (c t) -> p c t", c=8)

    P.vector.run(lambda e: e.memset(gtb[:], 0.0), inc=s_ve)
    for i in range(n_t):
        t0 = i * T
        P.sync.wait(s_ve)
        P.sync.wait(s_st)
        P.dma(stv, s_v[:, :, t0:t0 + T], s_ld)
        P.dma(h2v, h2_v[:, :, t0:t0 + T], s_ld)
        P.dma(gt[:], gate_in_dram[:, t0:t0 + T], s_ld)
        P.vector.wait(s_ld)
        P.vector.run(lambda e: e.tensor_copy(gtb[0:8, :], gt[:]), inc=s_ve)
        P.tensor.wait(s_ve)
        te_bank(0)
        te_bank(1)
        for e_ in range(8):
            bco = (e_ % 2) * 512
            if e_ >= 2:
                P.tensor.wait(s_ve)
            P.tensor.run(lambda e, e_=e_, bco=bco: e.matmul(
                ps[:, bco: bco + T],
                W["sel"][0:32, e_ * 128:(e_ + 1) * 128], gtb[:],
                start=True, stop=True), inc=s_te)
            P.vector.wait(s_te)
            P.vector.run(lambda e, e_=e_, bco=bco: e.tensor_copy(
                gb8[:, e_ * T:(e_ + 1) * T], ps[:, bco: bco + T]), inc=s_ve)
        consumed(0, s_ve)
        consumed(1, s_ve)
        for e_ in range(8):
            if e_ > 0:
                P.vector.wait(s_te)
            gbv = gb8[:, 0:8 * T].rearrange("p (o t) -> p o t", o=8)[:, 0:1, :].broadcast_to([128, 8, T]) if False else None
            gview = gb8[:, :].rearrange("p (ee t) -> p ee t", ee=8)[:, e_:e_ + 1, :].broadcast_to([128, 8, T])
            P.vector.run(lambda e, e_=e_, gview=gview: e.tensor_mul(
                HS[:].rearrange("p (c t) -> p c t", c=8), H2[:].rearrange("p (c t) -> p c t", c=8),
                gview), inc=s_ve)
            P.tensor.wait(s_ve)
            for mc in range(8):
                b = mc
                if e_ == 0:
                    te_bank(b)
                for kc in range(8):
                    sh = 256 if T == 256 else 0
                    P.tensor.run(lambda e, b=b, mc=mc, kc=kc, e_=e_, sh=sh: e.matmul(
                        ps[:, b * 512 + sh: b * 512 + sh + T] if b <= 1 else ps[:, b * 512: b * 512 + T],
                        W["we"][:, e_ * 8192 + kc * 1024 + mc * 128: e_ * 8192 + kc * 1024 + (mc + 1) * 128],
                        HS[:, kc * T:(kc + 1) * T],
                        start=(e_ == 0 and kc == 0), stop=(e_ == 7 and kc == 7)),
                        inc=(s_te if (kc == 7 and mc == 7) else None))
        P.vector.wait(s_te)
        if dbg is not None and i == n_t - 1:
            P.vector.run(lambda e: e.tensor_copy(dbg_gb[:], gb8[:]))
            P.vector.run(lambda e: e.tensor_copy(dbg_hs[:], HS[:]), inc=s_ve)
        for mc in range(8):
            b = mc
            cs = slice(mc * T, (mc + 1) * T)
            sh = 256 if T == 256 else 0
            src = ps[:, b * 512 + sh: b * 512 + sh + T] if b <= 1 else ps[:, b * 512: b * 512 + T]
            P.vector.run(lambda e, src=src, cs=cs: e.scalar_tensor_tensor(
                stt[:, cs], src, P.cc("half"), stt[:, cs],
                ALU.mult, ALU.add), inc=s_ve)
        for b in range(8):
            consumed(b, s_ve)
        P.sync.wait(s_ve)
        P.dma(o_v[:, :, t0:t0 + T], stv, s_st)
    if dbg is not None:
        P.sync.wait(s_ve)
        P.dma(dbg["gb8"], dbg_gb[:], s_st)
        P.dma(dbg["hs"], dbg_hs[:], s_st)
    P.sync.wait(s_st)



from contextlib import ExitStack

CONSTS = {"eps5": 1e-5, "eps6": 1e-6, "negh": -0.5, "neg1": -1.0,
          "one": 1.0, "r1024": 1.0 / 1024.0, "q8": 0.125, "half": 0.5}


def build_program(L=4096, T_f=512, T_m=256):
    nc = bass.Bass("TRN2", target_bir_lowering=False, detect_race_conditions=False)
    dram = {}
    for k, (shape, dt) in INPUT_SPECS.items():
        dram[k] = nc.dram_tensor(k, list(shape), dt, kind="ExternalInput").ap()
    out = nc.dram_tensor("out", [1024, L], F32, kind="ExternalOutput").ap()
    s_scr = nc.dram_tensor("s_scr", [1024, L], F32, kind="Internal").ap()
    h2_scr = nc.dram_tensor("h2_scr", [1024, L], BF16, kind="Internal").ap()
    g_scr = nc.dram_tensor("g_scr", [8, L], F32, kind="Internal").ap()
    r1_scr = nc.dram_tensor("r1_scr", [1024, L], F32, kind="Internal").ap()
    r2_scr = nc.dram_tensor("r2_scr", [1024, L], F32, kind="Internal").ap()
    with ExitStack() as big:
        P = Prog(nc)
        P._stack = big
        P.setup_consts(CONSTS)
        with ExitStack() as st:
            W = emit_load_ffn_weights(P, st, dram, "A")
            emit_ffnA(P, st, W, dram["xT"], s_scr, h2_scr, g_scr, L=L, T=T_f)
            P.flush_phase()
        with ExitStack() as st:
            W = emit_load_ffn_weights(P, st, dram, "B")
            emit_ffnB(P, st, W, s_scr, h2_scr, g_scr, r1_scr, L=L, T=T_f)
            P.flush_phase()
        with ExitStack() as st:
            W = emit_load_mamba_weights(P, st, dram)
            emit_mamba(P, st, W, r1_scr, r2_scr, dram["c_sel"], L=L, T=T_m)
            P.flush_phase()
        with ExitStack() as st:
            W = emit_load_ffn_weights(P, st, dram, "A")
            emit_ffnA(P, st, W, r2_scr, s_scr, h2_scr, g_scr, L=L, T=T_f)
            P.flush_phase()
        with ExitStack() as st:
            W = emit_load_ffn_weights(P, st, dram, "B")
            emit_ffnB(P, st, W, s_scr, h2_scr, g_scr, out, L=L, T=T_f)
            P.flush_phase()
    return nc


def kernel(**inputs):
    common = host_layout(inputs, 0)
    del common["xT"]
    nc = build_program()
    in_maps = []
    for c in range(NCORES):
        m = dict(common)
        m["xT"] = np.ascontiguousarray(inputs["L0_attn_x"][c].T)
        in_maps.append(m)
    res = run_bass_kernel_spmd(nc, in_maps, core_ids=list(range(NCORES)))
    return np.stack([np.ascontiguousarray(r["out"].T) for r in res.results], axis=0)


# version 19 (final)
# speedup vs baseline: 1.0623x; 1.0071x over previous
import numpy as np
import ml_dtypes
import concourse.bass as bass
import concourse.mybir as mybir
from concourse.bass_utils import run_bass_kernel_spmd
from concourse.bass_isa import ReduceOp
RED_MAX = ReduceOp.max if hasattr(ReduceOp, "max") else "max"
RED_ADD = ReduceOp.add if hasattr(ReduceOp, "add") else "add" 

F32 = mybir.dt.float32
BF16 = mybir.dt.bfloat16
AF = mybir.ActivationFunctionType
ALU = mybir.AluOpType
AX = mybir.AxisListType

D_MODEL = 1024
D_INNER = 2048
D_STATE = 16
D_CONV = 4
DT_RANK = 64
NCORES = 8

LN_EPS_OUTER = 1e-5
LN_EPS_INNER = 1e-6
NEG_BIG = -30000.0
import os
SCE_EVAC = os.environ.get("SCE_EVAC", "0") == "1"



class Sem:
    __slots__ = ("h", "n")

    def __init__(self, h):
        self.h = h
        self.n = 0


class EngPlan:
    def __init__(self, name):
        self.name = name
        self.steps = []
        self.waited = {}

    def wait(self, sem, val=None):
        v = sem.n if val is None else val
        if self.waited.get(id(sem), -1) >= v:
            return
        self.waited[id(sem)] = v
        h = sem.h
        self.steps.append(lambda e, h=h, v=v: e.wait_ge(h, v))

    def run(self, fn, inc=None, n=1):
        if inc is not None:
            inc.n += n
            h = inc.h
            self.steps.append(lambda e, fn=fn, h=h, n=n: fn(e).then_inc(h, n))
        else:
            self.steps.append(fn)


class Prog:
    ENGS = ("sync", "vector", "scalar", "tensor", "gpsimd")

    def __init__(self, nc):
        self.nc = nc
        self.sems = {}
        self.plans = {k: EngPlan(k) for k in self.ENGS}
        self.sync = self.plans["sync"]
        self.vector = self.plans["vector"]
        self.scalar = self.plans["scalar"]
        self.tensor = self.plans["tensor"]
        self.gpsimd = self.plans["gpsimd"]
        self._phase_idx = 0
        self._stack = None

    def sem(self, name):
        if name not in self.sems:
            h = self._stack.enter_context(self.nc.semaphore(name))
            self.sems[name] = Sem(h)
        return self.sems[name]

    def setup_consts(self, values):
        n = len(values)
        t = self._stack.enter_context(self.nc.sbuf_tensor("cconsts", [128, max(n, 1)], F32))
        self._consts = t
        self._cmap = {}
        for i, (k, v) in enumerate(values.items()):
            self._cmap[k] = i
            self.vector.run(lambda e, i=i, v=v: e.memset(t[:, i:i + 1], v))
        s = self.sem("s_const")
        self.vector.run(lambda e: e.engine_nop(), inc=s)
        for k in ("scalar", "vector", "gpsimd", "tensor"):
            self.plans[k].wait(s)

    def tile_barrier(self):
        bar = self.sem("tilebar")
        for k in ("vector", "scalar", "tensor"):
            self.plans[k].run(lambda e: e.drain(), inc=bar, n=1)
        target = bar.n
        for k in ("vector", "scalar", "tensor", "sync", "gpsimd"):
            self.plans[k].wait(bar, target)

    def cc(self, name, rows=128):
        i = self._cmap[name]
        return self._consts[0:rows, i:i + 1]

    def dma(self, out, in_, sem, eng="sync"):
        self.plans[eng].run(lambda e, o=out, i=in_: e.dma_start(out=o, in_=i), inc=sem, n=16)

    def flush_phase(self):
        bar = self.sem("phasebar")
        for k in ("vector", "scalar", "tensor", "gpsimd"):
            self.plans[k].run(lambda e: e.drain(), inc=bar, n=1)
        self.plans["sync"].run(lambda e: e.sem_inc(bar.h, 1))
        bar.n += 1
        nc = self.nc
        with nc.Block() as blk:
            for k in self.ENGS:
                steps = self.plans[k].steps
                getattr(blk, k)(lambda e, steps=steps: [f(e) for f in steps])
        self._phase_idx += 1
        target = bar.n
        for k in self.ENGS:
            self.plans[k] = EngPlan(k)
        self.sync = self.plans["sync"]
        self.vector = self.plans["vector"]
        self.scalar = self.plans["scalar"]
        self.tensor = self.plans["tensor"]
        self.gpsimd = self.plans["gpsimd"]
        for k in self.ENGS:
            self.plans[k].wait(self.sems["phasebar"], target)



def _img_chunked(w, rows=128):
    K, M = w.shape
    return np.ascontiguousarray(w.reshape(K // rows, rows, M).transpose(1, 0, 2).reshape(rows, -1))


def sel_const():
    sel = np.zeros((32, 32 * 128), np.float32)
    for n in range(32):
        sel[n, n * 128:(n + 1) * 128] = 1.0
    return sel


def upv_img(wup, bup):
    out = np.zeros((65, 2048), np.float32)
    for dh in range(8):
        for q in range(2):
            for m in range(4):
                dd = dh * 8 + q * 4 + m
                col0 = dh * 256 + q * 128 + m * 32
                out[0:64, col0:col0 + 16] = wup[:, 16 * 0:16 * 16:128][:, :] if False else 0
                for g in range(16):
                    out[0:64, col0 + g] = wup[:, g * 128 + 64 + dd]
                    out[64, col0 + g] = bup[g * 128 + 64 + dd]
    return np.ascontiguousarray(out)


def host_layout(inp, core):
    f = np.ascontiguousarray
    d = {}
    d["xT"] = f(inp["L0_attn_x"][core].T)
    d["w_inproj"] = _img_chunked(f(inp["L0_attn_in_proj_w"].T))
    d["w_outproj"] = _img_chunked(f(inp["L0_attn_out_proj_w"].T))
    d["w_xproj"] = _img_chunked(f(inp["L0_attn_x_proj_w"].T))
    d["w_dtproj"] = f(inp["L0_attn_dt_proj_w"].T)
    d["v_convw"] = f(inp["L0_attn_conv_w"][:, 0, :].reshape(16, 128, 4).transpose(1, 0, 2).reshape(128, 64))
    d["v_convb"] = f(inp["L0_attn_conv_b"].reshape(16, 128).T)
    d["v_D"] = f(inp["L0_attn_D"].reshape(16, 128).T)
    d["v_dtb"] = f(inp["L0_attn_dt_proj_b"].reshape(16, 128).T)
    d["v_Alog"] = f(inp["L0_attn_A_log"].reshape(16, 128, 16).transpose(1, 0, 2).reshape(128, 256))
    d["v_lng"] = f(inp["L0_attn_ln_gamma"].reshape(8, 128).T)
    d["v_lnb"] = f(inp["L0_attn_ln_beta"].reshape(8, 128).T)
    d["w_q"] = _img_chunked(f(inp["L0_ffn_Wq"]))
    d["w_down"] = _img_chunked(f(inp["L0_ffn_Wdown"]))
    wup = inp["L0_ffn_Wup"]
    d["w_upk"] = f(wup.reshape(64, 16, 128)[:, :, 0:64].reshape(64, 16 * 64))
    d["w_upv"] = upv_img(wup, inp["L0_ffn_bup"])
    d["w_o"] = _img_chunked(f(inp["L0_ffn_Wo"]))
    d["w_g"] = _img_chunked(f(inp["L0_ffn_Wg"]))
    d["w_e"] = f(inp["L0_ffn_We"].reshape(8, 8, 128, 1024).transpose(2, 0, 1, 3).reshape(128, 8 * 8 * 1024))
    d["v_g1"] = f(inp["L0_ffn_gamma1"].reshape(8, 128).T)
    d["v_g2"] = f(inp["L0_ffn_gamma2"].reshape(8, 128).T)
    d["v_bq"] = f(inp["L0_ffn_bq"].reshape(8, 128).T)
    d["v_bo"] = f(inp["L0_ffn_bo"].reshape(8, 128).T)
    d["v_bdown"] = f(inp["L0_ffn_bdown"].reshape(64, 1))
    bup = inp["L0_ffn_bup"]
    d["v_bupk"] = f(bup.reshape(16, 128)[:, 0:64].T)
    d["v_bg"] = f(inp["L0_ffn_bg"].reshape(8, 1))
    d["c_sel"] = sel_const()
    d["c_mask"] = mask_const()
    d["c_eye"] = np.eye(128, dtype=np.float32)
    d["c_bgr"] = f(np.broadcast_to(inp["L0_ffn_bg"][None, :], (128, 8)).copy())
    return d


INPUT_SPECS = {
    "xT": ([1024, 4096], F32),
    "w_inproj": ([128, 8 * 4096], F32),
    "w_outproj": ([128, 16 * 1024], F32),
    "w_xproj": ([128, 16 * 96], F32),
    "w_dtproj": ([64, 2048], F32),
    "v_convw": ([128, 64], F32),
    "v_convb": ([128, 16], F32),
    "v_D": ([128, 16], F32),
    "v_dtb": ([128, 16], F32),
    "v_Alog": ([128, 256], F32),
    "v_lng": ([128, 8], F32),
    "v_lnb": ([128, 8], F32),
    "w_q": ([128, 8 * 1024], F32),
    "w_down": ([128, 8 * 64], F32),
    "w_upk": ([64, 16 * 64], F32),
    "w_upv": ([65, 2048], F32),
    "w_o": ([128, 8 * 1024], F32),
    "w_g": ([128, 64], F32),
    "w_e": ([128, 8 * 8 * 1024], F32),
    "v_g1": ([128, 8], F32),
    "v_g2": ([128, 8], F32),
    "v_bq": ([128, 8], F32),
    "v_bo": ([128, 8], F32),
    "v_bdown": ([64, 1], F32),
    "v_bupk": ([64, 16], F32),
    "v_bg": ([8, 1], F32),
    "c_sel": ([32, 32 * 128], F32),
    "c_mask": ([128, 512], F32),
    "c_eye": ([128, 128], F32),
    "c_bgr": ([128, 8], F32),
}



def emit_load_mamba_weights(P, st, dram):
    nc = P.nc
    sb = lambda n, s, d: st.enter_context(nc.sbuf_tensor(f"{n}_ph{P._phase_idx}", s, d))
    W = {}
    W["inW"] = sb("m_inW", [128, 8 * 4096], BF16)
    W["opW"] = sb("m_opW", [128, 16 * 1024], BF16)
    W["xpW"] = sb("m_xpW", [128, 16 * 96], BF16)
    W["dtW"] = sb("m_dtW", [64, 2048], BF16)
    W["cw"] = sb("m_cw", [128, 64], F32)
    W["cb"] = sb("m_cb", [128, 16], F32)
    W["Dv"] = sb("m_Dv", [128, 16], F32)
    W["dtb"] = sb("m_dtb", [128, 16], F32)
    W["An"] = sb("m_An", [128, 256], F32)
    W["lng"] = sb("m_lng", [128, 8], F32)
    W["lnb"] = sb("m_lnb", [128, 8], F32)
    W["onesc"] = sb("m_onesc", [128, 1], BF16)
    W["ones1"] = sb("m_ones1", [1, 128], BF16)
    W["ones1f"] = sb("m_ones1f", [1, 128], F32)
    s_w = P.sem("s_w")
    s_wg = P.sem("s_wg")
    P.dma(W["inW"][:], dram["w_inproj"], s_wg, eng="gpsimd")
    P.dma(W["opW"][:], dram["w_outproj"], s_wg, eng="gpsimd")
    P.dma(W["xpW"][:], dram["w_xproj"], s_wg, eng="gpsimd")
    P.dma(W["dtW"][:], dram["w_dtproj"], s_wg, eng="gpsimd")
    for k, src in [("cw", "v_convw"), ("cb", "v_convb"), ("Dv", "v_D"),
                   ("dtb", "v_dtb"), ("An", "v_Alog"), ("lng", "v_lng"), ("lnb", "v_lnb")]:
        P.dma(W[k][:], dram[src], s_w, eng="sync")
    P.vector.run(lambda e, W=W: e.memset(W["onesc"][:], 1.0))
    P.vector.run(lambda e, W=W: e.memset(W["ones1"][:], 1.0))
    P.vector.run(lambda e, W=W: e.memset(W["ones1f"][:], 1.0))
    P.scalar.wait(s_w)
    P.scalar.wait(s_wg)
    s_se = P.sem("s_se")
    P.scalar.run(lambda e, W=W: e.activation(W["An"][:], W["An"][:], AF.Exp), inc=s_se)
    P.vector.wait(s_se)
    s_ve = P.sem("s_ve")
    P.vector.run(lambda e, W=W: e.tensor_scalar_mul(W["An"][:], W["An"][:], -1.0), inc=s_ve)
    return W



def emit_mamba(P, st, W, in_dram, out_dram, in_sel, L=4096, T=256, dbg=None):
    nc = P.nc
    n_t = L // T
    NC = 16
    sb = lambda n, s, d: st.enter_context(nc.sbuf_tensor(f"{n}_ph{P._phase_idx}", s, d))
    rt = sb("m_rt", [128, 8 * T], F32)
    LN0 = sb("m_ln0", [128, 8 * T], F32)
    hm = sb("m_hm", [128, 8 * T], BF16)
    rb = hm
    UC = sb("m_UC", [128, NC * T], BF16)
    ZS = sb("m_ZS", [128, NC * T], BF16)
    Bb = sb("m_Bb", [128, NC * T], BF16)
    Cb = sb("m_Cb", [128, NC * T], BF16)
    DELTA = sb("m_DELTA", [128, NC * T], F32)
    SEL = sb("m_SEL", [32, 32 * 128], BF16)
    dtb_sb = sb("m_dtbsb", [64, T], BF16)
    bcrow = sb("m_bcrow", [32, T], BF16)
    ue = sb("m_ue", [128, T + 4], BF16)
    utail = sb("m_utail", [128, NC * 3], BF16)
    cacc = sb("m_cacc", [128, T], F32)
    accv = sb("m_accv", [128, T], F32)
    sp_e = sb("m_spe", [128, T], F32)
    sp_r = sb("m_spr", [128, T], F32)
    sg = sb("m_sg", [128, T], BF16)
    xt = sb("m_xt", [128, T], BF16)
    dab = [sb(f"m_dab{i}", [128, T], F32) for i in range(2)]
    h32 = [sb(f"m_h32{i}", [128, T], F32) for i in range(2)]
    DBU = sb("m_DBU", [128, 16 * T], BF16)
    yh = sb("m_yh", [128, T], BF16)
    y2 = sb("m_y2", [128, T], BF16)
    STATE = sb("m_state", [128, 256], F32)
    mu_r = sb("m_mu", [1, T], F32)
    ms_r = sb("m_ms", [1, T], F32)
    rs_r = sb("m_rs", [1, T], F32)
    ps = [st.enter_context(nc.psum_tensor(f"m_ps{i}_ph{P._phase_idx}", [128, 512], F32)) for i in range(8)]

    s_ld = P.sem("s_ld")
    s_st = P.sem("s_st")
    s_te = P.sem("s_te")
    s_ve = P.sem("s_ve")
    s_se = P.sem("s_se")
    s_w = P.sem("s_w")
    s_gp = P.sem("s_gp")

    bank_free = {}

    def te_bank(b):
        if b in bank_free:
            sem, cnt = bank_free[b]
            P.tensor.wait(sem, cnt)

    def consumed(b, sem):
        bank_free[b] = (sem, sem.n)

    in_v = in_dram.rearrange("(c p) t -> p c t", p=128)
    out_v = out_dram.rearrange("(c p) t -> p c t", p=128)
    rtv = rt[:].rearrange("p (c t) -> p c t", c=8)

    s_wg2 = P.sem("s_wg2")
    P.dma(SEL[:], in_sel, s_wg2, eng="gpsimd")
    for eng in ("vector", "scalar", "tensor"):
        P.plans[eng].wait(s_wg2)
    P.vector.run(lambda e: e.memset(STATE[:], 0.0))
    P.vector.run(lambda e: e.memset(utail[:], 0.0), inc=s_ve)
    for eng in ("vector", "scalar", "tensor"):
        P.plans[eng].wait(s_w)
        P.plans[eng].wait(P.sem("s_wg"))

    for i in range(n_t):
        t0 = i * T
        P.sync.wait(s_ve)
        P.sync.wait(s_st)
        P.dma(rtv, in_v[:, :, t0:t0 + T], s_ld)
        P.vector.wait(s_ld)
        P.vector.wait(s_te)
        P.vector.run(lambda e: e.tensor_copy(rb[:], rt[:]), inc=s_ve)
        P.tensor.wait(s_ve)
        te_bank(0)
        for c in range(8):
            P.tensor.run(lambda e, c=c: e.matmul(
                ps[0][0:1, 0:T], W["onesc"][:], rb[:, c * T:(c + 1) * T],
                start=(c == 0), stop=(c == 7)), inc=(s_te if c == 7 else None))
        P.scalar.wait(s_te)
        P.scalar.run(lambda e: e.activation(rb[:], rb[:], AF.Square), inc=s_se)
        P.tensor.wait(s_se)
        te_bank(1)
        for c in range(8):
            P.tensor.run(lambda e, c=c: e.matmul(
                ps[1][0:1, 0:T], W["onesc"][:], rb[:, c * T:(c + 1) * T],
                start=(c == 0), stop=(c == 7)), inc=(s_te if c == 7 else None))
        P.scalar.wait(s_te)
        P.scalar.run(lambda e: e.activation(mu_r[:], ps[0][0:1, 0:T], AF.Copy, scale=P.cc("r1024", 1)))
        P.scalar.run(lambda e: e.activation(ms_r[:], ps[1][0:1, 0:T], AF.Copy, scale=P.cc("r1024", 1)))
        P.scalar.run(lambda e: e.activation(rs_r[:], mu_r[:], AF.Square), inc=s_se)
        consumed(0, s_se)
        consumed(1, s_se)
        P.vector.wait(s_se)
        P.vector.run(lambda e: e.tensor_sub(ms_r[:], ms_r[:], rs_r[:]), inc=s_ve)
        P.scalar.wait(s_ve)
        P.scalar.run(lambda e: e.activation(ms_r[:], ms_r[:], AF.Ln, bias=P.cc("eps5", 1)))
        P.scalar.run(lambda e: e.activation(rs_r[:], ms_r[:], AF.Exp, scale=P.cc("negh", 1)), inc=s_se)
        P.tensor.wait(s_se)
        te_bank(0)
        te_bank(1)
        P.tensor.run(lambda e: e.matmul(ps[0][:, 0:T], W["ones1f"][:], mu_r[:], start=True, stop=True))
        P.tensor.run(lambda e: e.matmul(ps[1][:, 0:T], W["ones1f"][:], rs_r[:], start=True, stop=True), inc=s_te)
        P.vector.wait(s_te)
        mub = ps[0][:, 0:T].rearrange("p (o t) -> p o t", o=1).broadcast_to([128, 8, T])
        rsb = ps[1][:, 0:T].rearrange("p (o t) -> p o t", o=1).broadcast_to([128, 8, T])
        l3 = LN0[:].rearrange("p (c t) -> p c t", c=8)
        r3 = rt[:].rearrange("p (c t) -> p c t", c=8)
        P.vector.run(lambda e, l3=l3, r3=r3, mub=mub: e.tensor_sub(l3, r3, mub))
        P.vector.run(lambda e, l3=l3, rsb=rsb: e.tensor_mul(l3, l3, rsb))
        P.vector.run(lambda e, l3=l3, r3=r3: e.tensor_add(r3, r3, l3), inc=s_ve)
        consumed(0, s_ve)
        consumed(1, s_ve)
        P.vector.run(lambda e: e.tensor_copy(rb[:], LN0[:]), inc=s_ve)
        P.tensor.wait(s_ve)
        te_bank(0)
        for c in range(8):
            P.tensor.run(lambda e, c=c: e.matmul(
                ps[0][0:1, 0:T], W["onesc"][:], rb[:, c * T:(c + 1) * T],
                start=(c == 0), stop=(c == 7)), inc=(s_te if c == 7 else None))
        P.scalar.wait(s_te)
        P.scalar.run(lambda e: e.activation(rb[:], rb[:], AF.Square), inc=s_se)
        P.tensor.wait(s_se)
        te_bank(1)
        for c in range(8):
            P.tensor.run(lambda e, c=c: e.matmul(
                ps[1][0:1, 0:T], W["onesc"][:], rb[:, c * T:(c + 1) * T],
                start=(c == 0), stop=(c == 7)), inc=(s_te if c == 7 else None))
        P.scalar.wait(s_te)
        P.scalar.run(lambda e: e.activation(mu_r[:], ps[0][0:1, 0:T], AF.Copy, scale=P.cc("r1024", 1)))
        P.scalar.run(lambda e: e.activation(ms_r[:], ps[1][0:1, 0:T], AF.Copy, scale=P.cc("r1024", 1)))
        P.scalar.run(lambda e: e.activation(rs_r[:], mu_r[:], AF.Square), inc=s_se)
        consumed(0, s_se)
        consumed(1, s_se)
        P.vector.wait(s_se)
        P.vector.run(lambda e: e.tensor_sub(ms_r[:], ms_r[:], rs_r[:]), inc=s_ve)
        P.scalar.wait(s_ve)
        P.scalar.run(lambda e: e.activation(ms_r[:], ms_r[:], AF.Ln, bias=P.cc("eps6", 1)))
        P.scalar.run(lambda e: e.activation(rs_r[:], ms_r[:], AF.Exp, scale=P.cc("negh", 1)), inc=s_se)
        P.tensor.wait(s_se)
        te_bank(0)
        te_bank(1)
        P.tensor.run(lambda e: e.matmul(ps[0][:, 0:T], W["ones1f"][:], mu_r[:], start=True, stop=True))
        P.tensor.run(lambda e: e.matmul(ps[1][:, 0:T], W["ones1f"][:], rs_r[:], start=True, stop=True), inc=s_te)
        P.vector.wait(s_te)
        for c in range(8):
            cs = slice(c * T, (c + 1) * T)
            P.vector.run(lambda e, cs=cs: e.tensor_sub(accv[:], LN0[:, cs], ps[0][:, 0:T]))
            P.vector.run(lambda e: e.tensor_mul(accv[:], accv[:], ps[1][:, 0:T]))
            P.vector.run(lambda e, cs=cs, c=c: e.tensor_scalar(
                hm[:, cs], accv[:], W["lng"][:, c:c + 1], W["lnb"][:, c:c + 1],
                ALU.mult, ALU.add), inc=s_ve)
        consumed(0, s_ve)
        consumed(1, s_ve)
        P.tensor.wait(s_ve)
        te_bank(2)
        for mc in range(32):
            b = 3 + (mc % 2) * 4
            te_bank(b)
            for kc in range(8):
                P.tensor.run(lambda e, b=b, kc=kc, mc=mc: e.matmul(
                    ps[b][:, 0:T],
                    W["inW"][:, kc * 4096 + mc * 128: kc * 4096 + (mc + 1) * 128],
                    hm[:, kc * T:(kc + 1) * T],
                    start=(kc == 0), stop=(kc == 7)), inc=(s_te if kc == 7 else None))
            if mc < 16:
                P.vector.wait(s_te)
                P.vector.run(lambda e, mc=mc: e.tensor_copy(ue[:, 0:3], utail[:, mc * 3:mc * 3 + 3]))
                P.vector.run(lambda e, b=b: e.tensor_copy(ue[:, 3:3 + T], ps[b][:, 0:T]), inc=s_ve)
                consumed(b, s_ve)
                P.vector.run(lambda e, mc=mc: e.tensor_scalar(
                    cacc[:], ue[:, 0:T], W["cw"][:, mc * 4:mc * 4 + 1],
                    W["cb"][:, mc:mc + 1], ALU.mult, ALU.add))
                for k in range(1, 4):
                    P.vector.run(lambda e, k=k, mc=mc: e.scalar_tensor_tensor(
                        cacc[:], ue[:, k:k + T], W["cw"][:, mc * 4 + k:mc * 4 + k + 1], cacc[:],
                        ALU.mult, ALU.add), inc=(s_ve if k == 3 else None))
                P.vector.run(lambda e, mc=mc: e.tensor_copy(
                    utail[:, mc * 3:mc * 3 + 3], ue[:, T:T + 3]))
                P.scalar.wait(s_ve)
                P.scalar.run(lambda e, mc=mc: e.activation(
                    sg[:], cacc[:], AF.Sigmoid), inc=s_se)
                P.vector.wait(s_se)
                P.vector.run(lambda e, mc=mc: e.tensor_mul(
                    UC[:, mc * T:(mc + 1) * T], cacc[:], sg[:]), inc=s_ve)
                P.tensor.wait(s_ve)
                P.tensor.run(lambda e, mc=mc: e.matmul(
                    ps[2][0:96, 0:T], W["xpW"][:, mc * 96:(mc + 1) * 96],
                    UC[:, mc * T:(mc + 1) * T],
                    start=(mc == 0), stop=(mc == 15)), inc=(s_te if mc == 15 else None))
            else:
                zc = mc - 16
                P.scalar.wait(s_te)
                P.scalar.run(lambda e, b=b: e.activation(sg[:], ps[b][:, 0:T], AF.Sigmoid), inc=s_se)
                P.vector.wait(s_se)
                P.vector.run(lambda e, b=b, zc=zc: e.tensor_mul(
                    ZS[:, zc * T:(zc + 1) * T], ps[b][:, 0:T], sg[:]), inc=s_ve)
                consumed(b, s_ve)
        P.vector.wait(s_te)
        P.vector.run(lambda e: e.tensor_copy(dtb_sb[:], ps[2][0:64, 0:T]))
        P.vector.run(lambda e: e.tensor_copy(bcrow[:], ps[2][64:96, 0:T]), inc=s_ve)
        consumed(2, s_ve)
        P.tensor.wait(s_ve)
        for c in range(NC):
            bd = 2 + (c % 2)
            te_bank(bd)
            P.tensor.run(lambda e, b=bd, c=c: e.matmul(
                ps[b][:, 0:T], W["dtW"][:, c * 128:(c + 1) * 128], dtb_sb[:],
                start=True, stop=True), inc=s_te)
            dt_mark = s_te.n
            for n in (2 * c, 2 * c + 1):
                b = n % 2
                te_bank(b)
                P.tensor.run(lambda e, b=b, n=n: e.matmul(
                    ps[b][:, 0:T], SEL[:, n * 128:(n + 1) * 128], bcrow[:, 0:T],
                    start=True, stop=True), inc=s_te)
                P.vector.wait(s_te)
                dst = Bb if n < 16 else Cb
                nn = n % 16
                P.vector.run(lambda e, b=b, dst=dst, nn=nn: e.tensor_copy(
                    dst[:, nn * T:(nn + 1) * T], ps[b][:, 0:T]), inc=s_ve)
                consumed(b, s_ve)
            P.scalar.wait(s_te, dt_mark)
            P.scalar.wait(s_ve)
            P.scalar.run(lambda e, b=bd, c=c: e.activation(
                sp_e[:], ps[b][:, 0:T], AF.Abs, bias=W["dtb"][:, c:c + 1]))
            P.scalar.run(lambda e: e.activation(sp_e[:], sp_e[:], AF.Exp, scale=P.cc("neg1")))
            P.scalar.run(lambda e: e.activation(sp_e[:], sp_e[:], AF.Ln, bias=P.cc("one")))
            P.scalar.run(lambda e, b=bd, c=c: e.activation(
                sp_r[:], ps[b][:, 0:T], AF.Relu, bias=W["dtb"][:, c:c + 1]), inc=s_se)
            consumed(bd, s_se)
            P.vector.wait(s_se)
            P.vector.run(lambda e, c=c: e.tensor_add(
                DELTA[:, c * T:(c + 1) * T], sp_e[:], sp_r[:]), inc=s_ve)
        for b in range(8):
            te_bank(b)
        scan_ve_marks = {}
        for c in range(NC):
            ds = slice(c * T, (c + 1) * T)
            P.vector.run(lambda e, ds=ds, c=c: e.tensor_mul(xt[:], DELTA[:, ds], UC[:, ds]), inc=s_ve)
            P.scalar.wait(s_ve)
            xtv = xt[:].rearrange("p (o t) -> p o t", o=1).broadcast_to([128, 16, T])
            P.vector.run(lambda e, xtv=xtv: e.tensor_mul(
                DBU[:].rearrange("p (n t) -> p n t", n=16), xtv,
                Bb[:].rearrange("p (n t) -> p n t", n=16)))
            for n in range(16):
                pn = n % 2
                col = c * 16 + n
                if col >= 2:
                    P.scalar.wait(s_ve, scan_ve_marks[col - 2])
                    P.scalar.run(lambda e, pn=pn, col=col: e.activation(
                        STATE[:, col - 2:col - 1], h32[pn][:, T - 1:T],
                        AF.Copy, scale=P.cc("one")))
                P.scalar.run(lambda e, pn=pn, col=col, ds=ds: e.activation(
                    dab[pn][:], DELTA[:, ds], AF.Exp, scale=W["An"][:, col:col + 1]), inc=s_se)
                P.vector.wait(s_se)
                P.vector.run(lambda e, pn=pn, col=col, n=n: e.tensor_tensor_scan(
                    h32[pn][:], dab[pn][:], DBU[:, n * T:(n + 1) * T],
                    STATE[:, col:col + 1], ALU.mult, ALU.add))
                P.vector.run(lambda e, pn=pn, n=n: e.tensor_mul(
                    yh[:], h32[pn][:], Cb[:, n * T:(n + 1) * T]))
                if n == 0:
                    P.vector.run(lambda e: e.tensor_copy(accv[:], yh[:]), inc=s_ve)
                else:
                    P.vector.run(lambda e: e.tensor_add(accv[:], accv[:], yh[:]), inc=s_ve)
                scan_ve_marks[col] = s_ve.n
            if c > 0:
                P.vector.wait(s_te)
            P.vector.run(lambda e, c=c: e.scalar_tensor_tensor(
                accv[:], UC[:, c * T:(c + 1) * T], W["Dv"][:, c:c + 1], accv[:],
                ALU.mult, ALU.add))
            P.vector.run(lambda e, c=c: e.tensor_mul(y2[:], accv[:], ZS[:, c * T:(c + 1) * T]), inc=s_ve)
            P.tensor.wait(s_ve)
            for mc in range(8):
                P.tensor.run(lambda e, b=mc, mc=mc, c=c: e.matmul(
                    ps[b][:, 0:T],
                    W["opW"][:, c * 1024 + mc * 128: c * 1024 + (mc + 1) * 128],
                    y2[:],
                    start=(c == 0), stop=(c == NC - 1)),
                    inc=(s_te if mc == 7 else None))
        for col in (NC * 16 - 2, NC * 16 - 1):
            P.scalar.wait(s_ve, scan_ve_marks[col])
            P.scalar.run(lambda e, col=col: e.activation(
                STATE[:, col:col + 1], h32[col % 2][:, T - 1:T],
                AF.Copy, scale=P.cc("one")), inc=(s_se if col == NC * 16 - 1 else None))
        P.vector.wait(s_te)
        for mc in range(8):
            P.vector.run(lambda e, b=mc, mc=mc: e.tensor_add(
                rt[:, mc * T:(mc + 1) * T], rt[:, mc * T:(mc + 1) * T], ps[b][:, 0:T]),
                inc=s_ve)
        for b in range(8):
            consumed(b, s_ve)
        P.sync.wait(s_ve)
        P.dma(out_v[:, :, t0:t0 + T], rtv, s_st)
    if dbg is not None:
        P.plans["gpsimd"].wait(s_ve)
        P.plans["gpsimd"].wait(s_se)
        for k, buf in [("UC", UC), ("ZS", ZS), ("Bb", Bb), ("Cb", Cb),
                       ("DELTA", DELTA), ("hm", hm), ("LN0", LN0)]:
            if k in dbg:
                P.dma(dbg[k], buf[:], P.sem("s_stg"), eng="gpsimd")
        P.sync.wait(P.sem("s_stg"))
    P.sync.wait(s_st)



def emit_load_ffn_weights(P, st, dram, phase):
    nc = P.nc
    sb = lambda n, s, d: st.enter_context(nc.sbuf_tensor(f"{n}_ph{P._phase_idx}", s, d))
    W = {}
    s_w = P.sem("s_w")
    s_wg = P.sem(f"s_wgf{P._phase_idx}")
    if phase == "A":
        W["wq"] = sb("f_wq", [128, 8 * 1024], BF16)
        W["wdn"] = sb("f_wdn", [128, 8 * 64], BF16)
        W["wupk"] = sb("f_wupk", [64, 16 * 64], BF16)
        W["wupv"] = sb("f_wupv", [65, 2048], BF16)
        W["wo"] = sb("f_wo", [128, 8 * 1024], BF16)
        W["wg"] = sb("f_wg", [128, 64], BF16)
        W["g1"] = sb("f_g1", [128, 8], F32)
        W["g2"] = sb("f_g2", [128, 8], F32)
        W["bq"] = sb("f_bq", [128, 8], F32)
        W["bo"] = sb("f_bo", [128, 8], F32)
        W["bdn"] = sb("f_bdn", [64, 1], F32)
        W["bupk"] = sb("f_bupk", [64, 16], F32)
        W["bg"] = sb("f_bg", [8, 1], F32)
        W["bgr"] = sb("f_bgr", [128, 8], F32)
        W["eye"] = sb("f_eye", [128, 128], F32)
        W["mask"] = sb("f_mask", [128, 512], F32)
        W["sel"] = sb("f_sel", [32, 32 * 128], BF16)
        W["onesc"] = sb("f_onesc", [128, 1], BF16)
        W["ones1f"] = sb("f_ones1f", [1, 128], F32)
        for k, src in [("wq", "w_q"), ("wdn", "w_down"), ("wupk", "w_upk"),
                       ("wupv", "w_upv"), ("wo", "w_o"), ("wg", "w_g"), ("sel", "c_sel")]:
            P.dma(W[k][:], dram[src], s_wg, eng="gpsimd")
        for k, src in [("g1", "v_g1"), ("g2", "v_g2"), ("bq", "v_bq"), ("bo", "v_bo"),
                       ("bdn", "v_bdown"), ("bupk", "v_bupk"),
                       ("bg", "v_bg"), ("bgr", "c_bgr"), ("eye", "c_eye"), ("mask", "c_mask")]:
            P.dma(W[k][:], dram[src], s_w, eng="sync")
        P.vector.run(lambda e, W=W: e.memset(W["onesc"][:], 1.0))
        P.vector.run(lambda e, W=W: e.memset(W["ones1f"][:], 1.0), inc=P.sem("s_ve"))
    else:
        W["we"] = sb("f_we", [128, 8 * 8 * 1024], BF16)
        W["sel"] = sb("f_sel", [32, 32 * 128], BF16)
        P.dma(W["we"][:], dram["w_e"], s_wg, eng="gpsimd")
        P.dma(W["sel"][:], dram["c_sel"], s_wg, eng="gpsimd")
    for eng in ("vector", "scalar", "tensor"):
        P.plans[eng].wait(s_w)
        P.plans[eng].wait(s_wg)
    return W


def mask_const():
    m = np.full((128, 512), NEG_BIG, np.float32)
    for row in range(128):
        j = (row // 32) % 2
        for q in range(8):
            base = q * 64 + j * 32
            m[row, base:base + 16] = 0.0
    return m


def emit_ffnA(P, st, W, in_dram, s_out_dram, h2_out_dram, gate_out_dram, L=4096, T=256, dbg=None, do_att=True, do_gate=True):
    nc = P.nc
    n_t = L // T
    NP = T // 2
    NG = NP // 32
    sb = lambda n, s, d: st.enter_context(nc.sbuf_tensor(f"{n}_ph{P._phase_idx}", s, d))
    rt = sb("f_rt", [128, 8 * T], F32)
    LN0 = sb("f_ln0", [128, 8 * T], F32)
    X2 = sb("f_x2", [128, 8 * T], F32)
    h1 = sb("f_h1", [128, 8 * T], BF16)
    H2 = sb("f_h2", [128, 8 * T], BF16)
    rb = H2
    qzz = sb("f_qzz", [128, 64 * NP], BF16)
    kzz = sb("f_kzz", [128, 64 * NP], BF16)
    vzz = sb("f_vzz", [64, 32 * 128], BF16)
    lat = sb("f_lat", [65, T], BF16)
    Ohp = sb("f_ohp", [128, 8 * T], BF16)
    Asb = sb("f_asb", [128, 512], F32)
    Abf = sb("f_abf", [128, 512], BF16)
    At = sb("f_at", [128, 512], BF16)
    mx = sb("f_mx", [128, 8], F32)
    sm = sb("f_sm", [128, 8], F32)
    rc = sb("f_rc", [128, 8], F32)
    gt = sb("f_gt", [8, T], F32)
    gsb = sb("f_gsb", [128, 8 * (T // 128)], F32)
    bq8 = sb("f_bq8", [128, 8], F32)
    mu_r = sb("f_mu", [1, T], F32)
    ms_r = sb("f_ms", [1, T], F32)
    rs_r = sb("f_rs", [1, T], F32)
    ps = st.enter_context(nc.psum_tensor(f"f_ps_ph{P._phase_idx}", [128, 4096], F32))
    if dbg is not None:
        dbg_sraw = sb("dbg_sraw", [64, 4 * 512], F32)
        dbg_abf = sb("dbg_abf", [64, 4 * 512], F32)
        dbg_at = sb("dbg_at", [64, 4 * 512], F32)
        dbg_vzz = sb("dbg_vzz", [64, 4096], BF16)

    s_ld = P.sem("s_ld")
    s_st = P.sem("s_st")
    s_te = P.sem("s_te")
    s_ve = P.sem("s_ve")
    s_se = P.sem("s_se")
    s_gp = P.sem("s_gp")

    bank_free = {}

    def te_bank(b):
        for sem, cnt in bank_free.pop(b, []):
            P.tensor.wait(sem, cnt)

    def consumed(b, sem):
        bank_free.setdefault(b, []).append((sem, sem.n))

    def B(b, rows=128, cols=512, co=0):
        return ps[0:rows, b * 512 + co: b * 512 + co + cols]

    in_v = in_dram.rearrange("(c p) t -> p c t", p=128)
    s_v = s_out_dram.rearrange("(c p) t -> p c t", p=128)
    h2_v = h2_out_dram.rearrange("(c p) t -> p c t", p=128)
    rtv = rt[:].rearrange("p (c t) -> p c t", c=8)
    h2v = H2[:].rearrange("p (c t) -> p c t", c=8)

    P.vector.run(lambda e: e.memset(qzz[:], 0.0))
    P.vector.run(lambda e: e.memset(kzz[:], 0.0))
    P.vector.run(lambda e: e.memset(vzz[:], 0.0))
    P.vector.run(lambda e: e.memset(lat[64:65, :], 1.0))
    P.vector.run(lambda e: e.tensor_scalar_mul(bq8[:], W["bq"][:], 0.125), inc=s_ve)

    def ln_stats(src_bf, eps_name, out_mu=True):
        P.tensor.wait(s_ve)
        te_bank(0)
        te_bank(1)
        for c in range(8):
            P.tensor.run(lambda e, c=c: e.matmul(
                B(0, 1, T), W["onesc"][:], src_bf[:, c * T:(c + 1) * T],
                start=(c == 0), stop=(c == 7)), inc=(s_te if c == 7 else None))
        P.scalar.wait(s_te)
        P.scalar.run(lambda e: e.activation(src_bf[:], src_bf[:], AF.Square), inc=s_se)
        P.tensor.wait(s_se)
        for c in range(8):
            P.tensor.run(lambda e, c=c: e.matmul(
                B(1, 1, T), W["onesc"][:], src_bf[:, c * T:(c + 1) * T],
                start=(c == 0), stop=(c == 7)), inc=(s_te if c == 7 else None))
        P.scalar.wait(s_te)
        if out_mu:
            P.scalar.run(lambda e: e.activation(mu_r[:], B(0, 1, T), AF.Copy, scale=P.cc("r1024", 1)))
            P.scalar.run(lambda e: e.activation(ms_r[:], B(1, 1, T), AF.Copy, scale=P.cc("r1024", 1)))
            P.scalar.run(lambda e: e.activation(rs_r[:], mu_r[:], AF.Square), inc=s_se)
            consumed(0, s_se)
            consumed(1, s_se)
            P.vector.wait(s_se)
            P.vector.run(lambda e: e.tensor_sub(ms_r[:], ms_r[:], rs_r[:]), inc=s_ve)
            P.scalar.wait(s_ve)
        else:
            P.scalar.run(lambda e: e.activation(ms_r[:], B(1, 1, T), AF.Copy, scale=P.cc("r1024", 1)), inc=s_se)
            consumed(0, s_se)
            consumed(1, s_se)
        P.scalar.run(lambda e, eps=eps_name: e.activation(ms_r[:], ms_r[:], AF.Ln, bias=P.cc(eps, 1)))
        P.scalar.run(lambda e: e.activation(rs_r[:], ms_r[:], AF.Exp, scale=P.cc("negh", 1)), inc=s_se)
        P.tensor.wait(s_se)
        te_bank(0)
        te_bank(1)
        if out_mu:
            P.tensor.run(lambda e: e.matmul(B(0, cols=T), W["ones1f"][:], mu_r[:], start=True, stop=True))
        P.tensor.run(lambda e: e.matmul(B(1, cols=T), W["ones1f"][:], rs_r[:], start=True, stop=True), inc=s_te)
        P.vector.wait(s_te)

    for i in range(n_t):
        t0 = i * T
        P.sync.wait(s_ve)
        P.sync.wait(s_st)
        P.dma(rtv, in_v[:, :, t0:t0 + T], s_ld)
        P.vector.wait(s_ld)
        P.vector.wait(s_te)
        P.vector.wait(s_st)
        P.vector.run(lambda e: e.tensor_copy(rb[:], rt[:]), inc=s_ve)
        ln_stats(rb, "eps5", out_mu=True)
        mub = B(0, cols=T).rearrange("p (o t) -> p o t", o=1).broadcast_to([128, 8, T])
        rsb = B(1, cols=T).rearrange("p (o t) -> p o t", o=1).broadcast_to([128, 8, T])
        l3 = LN0[:].rearrange("p (c t) -> p c t", c=8)
        r3 = rt[:].rearrange("p (c t) -> p c t", c=8)
        P.vector.run(lambda e, l3=l3, r3=r3, mub=mub: e.tensor_sub(l3, r3, mub))
        P.vector.run(lambda e, l3=l3, rsb=rsb: e.tensor_mul(l3, l3, rsb), inc=s_ve)
        consumed(0, s_ve)
        consumed(1, s_ve)
        P.vector.run(lambda e: e.tensor_copy(rb[:], LN0[:]), inc=s_ve)
        ln_stats(rb, "eps6", out_mu=False)
        for c in range(8):
            cs = slice(c * T, (c + 1) * T)
            P.vector.run(lambda e, cs=cs, c=c: e.scalar_tensor_tensor(
                h1[:, cs], LN0[:, cs], W["g1"][:, c:c + 1], B(1, cols=T),
                ALU.mult, ALU.mult), inc=s_ve)
        consumed(1, s_ve)
        P.tensor.wait(s_ve)
        for hp in range(8):
            b = 2 + hp % 2
            te_bank(b)
            for kc in range(8):
                P.tensor.run(lambda e, b=b, hp=hp, kc=kc: e.matmul(
                    B(b, cols=T), W["wq"][:, kc * 1024 + hp * 128: kc * 1024 + (hp + 1) * 128],
                    h1[:, kc * T:(kc + 1) * T],
                    start=(kc == 0), stop=(kc == 7)), inc=(s_te if kc == 7 else None))
            P.vector.wait(s_te)
            P.scalar.wait(s_te)
            for hh in range(2):
                h = 2 * hp + hh
                for j in range(2):
                    src = B(b, cols=T).rearrange("p (pp two) -> p pp two", two=2)[hh * 64:(hh + 1) * 64, :, j]
                    dst = qzz[:].rearrange("p (pp cc) -> p pp cc", cc=64)[j * 64:(j + 1) * 64, :, j * 32 + h]
                    if SCE_EVAC and (hh + j) % 2 == 1:
                        P.scalar.run(lambda e, src=src, dst=dst, hh=hh, hp=hp: e.activation(
                            dst, src, AF.Identity, bias=bq8[hh * 64:(hh + 1) * 64, hp:hp + 1],
                            scale=P.cc("q8", 64)), inc=s_se)
                    else:
                        P.vector.run(lambda e, src=src, dst=dst, hh=hh, hp=hp: e.tensor_scalar(
                            dst, src, W["bq"][hh * 64:(hh + 1) * 64, hp:hp + 1],
                            P.cc("q8", 64), ALU.add, ALU.mult), inc=s_ve)
            consumed(b, s_ve)
            consumed(b, s_se)
        te_bank(4)
        P.tensor.wait(s_ve)
        for kc in range(8):
            P.tensor.run(lambda e, kc=kc: e.matmul(
                B(4, 64, T), W["wdn"][:, kc * 64:(kc + 1) * 64],
                h1[:, kc * T:(kc + 1) * T],
                start=(kc == 0), stop=(kc == 7)), inc=(s_te if kc == 7 else None))
        P.vector.wait(s_te)
        P.vector.run(lambda e: e.tensor_scalar(
            lat[0:64, :], B(4, 64, T), W["bdn"][:, 0:1], None, ALU.add), inc=s_ve)
        consumed(4, s_ve)
        P.tensor.wait(s_ve)
        for g in range(16):
            b = 2 + g % 2
            te_bank(b)
            P.tensor.run(lambda e, b=b, g=g: e.matmul(
                B(b, 64, T), W["wupk"][:, g * 64:(g + 1) * 64], lat[0:64, :],
                start=True, stop=True), inc=s_te)
            P.vector.wait(s_te)
            P.scalar.wait(s_te)
            for j in range(2):
                src = B(b, 64, T).rearrange("p (pp two) -> p pp two", two=2)[:, :, j]
                dst = kzz[:].rearrange("p (pp cc) -> p pp cc", cc=64)[j * 64:(j + 1) * 64, :, j * 32 + g]
                if SCE_EVAC and j == 1:
                    P.scalar.run(lambda e, src=src, dst=dst, g=g: e.activation(
                        dst, src, AF.Identity, bias=W["bupk"][:, g:g + 1]), inc=s_se)
                else:
                    P.vector.run(lambda e, src=src, dst=dst, g=g: e.tensor_scalar(
                        dst, src, W["bupk"][:, g:g + 1], None, ALU.add), inc=s_ve)
            consumed(b, s_ve)
            consumed(b, s_se)
        if not do_att:
            P.vector.run(lambda e: e.memset(Ohp[:], 0.0), inc=s_ve)
        for g0 in range(NG if do_att else 0):
            tok0 = g0 * 64
            te_bank(4)
            te_bank(5)
            P.tensor.wait(s_ve)
            P.tensor.wait(s_se)
            for dq in range(16):
                P.tensor.run(lambda e, dq=dq, tok0=tok0: e.matmul(
                    ps[:, 4 * 512 + dq * 64: 4 * 512 + (dq + 1) * 64],
                    W["wupv"][:, dq * 128:(dq + 1) * 128],
                    lat[:, tok0:tok0 + 64],
                    start=True, stop=True), inc=(s_te if dq == 15 else None))
            P.vector.wait(s_te)
            P.scalar.wait(s_te)
            for m in range(4):
                for j in range(2):
                    src = ps[32 * m:32 * m + 16, 4 * 512: 4 * 512 + 1024].rearrange(
                        "p (dh q pp two) -> p dh q pp two", dh=8, q=2, two=2)[:, :, :, :, j]
                    dst = vzz[j * 32:j * 32 + 16, :].rearrange(
                        "p (pp jj dh q mm) -> p dh q pp jj mm", jj=2, dh=8, q=2, mm=4)[:, :, :, :, j, m]
                    if SCE_EVAC and (m + j) % 2 == 1:
                        P.scalar.run(lambda e, src=src, dst=dst: e.copy(dst, src), inc=s_se)
                    else:
                        P.vector.run(lambda e, src=src, dst=dst: e.tensor_copy(dst, src), inc=s_ve)
            consumed(4, s_ve)
            consumed(4, s_se)
            consumed(5, s_ve)
            consumed(5, s_se)
            if dbg is not None and i == 0 and g0 == 0:
                P.vector.wait(s_se)
                P.vector.run(lambda e: e.tensor_copy(dbg_vzz[:], vzz[:]), inc=s_ve)
            for b in range(4):
                te_bank(b)
            P.tensor.wait(s_ve)
            P.tensor.wait(s_se)
            for p in range(32):
                pa = tok0 // 2 + p
                P.tensor.run(lambda e, p=p, pa=pa: e.matmul(
                    ps[0:64, (p // 8) * 512 + (p % 8) * 64: (p // 8) * 512 + (p % 8) * 64 + 64],
                    qzz[:, pa * 64:(pa + 1) * 64], kzz[:, pa * 64:(pa + 1) * 64],
                    start=True, stop=True), inc=(s_te if p == 31 else None))
            for sub in range(4):
                P.vector.wait(s_te)
                if dbg is not None and i == 0 and sub == 0:
                    P.vector.run(lambda e, g0=g0: e.tensor_copy(
                        dbg_sraw[:, g0 * 512:(g0 + 1) * 512], ps[0:64, 0:512]), inc=s_ve)
                P.vector.run(lambda e, sub=sub: e.tensor_add(
                    Asb[0:64, :], ps[0:64, sub * 512:(sub + 1) * 512],
                    W["mask"][0:64, :]), inc=s_ve)
                consumed(sub, s_ve)
                v3 = Asb[0:64, :].rearrange("p (q c) -> p q c", c=64)
                P.vector.run(lambda e, v3=v3: e.reduce_max(mx[0:64, :], v3, AX.X), inc=s_ve)
                P.vector.wait(s_ve)
                mxb = mx[0:64, :].rearrange("p (q o) -> p q o", o=1).broadcast_to([64, 8, 64])
                P.vector.run(lambda e, v3=v3, mxb=mxb: e.tensor_sub(v3, v3, mxb), inc=s_ve)
                P.scalar.wait(s_ve)
                P.scalar.run(lambda e: e.activation(Abf[0:64, :], Asb[0:64, :], AF.Exp), inc=s_se)
                P.vector.wait(s_se)
                a3 = Abf[0:64, :].rearrange("p (q c) -> p q c", c=64)
                P.vector.run(lambda e, a3=a3: e.reduce_sum(sm[0:64, :], a3, AX.X), inc=s_ve)
                P.vector.wait(s_ve)
                P.vector.run(lambda e: e.reciprocal(rc[0:64, :], sm[0:64, :]), inc=s_ve)
                P.vector.wait(s_ve)
                rcb = rc[0:64, :].rearrange("p (q o) -> p q o", o=1).broadcast_to([64, 8, 64])
                P.vector.run(lambda e, a3=a3, rcb=rcb: e.tensor_mul(a3, a3, rcb))
                P.vector.run(lambda e: e.transpose(At[0:64, :], Abf[0:64, :]), inc=s_ve)
                if dbg is not None and i == 0 and sub == 0:
                    P.vector.run(lambda e, g0=g0: e.tensor_copy(
                        dbg_abf[:, g0 * 512:(g0 + 1) * 512], Abf[0:64, :]))
                    P.vector.run(lambda e, g0=g0: e.tensor_copy(
                        dbg_at[:, g0 * 512:(g0 + 1) * 512], At[0:64, :]), inc=s_ve)
                bb = 4 + sub
                te_bank(bb)
                P.tensor.wait(s_ve)
                for qp in range(8):
                    p = sub * 8 + qp
                    P.tensor.run(lambda e, p=p, qp=qp, bb=bb: e.matmul(
                        ps[:, bb * 512 + qp * 64: bb * 512 + qp * 64 + 64],
                        vzz[:, p * 128:(p + 1) * 128],
                        At[0:64, qp * 64:(qp + 1) * 64],
                        start=True, stop=True), inc=(s_te if qp == 7 else None))
            P.vector.wait(s_te)
            P.scalar.wait(s_te)
            for h in range(16):
                for jm in range(2):
                    src = ps[jm * 64:(jm + 1) * 64, 4 * 512: 4 * 512 + 2048].rearrange(
                        "p (pp cc) -> p pp cc", cc=64)[:, :, jm * 32 + h]
                    hp = h // 2
                    dst = Ohp[(h % 2) * 64:(h % 2) * 64 + 64,
                              hp * T + tok0: hp * T + tok0 + 64].rearrange(
                        "p (pp two) -> p pp two", two=2)[:, :, jm]
                    if SCE_EVAC and (h + jm) % 2 == 1:
                        P.scalar.run(lambda e, src=src, dst=dst: e.copy(dst, src), inc=s_se)
                    else:
                        P.vector.run(lambda e, src=src, dst=dst: e.tensor_copy(dst, src), inc=s_ve)
            for b in range(4, 8):
                consumed(b, s_ve)
                consumed(b, s_se)
        P.tensor.wait(s_ve)
        P.tensor.wait(s_se)
        for mc in range(8):
            b = 2 + mc % 2
            te_bank(b)
            for kc in range(8):
                P.tensor.run(lambda e, b=b, mc=mc, kc=kc: e.matmul(
                    B(b, cols=T), W["wo"][:, kc * 1024 + mc * 128: kc * 1024 + (mc + 1) * 128],
                    Ohp[:, kc * T:(kc + 1) * T],
                    start=(kc == 0), stop=(kc == 7)), inc=(s_te if kc == 7 else None))
            P.vector.wait(s_te)
            cs = slice(mc * T, (mc + 1) * T)
            P.vector.run(lambda e, b=b, cs=cs, mc=mc: e.scalar_tensor_tensor(
                X2[:, cs], B(b, cols=T), W["bo"][:, mc:mc + 1], LN0[:, cs],
                ALU.add, ALU.add), inc=s_ve)
            consumed(b, s_ve)
        P.vector.run(lambda e: e.tensor_copy(rb[:], X2[:]), inc=s_ve)
        ln_stats(rb, "eps6", out_mu=False)
        for c in range(8):
            cs = slice(c * T, (c + 1) * T)
            P.vector.run(lambda e, cs=cs, c=c: e.scalar_tensor_tensor(
                H2[:, cs], X2[:, cs], W["g2"][:, c:c + 1], B(1, cols=T),
                ALU.mult, ALU.mult), inc=s_ve)
        consumed(1, s_ve)
        if not do_gate:
            P.vector.run(lambda e: e.memset(gt[:], 0.125), inc=s_ve)
        te_bank(2)
        te_bank(3)
        P.tensor.wait(s_ve)
        for tb in range(T // 128 if do_gate else 0):
            for kc in range(8):
                P.tensor.run(lambda e, tb=tb, kc=kc: e.matmul(
                    B(2, 128, 8, co=tb * 8),
                    H2[:, kc * T + tb * 128: kc * T + tb * 128 + 128],
                    W["wg"][:, kc * 8:(kc + 1) * 8],
                    start=(kc == 0), stop=(kc == 7)),
                    inc=(s_te if kc == 7 else None))
        P.vector.wait(s_te)
        for tb in range(T // 128 if do_gate else 0):
            P.vector.run(lambda e, tb=tb: e.tensor_add(
                gsb[:, tb * 8:(tb + 1) * 8], B(2, 128, 8, co=tb * 8),
                W["bgr"][0:1, :].broadcast_to([128, 8]) if False else W["bgr"][:, :]), inc=s_ve)
        consumed(2, s_ve)
        for tb in range(T // 128 if do_gate else 0):
            gs = gsb[:, tb * 8:(tb + 1) * 8]
            g3 = gs.rearrange("p (q c) -> p q c", c=8)
            P.vector.run(lambda e, gs=gs, g3=g3: e.reduce_max(mx[:, 0:1], g3, AX.X), inc=s_ve)
            P.vector.wait(s_ve)
            P.vector.run(lambda e, gs=gs: e.tensor_sub(
                gs, gs, mx[:, 0:1].broadcast_to([128, 8])), inc=s_ve)
            P.scalar.wait(s_ve)
            P.scalar.run(lambda e, gs=gs: e.activation(gs, gs, AF.Exp), inc=s_se)
            P.vector.wait(s_se)
            P.vector.run(lambda e, gs=gs: e.reduce_sum(
                sm[:, 0:1], gs.rearrange("p (q c) -> p q c", c=8), AX.X), inc=s_ve)
            P.vector.wait(s_ve)
            P.vector.run(lambda e: e.reciprocal(rc[:, 0:1], sm[:, 0:1]), inc=s_ve)
            P.vector.wait(s_ve)
            P.vector.run(lambda e, gs=gs: e.tensor_mul(
                gs, gs, rc[:, 0:1].broadcast_to([128, 8])), inc=s_ve)
        if do_gate:
            P.tensor.wait(s_ve)
            for tb in range(T // 128):
                P.tensor.run(lambda e, tb=tb: e.transpose(
                    B(3, 8, 128, co=tb * 128), gsb[:, tb * 8:(tb + 1) * 8], W["eye"][:]),
                    inc=(s_te if tb == (T // 128) - 1 else None))
            P.vector.wait(s_te)
            for tb in range(T // 128):
                P.vector.run(lambda e, tb=tb: e.tensor_copy(
                    gt[:, tb * 128:(tb + 1) * 128], B(3, 8, 128, co=tb * 128)), inc=s_ve)
            consumed(3, s_ve)
        for c in range(8):
            cs = slice(c * T, (c + 1) * T)
            P.vector.run(lambda e, cs=cs: e.scalar_tensor_tensor(
                rt[:, cs], X2[:, cs], P.cc("half"), rt[:, cs],
                ALU.mult, ALU.add), inc=s_ve)
        P.sync.wait(s_ve)
        P.sync.wait(s_se)
        P.dma(s_v[:, :, t0:t0 + T], rtv, s_st)
        P.dma(h2_v[:, :, t0:t0 + T], h2v, s_st)
        P.dma(gate_out_dram[:, t0:t0 + T], gt[:], s_st)
    if dbg is not None:
        P.sync.wait(s_ve)
        P.dma(dbg["sraw"], dbg_sraw[:], s_st)
        P.dma(dbg["abf"], dbg_abf[:], s_st)
        P.dma(dbg["at"], dbg_at[:], s_st)
        P.plans["gpsimd"].wait(s_ve)
        P.dma(dbg["vzz"], dbg_vzz[:], P.sem("s_stg2"), eng="gpsimd")
        P.sync.wait(P.sem("s_stg2"))
    P.sync.wait(s_st)


def emit_ffnB(P, st, W, s_in_dram, h2_in_dram, gate_in_dram, out_dram, L=4096, T=256, dbg=None):
    nc = P.nc
    n_t = L // T
    sb = lambda n, s, d: st.enter_context(nc.sbuf_tensor(f"{n}_ph{P._phase_idx}", s, d))
    stt = sb("b_st", [128, 8 * T], F32)
    H2 = sb("b_h2", [128, 8 * T], BF16)
    HS = sb("b_hs", [128, 8 * T], BF16)
    gt = sb("b_gt", [8, T], F32)
    gtb = sb("b_gtb", [32, T], BF16)
    gb8 = sb("b_gb8", [128, 8 * T], BF16)
    ps = st.enter_context(nc.psum_tensor(f"b_ps_ph{P._phase_idx}", [128, 4096], F32))
    if dbg is not None:
        dbg_gb = sb("dbg_gb", [128, 8 * T], F32)
        dbg_hs = sb("dbg_hs", [128, 8 * T], F32)
    s_ld = P.sem("s_ld")
    s_st = P.sem("s_st")
    s_te = P.sem("s_te")
    s_ve = P.sem("s_ve")
    bank_free = {}

    def te_bank(b):
        for sem, cnt in bank_free.pop(b, []):
            P.tensor.wait(sem, cnt)

    def consumed(b, sem):
        bank_free.setdefault(b, []).append((sem, sem.n))

    s_v = s_in_dram.rearrange("(c p) t -> p c t", p=128)
    h2_v = h2_in_dram.rearrange("(c p) t -> p c t", p=128)
    o_v = out_dram.rearrange("(c p) t -> p c t", p=128)
    stv = stt[:].rearrange("p (c t) -> p c t", c=8)
    h2v = H2[:].rearrange("p (c t) -> p c t", c=8)

    P.vector.run(lambda e: e.memset(gtb[:], 0.0), inc=s_ve)
    for i in range(n_t):
        t0 = i * T
        P.sync.wait(s_ve)
        P.sync.wait(s_st)
        P.dma(stv, s_v[:, :, t0:t0 + T], s_ld)
        P.dma(h2v, h2_v[:, :, t0:t0 + T], s_ld)
        P.dma(gt[:], gate_in_dram[:, t0:t0 + T], s_ld)
        P.vector.wait(s_ld)
        P.vector.run(lambda e: e.tensor_copy(gtb[0:8, :], gt[:]), inc=s_ve)
        P.tensor.wait(s_ve)
        te_bank(0)
        te_bank(1)
        for e_ in range(8):
            bco = (e_ % 2) * 512
            if e_ >= 2:
                P.tensor.wait(s_ve)
            P.tensor.run(lambda e, e_=e_, bco=bco: e.matmul(
                ps[:, bco: bco + T],
                W["sel"][0:32, e_ * 128:(e_ + 1) * 128], gtb[:],
                start=True, stop=True), inc=s_te)
            P.vector.wait(s_te)
            P.vector.run(lambda e, e_=e_, bco=bco: e.tensor_copy(
                gb8[:, e_ * T:(e_ + 1) * T], ps[:, bco: bco + T]), inc=s_ve)
        consumed(0, s_ve)
        consumed(1, s_ve)
        for e_ in range(8):
            if e_ > 0:
                P.vector.wait(s_te)
            gbv = gb8[:, 0:8 * T].rearrange("p (o t) -> p o t", o=8)[:, 0:1, :].broadcast_to([128, 8, T]) if False else None
            gview = gb8[:, :].rearrange("p (ee t) -> p ee t", ee=8)[:, e_:e_ + 1, :].broadcast_to([128, 8, T])
            P.vector.run(lambda e, e_=e_, gview=gview: e.tensor_mul(
                HS[:].rearrange("p (c t) -> p c t", c=8), H2[:].rearrange("p (c t) -> p c t", c=8),
                gview), inc=s_ve)
            P.tensor.wait(s_ve)
            for mc in range(8):
                b = mc
                if e_ == 0:
                    te_bank(b)
                for kc in range(8):
                    sh = 256 if T == 256 else 0
                    P.tensor.run(lambda e, b=b, mc=mc, kc=kc, e_=e_, sh=sh: e.matmul(
                        ps[:, b * 512 + sh: b * 512 + sh + T] if b <= 1 else ps[:, b * 512: b * 512 + T],
                        W["we"][:, e_ * 8192 + kc * 1024 + mc * 128: e_ * 8192 + kc * 1024 + (mc + 1) * 128],
                        HS[:, kc * T:(kc + 1) * T],
                        start=(e_ == 0 and kc == 0), stop=(e_ == 7 and kc == 7)),
                        inc=(s_te if (kc == 7 and mc == 7) else None))
        P.vector.wait(s_te)
        if dbg is not None and i == n_t - 1:
            P.vector.run(lambda e: e.tensor_copy(dbg_gb[:], gb8[:]))
            P.vector.run(lambda e: e.tensor_copy(dbg_hs[:], HS[:]), inc=s_ve)
        for mc in range(8):
            b = mc
            cs = slice(mc * T, (mc + 1) * T)
            sh = 256 if T == 256 else 0
            src = ps[:, b * 512 + sh: b * 512 + sh + T] if b <= 1 else ps[:, b * 512: b * 512 + T]
            P.vector.run(lambda e, src=src, cs=cs: e.scalar_tensor_tensor(
                stt[:, cs], src, P.cc("half"), stt[:, cs],
                ALU.mult, ALU.add), inc=s_ve)
        for b in range(8):
            consumed(b, s_ve)
        P.sync.wait(s_ve)
        P.dma(o_v[:, :, t0:t0 + T], stv, s_st)
    if dbg is not None:
        P.sync.wait(s_ve)
        P.dma(dbg["gb8"], dbg_gb[:], s_st)
        P.dma(dbg["hs"], dbg_hs[:], s_st)
    P.sync.wait(s_st)



from contextlib import ExitStack

CONSTS = {"eps5": 1e-5, "eps6": 1e-6, "negh": -0.5, "neg1": -1.0,
          "one": 1.0, "r1024": 1.0 / 1024.0, "q8": 0.125, "half": 0.5}


def build_program(L=4096, T_f=512, T_m=256):
    nc = bass.Bass("TRN2", target_bir_lowering=False, detect_race_conditions=False)
    dram = {}
    for k, (shape, dt) in INPUT_SPECS.items():
        dram[k] = nc.dram_tensor(k, list(shape), dt, kind="ExternalInput").ap()
    out = nc.dram_tensor("out", [1024, L], F32, kind="ExternalOutput").ap()
    s_scr = nc.dram_tensor("s_scr", [1024, L], F32, kind="Internal").ap()
    h2_scr = nc.dram_tensor("h2_scr", [1024, L], BF16, kind="Internal").ap()
    g_scr = nc.dram_tensor("g_scr", [8, L], F32, kind="Internal").ap()
    r1_scr = nc.dram_tensor("r1_scr", [1024, L], F32, kind="Internal").ap()
    r2_scr = nc.dram_tensor("r2_scr", [1024, L], F32, kind="Internal").ap()
    with ExitStack() as big:
        P = Prog(nc)
        P._stack = big
        P.setup_consts(CONSTS)
        with ExitStack() as st:
            W = emit_load_ffn_weights(P, st, dram, "A")
            emit_ffnA(P, st, W, dram["xT"], s_scr, h2_scr, g_scr, L=L, T=T_f)
            P.flush_phase()
        with ExitStack() as st:
            W = emit_load_ffn_weights(P, st, dram, "B")
            emit_ffnB(P, st, W, s_scr, h2_scr, g_scr, r1_scr, L=L, T=T_f)
            P.flush_phase()
        with ExitStack() as st:
            W = emit_load_mamba_weights(P, st, dram)
            emit_mamba(P, st, W, r1_scr, r2_scr, dram["c_sel"], L=L, T=T_m)
            P.flush_phase()
        with ExitStack() as st:
            W = emit_load_ffn_weights(P, st, dram, "A")
            emit_ffnA(P, st, W, r2_scr, s_scr, h2_scr, g_scr, L=L, T=T_f)
            P.flush_phase()
        with ExitStack() as st:
            W = emit_load_ffn_weights(P, st, dram, "B")
            emit_ffnB(P, st, W, s_scr, h2_scr, g_scr, out, L=L, T=T_f)
            P.flush_phase()
    return nc


def kernel(**inputs):
    common = host_layout(inputs, 0)
    del common["xT"]
    nc = build_program()
    in_maps = []
    for c in range(NCORES):
        m = dict(common)
        m["xT"] = np.ascontiguousarray(inputs["L0_attn_x"][c].T)
        in_maps.append(m)
    res = run_bass_kernel_spmd(nc, in_maps, core_ids=list(range(NCORES)))
    return np.stack([np.ascontiguousarray(r["out"].T) for r in res.results], axis=0)


# version 20
# speedup vs baseline: 1.0625x; 1.0002x over previous
import numpy as np
import ml_dtypes
import concourse.bass as bass
import concourse.mybir as mybir
from concourse.bass_utils import run_bass_kernel_spmd
from concourse.bass_isa import ReduceOp
RED_MAX = ReduceOp.max if hasattr(ReduceOp, "max") else "max"
RED_ADD = ReduceOp.add if hasattr(ReduceOp, "add") else "add" 

F32 = mybir.dt.float32
BF16 = mybir.dt.bfloat16
AF = mybir.ActivationFunctionType
ALU = mybir.AluOpType
AX = mybir.AxisListType

D_MODEL = 1024
D_INNER = 2048
D_STATE = 16
D_CONV = 4
DT_RANK = 64
NCORES = 8

LN_EPS_OUTER = 1e-5
LN_EPS_INNER = 1e-6
NEG_BIG = -30000.0
import os
SCE_EVAC = os.environ.get("SCE_EVAC", "0") == "1"



class Sem:
    __slots__ = ("h", "n")

    def __init__(self, h):
        self.h = h
        self.n = 0


class EngPlan:
    def __init__(self, name):
        self.name = name
        self.steps = []
        self.waited = {}

    def wait(self, sem, val=None):
        v = sem.n if val is None else val
        if self.waited.get(id(sem), -1) >= v:
            return
        self.waited[id(sem)] = v
        h = sem.h
        self.steps.append(lambda e, h=h, v=v: e.wait_ge(h, v))

    def run(self, fn, inc=None, n=1):
        if inc is not None:
            inc.n += n
            h = inc.h
            self.steps.append(lambda e, fn=fn, h=h, n=n: fn(e).then_inc(h, n))
        else:
            self.steps.append(fn)


class Prog:
    ENGS = ("sync", "vector", "scalar", "tensor", "gpsimd")

    def __init__(self, nc):
        self.nc = nc
        self.sems = {}
        self.plans = {k: EngPlan(k) for k in self.ENGS}
        self.sync = self.plans["sync"]
        self.vector = self.plans["vector"]
        self.scalar = self.plans["scalar"]
        self.tensor = self.plans["tensor"]
        self.gpsimd = self.plans["gpsimd"]
        self._phase_idx = 0
        self._stack = None

    def sem(self, name):
        if name not in self.sems:
            h = self._stack.enter_context(self.nc.semaphore(name))
            self.sems[name] = Sem(h)
        return self.sems[name]

    def setup_consts(self, values):
        n = len(values)
        t = self._stack.enter_context(self.nc.sbuf_tensor("cconsts", [128, max(n, 1)], F32))
        self._consts = t
        self._cmap = {}
        for i, (k, v) in enumerate(values.items()):
            self._cmap[k] = i
            self.vector.run(lambda e, i=i, v=v: e.memset(t[:, i:i + 1], v))
        s = self.sem("s_const")
        self.vector.run(lambda e: e.engine_nop(), inc=s)
        for k in ("scalar", "vector", "gpsimd", "tensor"):
            self.plans[k].wait(s)

    def tile_barrier(self):
        bar = self.sem("tilebar")
        for k in ("vector", "scalar", "tensor"):
            self.plans[k].run(lambda e: e.drain(), inc=bar, n=1)
        target = bar.n
        for k in ("vector", "scalar", "tensor", "sync", "gpsimd"):
            self.plans[k].wait(bar, target)

    def cc(self, name, rows=128):
        i = self._cmap[name]
        return self._consts[0:rows, i:i + 1]

    def dma(self, out, in_, sem, eng="sync"):
        self.plans[eng].run(lambda e, o=out, i=in_: e.dma_start(out=o, in_=i), inc=sem, n=16)

    def flush_phase(self):
        bar = self.sem("phasebar")
        for k in ("vector", "scalar", "tensor", "gpsimd"):
            self.plans[k].run(lambda e: e.drain(), inc=bar, n=1)
        self.plans["sync"].run(lambda e: e.sem_inc(bar.h, 1))
        bar.n += 1
        nc = self.nc
        with nc.Block() as blk:
            for k in self.ENGS:
                steps = self.plans[k].steps
                getattr(blk, k)(lambda e, steps=steps: [f(e) for f in steps])
        self._phase_idx += 1
        target = bar.n
        for k in self.ENGS:
            self.plans[k] = EngPlan(k)
        self.sync = self.plans["sync"]
        self.vector = self.plans["vector"]
        self.scalar = self.plans["scalar"]
        self.tensor = self.plans["tensor"]
        self.gpsimd = self.plans["gpsimd"]
        for k in self.ENGS:
            self.plans[k].wait(self.sems["phasebar"], target)



def _img_chunked(w, rows=128):
    K, M = w.shape
    return np.ascontiguousarray(w.reshape(K // rows, rows, M).transpose(1, 0, 2).reshape(rows, -1))


def sel_const():
    sel = np.zeros((32, 32 * 128), np.float32)
    for n in range(32):
        sel[n, n * 128:(n + 1) * 128] = 1.0
    return sel


def upv_img(wup, bup):
    out = np.zeros((65, 2048), np.float32)
    for dh in range(8):
        for q in range(2):
            for m in range(4):
                dd = dh * 8 + q * 4 + m
                col0 = dh * 256 + q * 128 + m * 32
                out[0:64, col0:col0 + 16] = wup[:, 16 * 0:16 * 16:128][:, :] if False else 0
                for g in range(16):
                    out[0:64, col0 + g] = wup[:, g * 128 + 64 + dd]
                    out[64, col0 + g] = bup[g * 128 + 64 + dd]
    return np.ascontiguousarray(out)


def host_layout(inp, core):
    f = np.ascontiguousarray
    d = {}
    d["xT"] = f(inp["L0_attn_x"][core].T)
    d["w_inproj"] = _img_chunked(f(inp["L0_attn_in_proj_w"].T))
    d["w_outproj"] = _img_chunked(f(inp["L0_attn_out_proj_w"].T))
    d["w_xproj"] = _img_chunked(f(inp["L0_attn_x_proj_w"].T))
    d["w_dtproj"] = f(inp["L0_attn_dt_proj_w"].T)
    d["v_convw"] = f(inp["L0_attn_conv_w"][:, 0, :].reshape(16, 128, 4).transpose(1, 0, 2).reshape(128, 64))
    d["v_convb"] = f(inp["L0_attn_conv_b"].reshape(16, 128).T)
    d["v_D"] = f(inp["L0_attn_D"].reshape(16, 128).T)
    d["v_dtb"] = f(inp["L0_attn_dt_proj_b"].reshape(16, 128).T)
    d["v_Alog"] = f(inp["L0_attn_A_log"].reshape(16, 128, 16).transpose(1, 0, 2).reshape(128, 256))
    d["v_lng"] = f(inp["L0_attn_ln_gamma"].reshape(8, 128).T)
    d["v_lnb"] = f(inp["L0_attn_ln_beta"].reshape(8, 128).T)
    d["w_q"] = _img_chunked(f(inp["L0_ffn_Wq"]))
    d["w_down"] = _img_chunked(f(inp["L0_ffn_Wdown"]))
    wup = inp["L0_ffn_Wup"]
    d["w_upk"] = f(wup.reshape(64, 16, 128)[:, :, 0:64].reshape(64, 16 * 64))
    d["w_upv"] = upv_img(wup, inp["L0_ffn_bup"])
    d["w_o"] = _img_chunked(f(inp["L0_ffn_Wo"]))
    d["w_g"] = _img_chunked(f(inp["L0_ffn_Wg"]))
    d["w_e"] = f(inp["L0_ffn_We"].reshape(8, 8, 128, 1024).transpose(2, 0, 1, 3).reshape(128, 8 * 8 * 1024))
    d["v_g1"] = f(inp["L0_ffn_gamma1"].reshape(8, 128).T)
    d["v_g2"] = f(inp["L0_ffn_gamma2"].reshape(8, 128).T)
    d["v_bq"] = f(inp["L0_ffn_bq"].reshape(8, 128).T)
    d["v_bo"] = f(inp["L0_ffn_bo"].reshape(8, 128).T)
    d["v_bdown"] = f(inp["L0_ffn_bdown"].reshape(64, 1))
    bup = inp["L0_ffn_bup"]
    d["v_bupk"] = f(bup.reshape(16, 128)[:, 0:64].T)
    d["v_bg"] = f(inp["L0_ffn_bg"].reshape(8, 1))
    d["c_sel"] = sel_const()
    d["c_mask"] = mask_const()
    d["c_eye"] = np.eye(128, dtype=np.float32)
    d["c_bgr"] = f(np.broadcast_to(inp["L0_ffn_bg"][None, :], (128, 8)).copy())
    return d


INPUT_SPECS = {
    "xT": ([1024, 4096], F32),
    "w_inproj": ([128, 8 * 4096], F32),
    "w_outproj": ([128, 16 * 1024], F32),
    "w_xproj": ([128, 16 * 96], F32),
    "w_dtproj": ([64, 2048], F32),
    "v_convw": ([128, 64], F32),
    "v_convb": ([128, 16], F32),
    "v_D": ([128, 16], F32),
    "v_dtb": ([128, 16], F32),
    "v_Alog": ([128, 256], F32),
    "v_lng": ([128, 8], F32),
    "v_lnb": ([128, 8], F32),
    "w_q": ([128, 8 * 1024], F32),
    "w_down": ([128, 8 * 64], F32),
    "w_upk": ([64, 16 * 64], F32),
    "w_upv": ([65, 2048], F32),
    "w_o": ([128, 8 * 1024], F32),
    "w_g": ([128, 64], F32),
    "w_e": ([128, 8 * 8 * 1024], F32),
    "v_g1": ([128, 8], F32),
    "v_g2": ([128, 8], F32),
    "v_bq": ([128, 8], F32),
    "v_bo": ([128, 8], F32),
    "v_bdown": ([64, 1], F32),
    "v_bupk": ([64, 16], F32),
    "v_bg": ([8, 1], F32),
    "c_sel": ([32, 32 * 128], F32),
    "c_mask": ([128, 512], F32),
    "c_eye": ([128, 128], F32),
    "c_bgr": ([128, 8], F32),
}



def emit_load_mamba_weights(P, st, dram):
    nc = P.nc
    sb = lambda n, s, d: st.enter_context(nc.sbuf_tensor(f"{n}_ph{P._phase_idx}", s, d))
    W = {}
    W["inW"] = sb("m_inW", [128, 8 * 4096], BF16)
    W["opW"] = sb("m_opW", [128, 16 * 1024], BF16)
    W["xpW"] = sb("m_xpW", [128, 16 * 96], BF16)
    W["dtW"] = sb("m_dtW", [64, 2048], BF16)
    W["cw"] = sb("m_cw", [128, 64], F32)
    W["cb"] = sb("m_cb", [128, 16], F32)
    W["Dv"] = sb("m_Dv", [128, 16], F32)
    W["dtb"] = sb("m_dtb", [128, 16], F32)
    W["An"] = sb("m_An", [128, 256], F32)
    W["lng"] = sb("m_lng", [128, 8], F32)
    W["lnb"] = sb("m_lnb", [128, 8], F32)
    W["onesc"] = sb("m_onesc", [128, 1], BF16)
    W["ones1"] = sb("m_ones1", [1, 128], BF16)
    W["ones1f"] = sb("m_ones1f", [1, 128], F32)
    s_w = P.sem("s_w")
    s_wg = P.sem("s_wg")
    P.dma(W["inW"][:], dram["w_inproj"], s_wg, eng="gpsimd")
    P.dma(W["opW"][:], dram["w_outproj"], s_wg, eng="gpsimd")
    P.dma(W["xpW"][:], dram["w_xproj"], s_wg, eng="gpsimd")
    P.dma(W["dtW"][:], dram["w_dtproj"], s_wg, eng="gpsimd")
    for k, src in [("cw", "v_convw"), ("cb", "v_convb"), ("Dv", "v_D"),
                   ("dtb", "v_dtb"), ("An", "v_Alog"), ("lng", "v_lng"), ("lnb", "v_lnb")]:
        P.dma(W[k][:], dram[src], s_w, eng="sync")
    P.vector.run(lambda e, W=W: e.memset(W["onesc"][:], 1.0))
    P.vector.run(lambda e, W=W: e.memset(W["ones1"][:], 1.0))
    P.vector.run(lambda e, W=W: e.memset(W["ones1f"][:], 1.0))
    P.scalar.wait(s_w)
    P.scalar.wait(s_wg)
    s_se = P.sem("s_se")
    P.scalar.run(lambda e, W=W: e.activation(W["An"][:], W["An"][:], AF.Exp), inc=s_se)
    P.vector.wait(s_se)
    s_ve = P.sem("s_ve")
    P.vector.run(lambda e, W=W: e.tensor_scalar_mul(W["An"][:], W["An"][:], -1.0), inc=s_ve)
    return W



def emit_mamba(P, st, W, in_dram, out_dram, in_sel, L=4096, T=256, dbg=None):
    nc = P.nc
    n_t = L // T
    NC = 16
    sb = lambda n, s, d: st.enter_context(nc.sbuf_tensor(f"{n}_ph{P._phase_idx}", s, d))
    rt = sb("m_rt", [128, 8 * T], F32)
    LN0 = sb("m_ln0", [128, 8 * T], F32)
    hm = sb("m_hm", [128, 8 * T], BF16)
    rb = hm
    UC = sb("m_UC", [128, NC * T], BF16)
    ZS = sb("m_ZS", [128, NC * T], BF16)
    Bb = sb("m_Bb", [128, NC * T], BF16)
    Cb = sb("m_Cb", [128, NC * T], BF16)
    DELTA = sb("m_DELTA", [128, NC * T], F32)
    SEL = sb("m_SEL", [32, 32 * 128], BF16)
    dtb_sb = sb("m_dtbsb", [64, T], BF16)
    bcrow = sb("m_bcrow", [32, T], BF16)
    ue = sb("m_ue", [128, T + 4], BF16)
    utail = sb("m_utail", [128, NC * 3], BF16)
    cacc = sb("m_cacc", [128, T], F32)
    accv = sb("m_accv", [128, T], F32)
    sp_e = sb("m_spe", [128, T], F32)
    sp_r = sb("m_spr", [128, T], F32)
    sg = sb("m_sg", [128, T], BF16)
    xt = sb("m_xt", [128, T], BF16)
    dab = [sb(f"m_dab{i}", [128, T], F32) for i in range(2)]
    h32 = [sb(f"m_h32{i}", [128, T], F32) for i in range(2)]
    DBU = sb("m_DBU", [128, 16 * T], BF16)
    yh = sb("m_yh", [128, T], BF16)
    y2 = sb("m_y2", [128, T], BF16)
    STATE = sb("m_state", [128, 256], F32)
    mu_r = sb("m_mu", [1, T], F32)
    ms_r = sb("m_ms", [1, T], F32)
    rs_r = sb("m_rs", [1, T], F32)
    ps = [st.enter_context(nc.psum_tensor(f"m_ps{i}_ph{P._phase_idx}", [128, 512], F32)) for i in range(8)]

    s_ld = P.sem("s_ld")
    s_st = P.sem("s_st")
    s_te = P.sem("s_te")
    s_ve = P.sem("s_ve")
    s_se = P.sem("s_se")
    s_w = P.sem("s_w")
    s_gp = P.sem("s_gp")

    bank_free = {}

    def te_bank(b):
        if b in bank_free:
            sem, cnt = bank_free[b]
            P.tensor.wait(sem, cnt)

    def consumed(b, sem):
        bank_free[b] = (sem, sem.n)

    in_v = in_dram.rearrange("(c p) t -> p c t", p=128)
    out_v = out_dram.rearrange("(c p) t -> p c t", p=128)
    rtv = rt[:].rearrange("p (c t) -> p c t", c=8)

    s_wg2 = P.sem("s_wg2")
    P.dma(SEL[:], in_sel, s_wg2, eng="gpsimd")
    for eng in ("vector", "scalar", "tensor"):
        P.plans[eng].wait(s_wg2)
    P.vector.run(lambda e: e.memset(STATE[:], 0.0))
    P.vector.run(lambda e: e.memset(utail[:], 0.0), inc=s_ve)
    for eng in ("vector", "scalar", "tensor"):
        P.plans[eng].wait(s_w)
        P.plans[eng].wait(P.sem("s_wg"))

    for i in range(n_t):
        t0 = i * T
        P.sync.wait(s_ve)
        P.sync.wait(s_st)
        P.dma(rtv, in_v[:, :, t0:t0 + T], s_ld)
        P.vector.wait(s_ld)
        P.vector.wait(s_te)
        P.vector.run(lambda e: e.tensor_copy(rb[:], rt[:]), inc=s_ve)
        P.tensor.wait(s_ve)
        te_bank(0)
        for c in range(8):
            P.tensor.run(lambda e, c=c: e.matmul(
                ps[0][0:1, 0:T], W["onesc"][:], rb[:, c * T:(c + 1) * T],
                start=(c == 0), stop=(c == 7)), inc=(s_te if c == 7 else None))
        P.scalar.wait(s_te)
        P.scalar.run(lambda e: e.activation(rb[:], rb[:], AF.Square), inc=s_se)
        P.tensor.wait(s_se)
        te_bank(1)
        for c in range(8):
            P.tensor.run(lambda e, c=c: e.matmul(
                ps[1][0:1, 0:T], W["onesc"][:], rb[:, c * T:(c + 1) * T],
                start=(c == 0), stop=(c == 7)), inc=(s_te if c == 7 else None))
        P.scalar.wait(s_te)
        P.scalar.run(lambda e: e.activation(mu_r[:], ps[0][0:1, 0:T], AF.Copy, scale=P.cc("r1024", 1)))
        P.scalar.run(lambda e: e.activation(ms_r[:], ps[1][0:1, 0:T], AF.Copy, scale=P.cc("r1024", 1)))
        P.scalar.run(lambda e: e.activation(rs_r[:], mu_r[:], AF.Square), inc=s_se)
        consumed(0, s_se)
        consumed(1, s_se)
        P.vector.wait(s_se)
        P.vector.run(lambda e: e.tensor_sub(ms_r[:], ms_r[:], rs_r[:]), inc=s_ve)
        P.scalar.wait(s_ve)
        P.scalar.run(lambda e: e.activation(ms_r[:], ms_r[:], AF.Ln, bias=P.cc("eps5", 1)))
        P.scalar.run(lambda e: e.activation(rs_r[:], ms_r[:], AF.Exp, scale=P.cc("negh", 1)), inc=s_se)
        P.tensor.wait(s_se)
        te_bank(0)
        te_bank(1)
        P.tensor.run(lambda e: e.matmul(ps[0][:, 0:T], W["ones1f"][:], mu_r[:], start=True, stop=True))
        P.tensor.run(lambda e: e.matmul(ps[1][:, 0:T], W["ones1f"][:], rs_r[:], start=True, stop=True), inc=s_te)
        P.vector.wait(s_te)
        mub = ps[0][:, 0:T].rearrange("p (o t) -> p o t", o=1).broadcast_to([128, 8, T])
        rsb = ps[1][:, 0:T].rearrange("p (o t) -> p o t", o=1).broadcast_to([128, 8, T])
        l3 = LN0[:].rearrange("p (c t) -> p c t", c=8)
        r3 = rt[:].rearrange("p (c t) -> p c t", c=8)
        P.vector.run(lambda e, l3=l3, r3=r3, mub=mub: e.tensor_sub(l3, r3, mub))
        P.vector.run(lambda e, l3=l3, rsb=rsb: e.tensor_mul(l3, l3, rsb))
        P.vector.run(lambda e, l3=l3, r3=r3: e.tensor_add(r3, r3, l3), inc=s_ve)
        consumed(0, s_ve)
        consumed(1, s_ve)
        P.vector.run(lambda e: e.tensor_copy(rb[:], LN0[:]), inc=s_ve)
        P.tensor.wait(s_ve)
        te_bank(0)
        for c in range(8):
            P.tensor.run(lambda e, c=c: e.matmul(
                ps[0][0:1, 0:T], W["onesc"][:], rb[:, c * T:(c + 1) * T],
                start=(c == 0), stop=(c == 7)), inc=(s_te if c == 7 else None))
        P.scalar.wait(s_te)
        P.scalar.run(lambda e: e.activation(rb[:], rb[:], AF.Square), inc=s_se)
        P.tensor.wait(s_se)
        te_bank(1)
        for c in range(8):
            P.tensor.run(lambda e, c=c: e.matmul(
                ps[1][0:1, 0:T], W["onesc"][:], rb[:, c * T:(c + 1) * T],
                start=(c == 0), stop=(c == 7)), inc=(s_te if c == 7 else None))
        P.scalar.wait(s_te)
        P.scalar.run(lambda e: e.activation(mu_r[:], ps[0][0:1, 0:T], AF.Copy, scale=P.cc("r1024", 1)))
        P.scalar.run(lambda e: e.activation(ms_r[:], ps[1][0:1, 0:T], AF.Copy, scale=P.cc("r1024", 1)))
        P.scalar.run(lambda e: e.activation(rs_r[:], mu_r[:], AF.Square), inc=s_se)
        consumed(0, s_se)
        consumed(1, s_se)
        P.vector.wait(s_se)
        P.vector.run(lambda e: e.tensor_sub(ms_r[:], ms_r[:], rs_r[:]), inc=s_ve)
        P.scalar.wait(s_ve)
        P.scalar.run(lambda e: e.activation(ms_r[:], ms_r[:], AF.Ln, bias=P.cc("eps6", 1)))
        P.scalar.run(lambda e: e.activation(rs_r[:], ms_r[:], AF.Exp, scale=P.cc("negh", 1)), inc=s_se)
        P.tensor.wait(s_se)
        te_bank(0)
        te_bank(1)
        P.tensor.run(lambda e: e.matmul(ps[0][:, 0:T], W["ones1f"][:], mu_r[:], start=True, stop=True))
        P.tensor.run(lambda e: e.matmul(ps[1][:, 0:T], W["ones1f"][:], rs_r[:], start=True, stop=True), inc=s_te)
        P.vector.wait(s_te)
        for c in range(8):
            cs = slice(c * T, (c + 1) * T)
            P.vector.run(lambda e, cs=cs: e.tensor_sub(accv[:], LN0[:, cs], ps[0][:, 0:T]))
            P.vector.run(lambda e: e.tensor_mul(accv[:], accv[:], ps[1][:, 0:T]))
            P.vector.run(lambda e, cs=cs, c=c: e.tensor_scalar(
                hm[:, cs], accv[:], W["lng"][:, c:c + 1], W["lnb"][:, c:c + 1],
                ALU.mult, ALU.add), inc=s_ve)
        consumed(0, s_ve)
        consumed(1, s_ve)
        P.tensor.wait(s_ve)
        te_bank(2)
        for mc in range(32):
            b = 3 + (mc % 2) * 4
            te_bank(b)
            for kc in range(8):
                P.tensor.run(lambda e, b=b, kc=kc, mc=mc: e.matmul(
                    ps[b][:, 0:T],
                    W["inW"][:, kc * 4096 + mc * 128: kc * 4096 + (mc + 1) * 128],
                    hm[:, kc * T:(kc + 1) * T],
                    start=(kc == 0), stop=(kc == 7)), inc=(s_te if kc == 7 else None))
            if mc < 16:
                P.vector.wait(s_te)
                P.vector.run(lambda e, mc=mc: e.tensor_copy(ue[:, 0:3], utail[:, mc * 3:mc * 3 + 3]))
                P.vector.run(lambda e, b=b: e.tensor_copy(ue[:, 3:3 + T], ps[b][:, 0:T]), inc=s_ve)
                consumed(b, s_ve)
                P.vector.run(lambda e, mc=mc: e.tensor_scalar(
                    cacc[:], ue[:, 0:T], W["cw"][:, mc * 4:mc * 4 + 1],
                    W["cb"][:, mc:mc + 1], ALU.mult, ALU.add))
                for k in range(1, 4):
                    P.vector.run(lambda e, k=k, mc=mc: e.scalar_tensor_tensor(
                        cacc[:], ue[:, k:k + T], W["cw"][:, mc * 4 + k:mc * 4 + k + 1], cacc[:],
                        ALU.mult, ALU.add), inc=(s_ve if k == 3 else None))
                P.vector.run(lambda e, mc=mc: e.tensor_copy(
                    utail[:, mc * 3:mc * 3 + 3], ue[:, T:T + 3]))
                P.scalar.wait(s_ve)
                P.scalar.run(lambda e, mc=mc: e.activation(
                    sg[:], cacc[:], AF.Sigmoid), inc=s_se)
                P.vector.wait(s_se)
                P.vector.run(lambda e, mc=mc: e.tensor_mul(
                    UC[:, mc * T:(mc + 1) * T], cacc[:], sg[:]), inc=s_ve)
                P.tensor.wait(s_ve)
                P.tensor.run(lambda e, mc=mc: e.matmul(
                    ps[2][0:96, 0:T], W["xpW"][:, mc * 96:(mc + 1) * 96],
                    UC[:, mc * T:(mc + 1) * T],
                    start=(mc == 0), stop=(mc == 15)), inc=(s_te if mc == 15 else None))
            else:
                zc = mc - 16
                P.scalar.wait(s_te)
                P.scalar.run(lambda e, b=b: e.activation(sg[:], ps[b][:, 0:T], AF.Sigmoid), inc=s_se)
                P.vector.wait(s_se)
                P.vector.run(lambda e, b=b, zc=zc: e.tensor_mul(
                    ZS[:, zc * T:(zc + 1) * T], ps[b][:, 0:T], sg[:]), inc=s_ve)
                consumed(b, s_ve)
        P.vector.wait(s_te)
        P.vector.run(lambda e: e.tensor_copy(dtb_sb[:], ps[2][0:64, 0:T]))
        P.vector.run(lambda e: e.tensor_copy(bcrow[:], ps[2][64:96, 0:T]), inc=s_ve)
        consumed(2, s_ve)
        P.tensor.wait(s_ve)
        for c in range(NC):
            bd = 2 + (c % 2)
            te_bank(bd)
            P.tensor.run(lambda e, b=bd, c=c: e.matmul(
                ps[b][:, 0:T], W["dtW"][:, c * 128:(c + 1) * 128], dtb_sb[:],
                start=True, stop=True), inc=s_te)
            dt_mark = s_te.n
            for n in (2 * c, 2 * c + 1):
                b = n % 2
                te_bank(b)
                P.tensor.run(lambda e, b=b, n=n: e.matmul(
                    ps[b][:, 0:T], SEL[:, n * 128:(n + 1) * 128], bcrow[:, 0:T],
                    start=True, stop=True), inc=s_te)
                P.vector.wait(s_te)
                dst = Bb if n < 16 else Cb
                nn = n % 16
                P.vector.run(lambda e, b=b, dst=dst, nn=nn: e.tensor_copy(
                    dst[:, nn * T:(nn + 1) * T], ps[b][:, 0:T]), inc=s_ve)
                consumed(b, s_ve)
            P.scalar.wait(s_te, dt_mark)
            P.scalar.wait(s_ve)
            P.scalar.run(lambda e, b=bd, c=c: e.activation(
                sp_e[:], ps[b][:, 0:T], AF.Abs, bias=W["dtb"][:, c:c + 1]))
            P.scalar.run(lambda e: e.activation(sp_e[:], sp_e[:], AF.Exp, scale=P.cc("neg1")))
            P.scalar.run(lambda e: e.activation(sp_e[:], sp_e[:], AF.Ln, bias=P.cc("one")))
            P.scalar.run(lambda e, b=bd, c=c: e.activation(
                sp_r[:], ps[b][:, 0:T], AF.Relu, bias=W["dtb"][:, c:c + 1]), inc=s_se)
            consumed(bd, s_se)
            P.vector.wait(s_se)
            P.vector.run(lambda e, c=c: e.tensor_add(
                DELTA[:, c * T:(c + 1) * T], sp_e[:], sp_r[:]), inc=s_ve)
        for b in range(8):
            te_bank(b)
        scan_ve_marks = {}
        for c in range(NC):
            ds = slice(c * T, (c + 1) * T)
            P.vector.run(lambda e, ds=ds, c=c: e.tensor_mul(xt[:], DELTA[:, ds], UC[:, ds]), inc=s_ve)
            P.scalar.wait(s_ve)
            xtv = xt[:].rearrange("p (o t) -> p o t", o=1).broadcast_to([128, 16, T])
            P.vector.run(lambda e, xtv=xtv: e.tensor_mul(
                DBU[:].rearrange("p (n t) -> p n t", n=16), xtv,
                Bb[:].rearrange("p (n t) -> p n t", n=16)))
            for n in range(16):
                pn = n % 2
                col = c * 16 + n
                if col >= 2:
                    P.scalar.wait(s_ve, scan_ve_marks[col - 2])
                    P.scalar.run(lambda e, pn=pn, col=col: e.activation(
                        STATE[:, col - 2:col - 1], h32[pn][:, T - 1:T],
                        AF.Copy, scale=P.cc("one")))
                P.scalar.run(lambda e, pn=pn, col=col, ds=ds: e.activation(
                    dab[pn][:], DELTA[:, ds], AF.Exp, scale=W["An"][:, col:col + 1]), inc=s_se)
                P.vector.wait(s_se)
                P.vector.run(lambda e, pn=pn, col=col, n=n: e.tensor_tensor_scan(
                    h32[pn][:], dab[pn][:], DBU[:, n * T:(n + 1) * T],
                    STATE[:, col:col + 1], ALU.mult, ALU.add))
                P.vector.run(lambda e, pn=pn, n=n: e.tensor_mul(
                    yh[:], h32[pn][:], Cb[:, n * T:(n + 1) * T]))
                if n == 0:
                    P.vector.run(lambda e, c=c: e.scalar_tensor_tensor(
                        accv[:], UC[:, c * T:(c + 1) * T], W["Dv"][:, c:c + 1], yh[:],
                        ALU.mult, ALU.add), inc=s_ve)
                else:
                    P.vector.run(lambda e: e.tensor_add(accv[:], accv[:], yh[:]), inc=s_ve)
                scan_ve_marks[col] = s_ve.n
            if c > 0:
                P.vector.wait(s_te)
            P.vector.run(lambda e, c=c: e.tensor_mul(y2[:], accv[:], ZS[:, c * T:(c + 1) * T]), inc=s_ve)
            P.tensor.wait(s_ve)
            for mc in range(8):
                P.tensor.run(lambda e, b=mc, mc=mc, c=c: e.matmul(
                    ps[b][:, 0:T],
                    W["opW"][:, c * 1024 + mc * 128: c * 1024 + (mc + 1) * 128],
                    y2[:],
                    start=(c == 0), stop=(c == NC - 1)),
                    inc=(s_te if mc == 7 else None))
        for col in (NC * 16 - 2, NC * 16 - 1):
            P.scalar.wait(s_ve, scan_ve_marks[col])
            P.scalar.run(lambda e, col=col: e.activation(
                STATE[:, col:col + 1], h32[col % 2][:, T - 1:T],
                AF.Copy, scale=P.cc("one")), inc=(s_se if col == NC * 16 - 1 else None))
        P.vector.wait(s_te)
        for mc in range(8):
            P.vector.run(lambda e, b=mc, mc=mc: e.tensor_add(
                rt[:, mc * T:(mc + 1) * T], rt[:, mc * T:(mc + 1) * T], ps[b][:, 0:T]),
                inc=s_ve)
        for b in range(8):
            consumed(b, s_ve)
        P.sync.wait(s_ve)
        P.dma(out_v[:, :, t0:t0 + T], rtv, s_st)
    if dbg is not None:
        P.plans["gpsimd"].wait(s_ve)
        P.plans["gpsimd"].wait(s_se)
        for k, buf in [("UC", UC), ("ZS", ZS), ("Bb", Bb), ("Cb", Cb),
                       ("DELTA", DELTA), ("hm", hm), ("LN0", LN0)]:
            if k in dbg:
                P.dma(dbg[k], buf[:], P.sem("s_stg"), eng="gpsimd")
        P.sync.wait(P.sem("s_stg"))
    P.sync.wait(s_st)



def emit_load_ffn_weights(P, st, dram, phase):
    nc = P.nc
    sb = lambda n, s, d: st.enter_context(nc.sbuf_tensor(f"{n}_ph{P._phase_idx}", s, d))
    W = {}
    s_w = P.sem("s_w")
    s_wg = P.sem(f"s_wgf{P._phase_idx}")
    if phase == "A":
        W["wq"] = sb("f_wq", [128, 8 * 1024], BF16)
        W["wdn"] = sb("f_wdn", [128, 8 * 64], BF16)
        W["wupk"] = sb("f_wupk", [64, 16 * 64], BF16)
        W["wupv"] = sb("f_wupv", [65, 2048], BF16)
        W["wo"] = sb("f_wo", [128, 8 * 1024], BF16)
        W["wg"] = sb("f_wg", [128, 64], BF16)
        W["g1"] = sb("f_g1", [128, 8], F32)
        W["g2"] = sb("f_g2", [128, 8], F32)
        W["bq"] = sb("f_bq", [128, 8], F32)
        W["bo"] = sb("f_bo", [128, 8], F32)
        W["bdn"] = sb("f_bdn", [64, 1], F32)
        W["bupk"] = sb("f_bupk", [64, 16], F32)
        W["bg"] = sb("f_bg", [8, 1], F32)
        W["bgr"] = sb("f_bgr", [128, 8], F32)
        W["eye"] = sb("f_eye", [128, 128], F32)
        W["mask"] = sb("f_mask", [128, 512], F32)
        W["sel"] = sb("f_sel", [32, 32 * 128], BF16)
        W["onesc"] = sb("f_onesc", [128, 1], BF16)
        W["ones1f"] = sb("f_ones1f", [1, 128], F32)
        for k, src in [("wq", "w_q"), ("wdn", "w_down"), ("wupk", "w_upk"),
                       ("wupv", "w_upv"), ("wo", "w_o"), ("wg", "w_g"), ("sel", "c_sel")]:
            P.dma(W[k][:], dram[src], s_wg, eng="gpsimd")
        for k, src in [("g1", "v_g1"), ("g2", "v_g2"), ("bq", "v_bq"), ("bo", "v_bo"),
                       ("bdn", "v_bdown"), ("bupk", "v_bupk"),
                       ("bg", "v_bg"), ("bgr", "c_bgr"), ("eye", "c_eye"), ("mask", "c_mask")]:
            P.dma(W[k][:], dram[src], s_w, eng="sync")
        P.vector.run(lambda e, W=W: e.memset(W["onesc"][:], 1.0))
        P.vector.run(lambda e, W=W: e.memset(W["ones1f"][:], 1.0), inc=P.sem("s_ve"))
    else:
        W["we"] = sb("f_we", [128, 8 * 8 * 1024], BF16)
        W["sel"] = sb("f_sel", [32, 32 * 128], BF16)
        P.dma(W["we"][:], dram["w_e"], s_wg, eng="gpsimd")
        P.dma(W["sel"][:], dram["c_sel"], s_wg, eng="gpsimd")
    for eng in ("vector", "scalar", "tensor"):
        P.plans[eng].wait(s_w)
        P.plans[eng].wait(s_wg)
    return W


def mask_const():
    m = np.full((128, 512), NEG_BIG, np.float32)
    for row in range(128):
        j = (row // 32) % 2
        for q in range(8):
            base = q * 64 + j * 32
            m[row, base:base + 16] = 0.0
    return m


def emit_ffnA(P, st, W, in_dram, s_out_dram, h2_out_dram, gate_out_dram, L=4096, T=256, dbg=None, do_att=True, do_gate=True):
    nc = P.nc
    n_t = L // T
    NP = T // 2
    NG = NP // 32
    sb = lambda n, s, d: st.enter_context(nc.sbuf_tensor(f"{n}_ph{P._phase_idx}", s, d))
    rt = sb("f_rt", [128, 8 * T], F32)
    LN0 = sb("f_ln0", [128, 8 * T], F32)
    X2 = sb("f_x2", [128, 8 * T], F32)
    h1 = sb("f_h1", [128, 8 * T], BF16)
    H2 = sb("f_h2", [128, 8 * T], BF16)
    rb = H2
    qzz = sb("f_qzz", [128, 64 * NP], BF16)
    kzz = sb("f_kzz", [128, 64 * NP], BF16)
    vzz = sb("f_vzz", [64, 32 * 128], BF16)
    lat = sb("f_lat", [65, T], BF16)
    Ohp = sb("f_ohp", [128, 8 * T], BF16)
    Asb = sb("f_asb", [128, 512], F32)
    Abf = sb("f_abf", [128, 512], BF16)
    At = sb("f_at", [128, 512], BF16)
    mx = sb("f_mx", [128, 8], F32)
    sm = sb("f_sm", [128, 8], F32)
    rc = sb("f_rc", [128, 8], F32)
    gt = sb("f_gt", [8, T], F32)
    gsb = sb("f_gsb", [128, 8 * (T // 128)], F32)
    bq8 = sb("f_bq8", [128, 8], F32)
    mu_r = sb("f_mu", [1, T], F32)
    ms_r = sb("f_ms", [1, T], F32)
    rs_r = sb("f_rs", [1, T], F32)
    ps = st.enter_context(nc.psum_tensor(f"f_ps_ph{P._phase_idx}", [128, 4096], F32))
    if dbg is not None:
        dbg_sraw = sb("dbg_sraw", [64, 4 * 512], F32)
        dbg_abf = sb("dbg_abf", [64, 4 * 512], F32)
        dbg_at = sb("dbg_at", [64, 4 * 512], F32)
        dbg_vzz = sb("dbg_vzz", [64, 4096], BF16)

    s_ld = P.sem("s_ld")
    s_st = P.sem("s_st")
    s_te = P.sem("s_te")
    s_ve = P.sem("s_ve")
    s_se = P.sem("s_se")
    s_gp = P.sem("s_gp")

    bank_free = {}

    def te_bank(b):
        for sem, cnt in bank_free.pop(b, []):
            P.tensor.wait(sem, cnt)

    def consumed(b, sem):
        bank_free.setdefault(b, []).append((sem, sem.n))

    def B(b, rows=128, cols=512, co=0):
        return ps[0:rows, b * 512 + co: b * 512 + co + cols]

    in_v = in_dram.rearrange("(c p) t -> p c t", p=128)
    s_v = s_out_dram.rearrange("(c p) t -> p c t", p=128)
    h2_v = h2_out_dram.rearrange("(c p) t -> p c t", p=128)
    rtv = rt[:].rearrange("p (c t) -> p c t", c=8)
    h2v = H2[:].rearrange("p (c t) -> p c t", c=8)

    P.vector.run(lambda e: e.memset(qzz[:], 0.0))
    P.vector.run(lambda e: e.memset(kzz[:], 0.0))
    P.vector.run(lambda e: e.memset(vzz[:], 0.0))
    P.vector.run(lambda e: e.memset(lat[64:65, :], 1.0))
    P.vector.run(lambda e: e.tensor_scalar_mul(bq8[:], W["bq"][:], 0.125), inc=s_ve)

    def ln_stats(src_bf, eps_name, out_mu=True):
        P.tensor.wait(s_ve)
        te_bank(0)
        te_bank(1)
        for c in range(8):
            P.tensor.run(lambda e, c=c: e.matmul(
                B(0, 1, T), W["onesc"][:], src_bf[:, c * T:(c + 1) * T],
                start=(c == 0), stop=(c == 7)), inc=(s_te if c == 7 else None))
        P.scalar.wait(s_te)
        P.scalar.run(lambda e: e.activation(src_bf[:], src_bf[:], AF.Square), inc=s_se)
        P.tensor.wait(s_se)
        for c in range(8):
            P.tensor.run(lambda e, c=c: e.matmul(
                B(1, 1, T), W["onesc"][:], src_bf[:, c * T:(c + 1) * T],
                start=(c == 0), stop=(c == 7)), inc=(s_te if c == 7 else None))
        P.scalar.wait(s_te)
        if out_mu:
            P.scalar.run(lambda e: e.activation(mu_r[:], B(0, 1, T), AF.Copy, scale=P.cc("r1024", 1)))
            P.scalar.run(lambda e: e.activation(ms_r[:], B(1, 1, T), AF.Copy, scale=P.cc("r1024", 1)))
            P.scalar.run(lambda e: e.activation(rs_r[:], mu_r[:], AF.Square), inc=s_se)
            consumed(0, s_se)
            consumed(1, s_se)
            P.vector.wait(s_se)
            P.vector.run(lambda e: e.tensor_sub(ms_r[:], ms_r[:], rs_r[:]), inc=s_ve)
            P.scalar.wait(s_ve)
        else:
            P.scalar.run(lambda e: e.activation(ms_r[:], B(1, 1, T), AF.Copy, scale=P.cc("r1024", 1)), inc=s_se)
            consumed(0, s_se)
            consumed(1, s_se)
        P.scalar.run(lambda e, eps=eps_name: e.activation(ms_r[:], ms_r[:], AF.Ln, bias=P.cc(eps, 1)))
        P.scalar.run(lambda e: e.activation(rs_r[:], ms_r[:], AF.Exp, scale=P.cc("negh", 1)), inc=s_se)
        P.tensor.wait(s_se)
        te_bank(0)
        te_bank(1)
        if out_mu:
            P.tensor.run(lambda e: e.matmul(B(0, cols=T), W["ones1f"][:], mu_r[:], start=True, stop=True))
        P.tensor.run(lambda e: e.matmul(B(1, cols=T), W["ones1f"][:], rs_r[:], start=True, stop=True), inc=s_te)
        P.vector.wait(s_te)

    for i in range(n_t):
        t0 = i * T
        P.sync.wait(s_ve)
        P.sync.wait(s_st)
        P.dma(rtv, in_v[:, :, t0:t0 + T], s_ld)
        P.vector.wait(s_ld)
        P.vector.wait(s_te)
        P.vector.wait(s_st)
        P.vector.run(lambda e: e.tensor_copy(rb[:], rt[:]), inc=s_ve)
        ln_stats(rb, "eps5", out_mu=True)
        mub = B(0, cols=T).rearrange("p (o t) -> p o t", o=1).broadcast_to([128, 8, T])
        rsb = B(1, cols=T).rearrange("p (o t) -> p o t", o=1).broadcast_to([128, 8, T])
        l3 = LN0[:].rearrange("p (c t) -> p c t", c=8)
        r3 = rt[:].rearrange("p (c t) -> p c t", c=8)
        P.vector.run(lambda e, l3=l3, r3=r3, mub=mub: e.tensor_sub(l3, r3, mub))
        P.vector.run(lambda e, l3=l3, rsb=rsb: e.tensor_mul(l3, l3, rsb), inc=s_ve)
        consumed(0, s_ve)
        consumed(1, s_ve)
        P.vector.run(lambda e: e.tensor_copy(rb[:], LN0[:]), inc=s_ve)
        ln_stats(rb, "eps6", out_mu=False)
        for c in range(8):
            cs = slice(c * T, (c + 1) * T)
            P.vector.run(lambda e, cs=cs, c=c: e.scalar_tensor_tensor(
                h1[:, cs], LN0[:, cs], W["g1"][:, c:c + 1], B(1, cols=T),
                ALU.mult, ALU.mult), inc=s_ve)
        consumed(1, s_ve)
        P.tensor.wait(s_ve)
        for hp in range(8):
            b = 2 + hp % 2
            te_bank(b)
            for kc in range(8):
                P.tensor.run(lambda e, b=b, hp=hp, kc=kc: e.matmul(
                    B(b, cols=T), W["wq"][:, kc * 1024 + hp * 128: kc * 1024 + (hp + 1) * 128],
                    h1[:, kc * T:(kc + 1) * T],
                    start=(kc == 0), stop=(kc == 7)), inc=(s_te if kc == 7 else None))
            P.vector.wait(s_te)
            P.scalar.wait(s_te)
            for hh in range(2):
                h = 2 * hp + hh
                for j in range(2):
                    src = B(b, cols=T).rearrange("p (pp two) -> p pp two", two=2)[hh * 64:(hh + 1) * 64, :, j]
                    dst = qzz[:].rearrange("p (pp cc) -> p pp cc", cc=64)[j * 64:(j + 1) * 64, :, j * 32 + h]
                    if SCE_EVAC and (hh + j) % 2 == 1:
                        P.scalar.run(lambda e, src=src, dst=dst, hh=hh, hp=hp: e.activation(
                            dst, src, AF.Identity, bias=bq8[hh * 64:(hh + 1) * 64, hp:hp + 1],
                            scale=P.cc("q8", 64)), inc=s_se)
                    else:
                        P.vector.run(lambda e, src=src, dst=dst, hh=hh, hp=hp: e.tensor_scalar(
                            dst, src, W["bq"][hh * 64:(hh + 1) * 64, hp:hp + 1],
                            P.cc("q8", 64), ALU.add, ALU.mult), inc=s_ve)
            consumed(b, s_ve)
            consumed(b, s_se)
        te_bank(4)
        P.tensor.wait(s_ve)
        for kc in range(8):
            P.tensor.run(lambda e, kc=kc: e.matmul(
                B(4, 64, T), W["wdn"][:, kc * 64:(kc + 1) * 64],
                h1[:, kc * T:(kc + 1) * T],
                start=(kc == 0), stop=(kc == 7)), inc=(s_te if kc == 7 else None))
        P.vector.wait(s_te)
        P.vector.run(lambda e: e.tensor_scalar(
            lat[0:64, :], B(4, 64, T), W["bdn"][:, 0:1], None, ALU.add), inc=s_ve)
        consumed(4, s_ve)
        P.tensor.wait(s_ve)
        for g in range(16):
            b = 2 + g % 2
            te_bank(b)
            P.tensor.run(lambda e, b=b, g=g: e.matmul(
                B(b, 64, T), W["wupk"][:, g * 64:(g + 1) * 64], lat[0:64, :],
                start=True, stop=True), inc=s_te)
            P.vector.wait(s_te)
            P.scalar.wait(s_te)
            for j in range(2):
                src = B(b, 64, T).rearrange("p (pp two) -> p pp two", two=2)[:, :, j]
                dst = kzz[:].rearrange("p (pp cc) -> p pp cc", cc=64)[j * 64:(j + 1) * 64, :, j * 32 + g]
                if SCE_EVAC and j == 1:
                    P.scalar.run(lambda e, src=src, dst=dst, g=g: e.activation(
                        dst, src, AF.Identity, bias=W["bupk"][:, g:g + 1]), inc=s_se)
                else:
                    P.vector.run(lambda e, src=src, dst=dst, g=g: e.tensor_scalar(
                        dst, src, W["bupk"][:, g:g + 1], None, ALU.add), inc=s_ve)
            consumed(b, s_ve)
            consumed(b, s_se)
        if not do_att:
            P.vector.run(lambda e: e.memset(Ohp[:], 0.0), inc=s_ve)
        for g0 in range(NG if do_att else 0):
            tok0 = g0 * 64
            te_bank(4)
            te_bank(5)
            P.tensor.wait(s_ve)
            P.tensor.wait(s_se)
            for dq in range(16):
                P.tensor.run(lambda e, dq=dq, tok0=tok0: e.matmul(
                    ps[:, 4 * 512 + dq * 64: 4 * 512 + (dq + 1) * 64],
                    W["wupv"][:, dq * 128:(dq + 1) * 128],
                    lat[:, tok0:tok0 + 64],
                    start=True, stop=True), inc=(s_te if dq == 15 else None))
            P.vector.wait(s_te)
            P.scalar.wait(s_te)
            for m in range(4):
                for j in range(2):
                    src = ps[32 * m:32 * m + 16, 4 * 512: 4 * 512 + 1024].rearrange(
                        "p (dh q pp two) -> p dh q pp two", dh=8, q=2, two=2)[:, :, :, :, j]
                    dst = vzz[j * 32:j * 32 + 16, :].rearrange(
                        "p (pp jj dh q mm) -> p dh q pp jj mm", jj=2, dh=8, q=2, mm=4)[:, :, :, :, j, m]
                    if SCE_EVAC and (m + j) % 2 == 1:
                        P.scalar.run(lambda e, src=src, dst=dst: e.copy(dst, src), inc=s_se)
                    else:
                        P.vector.run(lambda e, src=src, dst=dst: e.tensor_copy(dst, src), inc=s_ve)
            consumed(4, s_ve)
            consumed(4, s_se)
            consumed(5, s_ve)
            consumed(5, s_se)
            if dbg is not None and i == 0 and g0 == 0:
                P.vector.wait(s_se)
                P.vector.run(lambda e: e.tensor_copy(dbg_vzz[:], vzz[:]), inc=s_ve)
            for b in range(4):
                te_bank(b)
            P.tensor.wait(s_ve)
            P.tensor.wait(s_se)
            for p in range(32):
                pa = tok0 // 2 + p
                P.tensor.run(lambda e, p=p, pa=pa: e.matmul(
                    ps[0:64, (p // 8) * 512 + (p % 8) * 64: (p // 8) * 512 + (p % 8) * 64 + 64],
                    qzz[:, pa * 64:(pa + 1) * 64], kzz[:, pa * 64:(pa + 1) * 64],
                    start=True, stop=True), inc=(s_te if p == 31 else None))
            for sub in range(4):
                P.vector.wait(s_te)
                if dbg is not None and i == 0 and sub == 0:
                    P.vector.run(lambda e, g0=g0: e.tensor_copy(
                        dbg_sraw[:, g0 * 512:(g0 + 1) * 512], ps[0:64, 0:512]), inc=s_ve)
                P.vector.run(lambda e, sub=sub: e.tensor_add(
                    Asb[0:64, :], ps[0:64, sub * 512:(sub + 1) * 512],
                    W["mask"][0:64, :]), inc=s_ve)
                consumed(sub, s_ve)
                v3 = Asb[0:64, :].rearrange("p (q c) -> p q c", c=64)
                P.vector.run(lambda e, v3=v3: e.reduce_max(mx[0:64, :], v3, AX.X), inc=s_ve)
                P.vector.wait(s_ve)
                mxb = mx[0:64, :].rearrange("p (q o) -> p q o", o=1).broadcast_to([64, 8, 64])
                P.vector.run(lambda e, v3=v3, mxb=mxb: e.tensor_sub(v3, v3, mxb), inc=s_ve)
                P.scalar.wait(s_ve)
                P.scalar.run(lambda e: e.activation(Abf[0:64, :], Asb[0:64, :], AF.Exp), inc=s_se)
                P.vector.wait(s_se)
                a3 = Abf[0:64, :].rearrange("p (q c) -> p q c", c=64)
                P.vector.run(lambda e, a3=a3: e.reduce_sum(sm[0:64, :], a3, AX.X), inc=s_ve)
                P.vector.wait(s_ve)
                P.vector.run(lambda e: e.reciprocal(rc[0:64, :], sm[0:64, :]), inc=s_ve)
                P.vector.wait(s_ve)
                rcb = rc[0:64, :].rearrange("p (q o) -> p q o", o=1).broadcast_to([64, 8, 64])
                P.vector.run(lambda e, a3=a3, rcb=rcb: e.tensor_mul(a3, a3, rcb))
                P.vector.run(lambda e: e.transpose(At[0:64, :], Abf[0:64, :]), inc=s_ve)
                if dbg is not None and i == 0 and sub == 0:
                    P.vector.run(lambda e, g0=g0: e.tensor_copy(
                        dbg_abf[:, g0 * 512:(g0 + 1) * 512], Abf[0:64, :]))
                    P.vector.run(lambda e, g0=g0: e.tensor_copy(
                        dbg_at[:, g0 * 512:(g0 + 1) * 512], At[0:64, :]), inc=s_ve)
                bb = 4 + sub
                te_bank(bb)
                P.tensor.wait(s_ve)
                for qp in range(8):
                    p = sub * 8 + qp
                    P.tensor.run(lambda e, p=p, qp=qp, bb=bb: e.matmul(
                        ps[:, bb * 512 + qp * 64: bb * 512 + qp * 64 + 64],
                        vzz[:, p * 128:(p + 1) * 128],
                        At[0:64, qp * 64:(qp + 1) * 64],
                        start=True, stop=True), inc=(s_te if qp == 7 else None))
            P.vector.wait(s_te)
            P.scalar.wait(s_te)
            for h in range(16):
                for jm in range(2):
                    src = ps[jm * 64:(jm + 1) * 64, 4 * 512: 4 * 512 + 2048].rearrange(
                        "p (pp cc) -> p pp cc", cc=64)[:, :, jm * 32 + h]
                    hp = h // 2
                    dst = Ohp[(h % 2) * 64:(h % 2) * 64 + 64,
                              hp * T + tok0: hp * T + tok0 + 64].rearrange(
                        "p (pp two) -> p pp two", two=2)[:, :, jm]
                    if SCE_EVAC and (h + jm) % 2 == 1:
                        P.scalar.run(lambda e, src=src, dst=dst: e.copy(dst, src), inc=s_se)
                    else:
                        P.vector.run(lambda e, src=src, dst=dst: e.tensor_copy(dst, src), inc=s_ve)
            for b in range(4, 8):
                consumed(b, s_ve)
                consumed(b, s_se)
        P.tensor.wait(s_ve)
        P.tensor.wait(s_se)
        for mc in range(8):
            b = 2 + mc % 2
            te_bank(b)
            for kc in range(8):
                P.tensor.run(lambda e, b=b, mc=mc, kc=kc: e.matmul(
                    B(b, cols=T), W["wo"][:, kc * 1024 + mc * 128: kc * 1024 + (mc + 1) * 128],
                    Ohp[:, kc * T:(kc + 1) * T],
                    start=(kc == 0), stop=(kc == 7)), inc=(s_te if kc == 7 else None))
            P.vector.wait(s_te)
            cs = slice(mc * T, (mc + 1) * T)
            P.vector.run(lambda e, b=b, cs=cs, mc=mc: e.scalar_tensor_tensor(
                X2[:, cs], B(b, cols=T), W["bo"][:, mc:mc + 1], LN0[:, cs],
                ALU.add, ALU.add), inc=s_ve)
            consumed(b, s_ve)
        P.vector.run(lambda e: e.tensor_copy(rb[:], X2[:]), inc=s_ve)
        ln_stats(rb, "eps6", out_mu=False)
        for c in range(8):
            cs = slice(c * T, (c + 1) * T)
            P.vector.run(lambda e, cs=cs, c=c: e.scalar_tensor_tensor(
                H2[:, cs], X2[:, cs], W["g2"][:, c:c + 1], B(1, cols=T),
                ALU.mult, ALU.mult), inc=s_ve)
        consumed(1, s_ve)
        if not do_gate:
            P.vector.run(lambda e: e.memset(gt[:], 0.125), inc=s_ve)
        te_bank(2)
        te_bank(3)
        P.tensor.wait(s_ve)
        for tb in range(T // 128 if do_gate else 0):
            for kc in range(8):
                P.tensor.run(lambda e, tb=tb, kc=kc: e.matmul(
                    B(2, 128, 8, co=tb * 8),
                    H2[:, kc * T + tb * 128: kc * T + tb * 128 + 128],
                    W["wg"][:, kc * 8:(kc + 1) * 8],
                    start=(kc == 0), stop=(kc == 7)),
                    inc=(s_te if kc == 7 else None))
        P.vector.wait(s_te)
        for tb in range(T // 128 if do_gate else 0):
            P.vector.run(lambda e, tb=tb: e.tensor_add(
                gsb[:, tb * 8:(tb + 1) * 8], B(2, 128, 8, co=tb * 8),
                W["bgr"][0:1, :].broadcast_to([128, 8]) if False else W["bgr"][:, :]), inc=s_ve)
        consumed(2, s_ve)
        for tb in range(T // 128 if do_gate else 0):
            gs = gsb[:, tb * 8:(tb + 1) * 8]
            g3 = gs.rearrange("p (q c) -> p q c", c=8)
            P.vector.run(lambda e, gs=gs, g3=g3: e.reduce_max(mx[:, 0:1], g3, AX.X), inc=s_ve)
            P.vector.wait(s_ve)
            P.vector.run(lambda e, gs=gs: e.tensor_sub(
                gs, gs, mx[:, 0:1].broadcast_to([128, 8])), inc=s_ve)
            P.scalar.wait(s_ve)
            P.scalar.run(lambda e, gs=gs: e.activation(gs, gs, AF.Exp), inc=s_se)
            P.vector.wait(s_se)
            P.vector.run(lambda e, gs=gs: e.reduce_sum(
                sm[:, 0:1], gs.rearrange("p (q c) -> p q c", c=8), AX.X), inc=s_ve)
            P.vector.wait(s_ve)
            P.vector.run(lambda e: e.reciprocal(rc[:, 0:1], sm[:, 0:1]), inc=s_ve)
            P.vector.wait(s_ve)
            P.vector.run(lambda e, gs=gs: e.tensor_mul(
                gs, gs, rc[:, 0:1].broadcast_to([128, 8])), inc=s_ve)
        if do_gate:
            P.tensor.wait(s_ve)
            for tb in range(T // 128):
                P.tensor.run(lambda e, tb=tb: e.transpose(
                    B(3, 8, 128, co=tb * 128), gsb[:, tb * 8:(tb + 1) * 8], W["eye"][:]),
                    inc=(s_te if tb == (T // 128) - 1 else None))
            P.vector.wait(s_te)
            for tb in range(T // 128):
                P.vector.run(lambda e, tb=tb: e.tensor_copy(
                    gt[:, tb * 128:(tb + 1) * 128], B(3, 8, 128, co=tb * 128)), inc=s_ve)
            consumed(3, s_ve)
        for c in range(8):
            cs = slice(c * T, (c + 1) * T)
            P.vector.run(lambda e, cs=cs: e.scalar_tensor_tensor(
                rt[:, cs], X2[:, cs], P.cc("half"), rt[:, cs],
                ALU.mult, ALU.add), inc=s_ve)
        P.sync.wait(s_ve)
        P.sync.wait(s_se)
        P.dma(s_v[:, :, t0:t0 + T], rtv, s_st)
        P.dma(h2_v[:, :, t0:t0 + T], h2v, s_st)
        P.dma(gate_out_dram[:, t0:t0 + T], gt[:], s_st)
    if dbg is not None:
        P.sync.wait(s_ve)
        P.dma(dbg["sraw"], dbg_sraw[:], s_st)
        P.dma(dbg["abf"], dbg_abf[:], s_st)
        P.dma(dbg["at"], dbg_at[:], s_st)
        P.plans["gpsimd"].wait(s_ve)
        P.dma(dbg["vzz"], dbg_vzz[:], P.sem("s_stg2"), eng="gpsimd")
        P.sync.wait(P.sem("s_stg2"))
    P.sync.wait(s_st)


def emit_ffnB(P, st, W, s_in_dram, h2_in_dram, gate_in_dram, out_dram, L=4096, T=256, dbg=None):
    nc = P.nc
    n_t = L // T
    sb = lambda n, s, d: st.enter_context(nc.sbuf_tensor(f"{n}_ph{P._phase_idx}", s, d))
    stt = sb("b_st", [128, 8 * T], F32)
    H2 = sb("b_h2", [128, 8 * T], BF16)
    HS = sb("b_hs", [128, 8 * T], BF16)
    gt = sb("b_gt", [8, T], F32)
    gtb = sb("b_gtb", [32, T], BF16)
    gb8 = sb("b_gb8", [128, 8 * T], BF16)
    ps = st.enter_context(nc.psum_tensor(f"b_ps_ph{P._phase_idx}", [128, 4096], F32))
    if dbg is not None:
        dbg_gb = sb("dbg_gb", [128, 8 * T], F32)
        dbg_hs = sb("dbg_hs", [128, 8 * T], F32)
    s_ld = P.sem("s_ld")
    s_st = P.sem("s_st")
    s_te = P.sem("s_te")
    s_ve = P.sem("s_ve")
    bank_free = {}

    def te_bank(b):
        for sem, cnt in bank_free.pop(b, []):
            P.tensor.wait(sem, cnt)

    def consumed(b, sem):
        bank_free.setdefault(b, []).append((sem, sem.n))

    s_v = s_in_dram.rearrange("(c p) t -> p c t", p=128)
    h2_v = h2_in_dram.rearrange("(c p) t -> p c t", p=128)
    o_v = out_dram.rearrange("(c p) t -> p c t", p=128)
    stv = stt[:].rearrange("p (c t) -> p c t", c=8)
    h2v = H2[:].rearrange("p (c t) -> p c t", c=8)

    P.vector.run(lambda e: e.memset(gtb[:], 0.0), inc=s_ve)
    for i in range(n_t):
        t0 = i * T
        P.sync.wait(s_ve)
        P.sync.wait(s_st)
        P.dma(stv, s_v[:, :, t0:t0 + T], s_ld)
        P.dma(h2v, h2_v[:, :, t0:t0 + T], s_ld)
        P.dma(gt[:], gate_in_dram[:, t0:t0 + T], s_ld)
        P.vector.wait(s_ld)
        P.vector.run(lambda e: e.tensor_copy(gtb[0:8, :], gt[:]), inc=s_ve)
        P.tensor.wait(s_ve)
        te_bank(0)
        te_bank(1)
        for e_ in range(8):
            bco = (e_ % 2) * 512
            if e_ >= 2:
                P.tensor.wait(s_ve)
            P.tensor.run(lambda e, e_=e_, bco=bco: e.matmul(
                ps[:, bco: bco + T],
                W["sel"][0:32, e_ * 128:(e_ + 1) * 128], gtb[:],
                start=True, stop=True), inc=s_te)
            P.vector.wait(s_te)
            P.vector.run(lambda e, e_=e_, bco=bco: e.tensor_copy(
                gb8[:, e_ * T:(e_ + 1) * T], ps[:, bco: bco + T]), inc=s_ve)
        consumed(0, s_ve)
        consumed(1, s_ve)
        for e_ in range(8):
            if e_ > 0:
                P.vector.wait(s_te)
            gbv = gb8[:, 0:8 * T].rearrange("p (o t) -> p o t", o=8)[:, 0:1, :].broadcast_to([128, 8, T]) if False else None
            gview = gb8[:, :].rearrange("p (ee t) -> p ee t", ee=8)[:, e_:e_ + 1, :].broadcast_to([128, 8, T])
            P.vector.run(lambda e, e_=e_, gview=gview: e.tensor_mul(
                HS[:].rearrange("p (c t) -> p c t", c=8), H2[:].rearrange("p (c t) -> p c t", c=8),
                gview), inc=s_ve)
            P.tensor.wait(s_ve)
            for mc in range(8):
                b = mc
                if e_ == 0:
                    te_bank(b)
                for kc in range(8):
                    sh = 256 if T == 256 else 0
                    P.tensor.run(lambda e, b=b, mc=mc, kc=kc, e_=e_, sh=sh: e.matmul(
                        ps[:, b * 512 + sh: b * 512 + sh + T] if b <= 1 else ps[:, b * 512: b * 512 + T],
                        W["we"][:, e_ * 8192 + kc * 1024 + mc * 128: e_ * 8192 + kc * 1024 + (mc + 1) * 128],
                        HS[:, kc * T:(kc + 1) * T],
                        start=(e_ == 0 and kc == 0), stop=(e_ == 7 and kc == 7)),
                        inc=(s_te if (kc == 7 and mc == 7) else None))
        P.vector.wait(s_te)
        if dbg is not None and i == n_t - 1:
            P.vector.run(lambda e: e.tensor_copy(dbg_gb[:], gb8[:]))
            P.vector.run(lambda e: e.tensor_copy(dbg_hs[:], HS[:]), inc=s_ve)
        for mc in range(8):
            b = mc
            cs = slice(mc * T, (mc + 1) * T)
            sh = 256 if T == 256 else 0
            src = ps[:, b * 512 + sh: b * 512 + sh + T] if b <= 1 else ps[:, b * 512: b * 512 + T]
            P.vector.run(lambda e, src=src, cs=cs: e.scalar_tensor_tensor(
                stt[:, cs], src, P.cc("half"), stt[:, cs],
                ALU.mult, ALU.add), inc=s_ve)
        for b in range(8):
            consumed(b, s_ve)
        P.sync.wait(s_ve)
        P.dma(o_v[:, :, t0:t0 + T], stv, s_st)
    if dbg is not None:
        P.sync.wait(s_ve)
        P.dma(dbg["gb8"], dbg_gb[:], s_st)
        P.dma(dbg["hs"], dbg_hs[:], s_st)
    P.sync.wait(s_st)



from contextlib import ExitStack

CONSTS = {"eps5": 1e-5, "eps6": 1e-6, "negh": -0.5, "neg1": -1.0,
          "one": 1.0, "r1024": 1.0 / 1024.0, "q8": 0.125, "half": 0.5}


def build_program(L=4096, T_f=512, T_m=256):
    nc = bass.Bass("TRN2", target_bir_lowering=False, detect_race_conditions=False)
    dram = {}
    for k, (shape, dt) in INPUT_SPECS.items():
        dram[k] = nc.dram_tensor(k, list(shape), dt, kind="ExternalInput").ap()
    out = nc.dram_tensor("out", [1024, L], F32, kind="ExternalOutput").ap()
    s_scr = nc.dram_tensor("s_scr", [1024, L], F32, kind="Internal").ap()
    h2_scr = nc.dram_tensor("h2_scr", [1024, L], BF16, kind="Internal").ap()
    g_scr = nc.dram_tensor("g_scr", [8, L], F32, kind="Internal").ap()
    r1_scr = nc.dram_tensor("r1_scr", [1024, L], F32, kind="Internal").ap()
    r2_scr = nc.dram_tensor("r2_scr", [1024, L], F32, kind="Internal").ap()
    with ExitStack() as big:
        P = Prog(nc)
        P._stack = big
        P.setup_consts(CONSTS)
        with ExitStack() as st:
            W = emit_load_ffn_weights(P, st, dram, "A")
            emit_ffnA(P, st, W, dram["xT"], s_scr, h2_scr, g_scr, L=L, T=T_f)
            P.flush_phase()
        with ExitStack() as st:
            W = emit_load_ffn_weights(P, st, dram, "B")
            emit_ffnB(P, st, W, s_scr, h2_scr, g_scr, r1_scr, L=L, T=T_f)
            P.flush_phase()
        with ExitStack() as st:
            W = emit_load_mamba_weights(P, st, dram)
            emit_mamba(P, st, W, r1_scr, r2_scr, dram["c_sel"], L=L, T=T_m)
            P.flush_phase()
        with ExitStack() as st:
            W = emit_load_ffn_weights(P, st, dram, "A")
            emit_ffnA(P, st, W, r2_scr, s_scr, h2_scr, g_scr, L=L, T=T_f)
            P.flush_phase()
        with ExitStack() as st:
            W = emit_load_ffn_weights(P, st, dram, "B")
            emit_ffnB(P, st, W, s_scr, h2_scr, g_scr, out, L=L, T=T_f)
            P.flush_phase()
    return nc


def kernel(**inputs):
    common = host_layout(inputs, 0)
    del common["xT"]
    nc = build_program()
    in_maps = []
    for c in range(NCORES):
        m = dict(common)
        m["xT"] = np.ascontiguousarray(inputs["L0_attn_x"][c].T)
        in_maps.append(m)
    res = run_bass_kernel_spmd(nc, in_maps, core_ids=list(range(NCORES)))
    return np.stack([np.ascontiguousarray(r["out"].T) for r in res.results], axis=0)
